# version 19
# baseline (speedup 1.0000x reference)
import numpy as np
from contextlib import ExitStack

import concourse.bass as bass
import concourse.mybir as mybir
from concourse.bass_utils import run_bass_kernel_spmd

F32 = mybir.dt.float32
BF16 = mybir.dt.bfloat16
I32 = mybir.dt.int32
AF = mybir.ActivationFunctionType
ALU = mybir.AluOpType
AX = mybir.AxisListType

NCORES = 8
D = 1024
NH = 16
C = 64
TILE = 128
T_OWN = 2048
WU = 256
T_ALL = T_OWN + WU
HID = 3072
NHC = HID // TILE
SCL = 1.0 / 127.0
EPS = 1e-5

DEBUG = False
TRUNC = 0


class _Sem:
    def __init__(self, h):
        self.h = h
        self.n = 0
        self.pending = None


class _Ops:

    def __init__(self):
        self.q = {k: [] for k in ("t", "v", "s", "sy", "g")}
        self.self_sem = {}

    def add(self, eng, fn, wait=(), inc=None, inc_by=1):
        wait = list(wait)
        ss = self.self_sem.get(eng)
        if ss is not None:
            wait.append((ss, ss.n))
            if inc is None:
                inc, inc_by = ss, 1
        if inc is not None:
            inc.n += inc_by
        w = tuple((s, v) for (s, v) in wait if v > 0)

        def run(e, fn=fn, w=w, inc=inc, inc_by=inc_by):
            for sem, val in w:
                e.wait_ge(sem.h, val)
            ins = fn(e)
            if inc is not None and ins is not None:
                ins.then_inc(inc.h, inc_by)
        self.q[eng].append(run)

    def dma(self, eng, sem, fn, wait=()):
        w = list(wait)
        if sem.pending is not None:
            w.append((sem, sem.pending))
            sem.pending = None
        self.add(eng, fn, wait=w, inc=sem, inc_by=16)

    def close_epoch(self, sem):
        sem.pending = sem.n
        return sem.n


def build_program():
    nc = bass.Bass("TRN2", target_bir_lowering=False)

    xs = nc.dram_tensor("xs", [T_ALL, D], F32, kind="ExternalInput")
    wqT = nc.dram_tensor("wqT", [D, D], F32, kind="ExternalInput")
    wkT = nc.dram_tensor("wkT", [D, D], F32, kind="ExternalInput")
    wvT = nc.dram_tensor("wvT", [D, D], F32, kind="ExternalInput")
    wgT = nc.dram_tensor("wgT", [D, D], F32, kind="ExternalInput")
    woT = nc.dram_tensor("woT", [D, D], F32, kind="ExternalInput")
    gamT = nc.dram_tensor("gamT", [TILE, 8], F32, kind="ExternalInput")
    betT = nc.dram_tensor("betT", [TILE, 8], F32, kind="ExternalInput")
    wgi = nc.dram_tensor("wgi", [D, HID], I32, kind="ExternalInput")
    wui = nc.dram_tensor("wui", [D, HID], I32, kind="ExternalInput")
    wdi = nc.dram_tensor("wdi", [HID, D], I32, kind="ExternalInput")
    out = nc.dram_tensor("out", [T_OWN, D], F32, kind="ExternalOutput")

    x2d = nc.dram_tensor("x2d", [T_OWN, D], F32)

    mask_np = np.tile(np.triu(np.ones((C, C), np.float32)), (2, NH))
    maskt = nc.inline_tensor(mask_np, "mask_rep")
    ones_r = nc.inline_tensor(np.ones((1, TILE), np.float32), "ones_r")
    ones_c = nc.inline_tensor(np.ones((TILE, 1), np.float32), "ones_c")

    dbg = {}
    if DEBUG:
        for name, shape in [
            ("d_xln", [TILE, 4096]), ("d_S", [TILE, 4096]), ("d_negL", [TILE, 2048]),
            ("d_qt", [TILE, 4096]), ("d_kt", [TILE, 4096]), ("d_V", [TILE, 4096]),
            ("d_h", [TILE, 512]), ("d_y", [TILE, 4096]),
            ("d_x2", [TILE, 1024]), ("d_hff", [TILE, 2048]),
        ]:
            dbg[name] = nc.dram_tensor(name, shape, F32, kind="ExternalOutput")

    es = ExitStack()
    E_ = es.enter_context
    s_wq = E_(nc.sbuf_tensor("s_wq", [TILE, 8 * D], BF16))
    s_wk = E_(nc.sbuf_tensor("s_wk", [TILE, 8 * D], BF16))
    s_wv = E_(nc.sbuf_tensor("s_wv", [TILE, 8 * D], BF16))
    s_wg = E_(nc.sbuf_tensor("s_wg", [TILE, 8 * D], BF16))
    s_wo = E_(nc.sbuf_tensor("s_wo", [TILE, 8 * D], BF16))
    s_gamT = E_(nc.sbuf_tensor("s_gamT", [TILE, 8], F32))
    s_betT = E_(nc.sbuf_tensor("s_betT", [TILE, 8], BF16))
    s_row = E_(nc.sbuf_tensor("s_row", [1, D], BF16))
    s_one = E_(nc.sbuf_tensor("s_one", [1, TILE], BF16))
    s_onec = E_(nc.sbuf_tensor("s_onec", [TILE, 1], BF16))
    s_bwB = E_(nc.sbuf_tensor("s_bwB", [TILE, D], BF16))
    s_wsB = E_(nc.sbuf_tensor("s_wsB", [TILE, D], BF16))
    s_eps = E_(nc.sbuf_tensor("s_eps", [TILE, 1], F32))
    s_mask = E_(nc.sbuf_tensor("s_mask", [TILE, 1024], BF16))
    s_x = E_(nc.sbuf_tensor("s_x", [TILE, 4096], F32))
    s_xln = E_(nc.sbuf_tensor("s_xln", [TILE, 4096], BF16))
    s_xlnT = E_(nc.sbuf_tensor("s_xlnT", [TILE, 4096], BF16))
    s_S = E_(nc.sbuf_tensor("s_S", [TILE, 4096], BF16))
    s_nL = E_(nc.sbuf_tensor("s_nL", [TILE, 2048], F32))
    s_E = E_(nc.sbuf_tensor("s_E", [TILE, 4096], BF16))
    s_Er = E_(nc.sbuf_tensor("s_Er", [TILE, 4096], BF16))
    s_ec = E_(nc.sbuf_tensor("s_ec", [TILE, 128], F32))
    s_qt = E_(nc.sbuf_tensor("s_qt", [TILE, 4096], BF16))
    s_kt = E_(nc.sbuf_tensor("s_kt", [TILE, 4096], BF16))
    s_V = E_(nc.sbuf_tensor("s_V", [TILE, 4096], BF16))
    s_ktm = E_(nc.sbuf_tensor("s_ktm", [TILE, 4096], BF16))
    s_P = E_(nc.sbuf_tensor("s_P", [TILE, 1024], BF16))
    s_h = [E_(nc.sbuf_tensor(f"s_h{i}", [TILE, 512], BF16)) for i in range(3)]
    s_y = E_(nc.sbuf_tensor("s_y", [TILE, 2048], BF16))
    s_ht = E_(nc.sbuf_tensor("s_ht", [TILE, 512], F32))
    s_VS = E_(nc.sbuf_tensor("s_VS", [TILE, 1024], BF16))
    s_xr = E_(nc.sbuf_tensor("s_xr", [TILE, 2048], F32))
    s_stats = E_(nc.sbuf_tensor("s_stats", [TILE, 128], F32))
    p_pp = [E_(nc.psum_tensor(f"p_pp{i}", [TILE, 512], F32)) for i in range(2)]
    p_py = E_(nc.psum_tensor("p_py", [TILE, 1024], F32))
    p_st = E_(nc.psum_tensor("p_st", [TILE, 1024], F32))
    p_fr = [E_(nc.psum_tensor(f"p_fr{i}", [TILE, 512], F32)) for i in range(2)]

    SX = _Sem(nc.alloc_semaphore("SX"))
    ST = _Sem(nc.alloc_semaphore("ST"))
    KT = _Sem(nc.alloc_semaphore("KT"))
    YT = _Sem(nc.alloc_semaphore("YT"))
    GQ = _Sem(nc.alloc_semaphore("GQ"))
    GX = _Sem(nc.alloc_semaphore("GX"))
    XQ = _Sem(nc.alloc_semaphore("XQ"))
    DQ = _Sem(nc.alloc_semaphore("DQ"))
    GE = _Sem(nc.alloc_semaphore("GE"))
    TE = _Sem(nc.alloc_semaphore("TE"))
    VE = _Sem(nc.alloc_semaphore("VE"))
    SE = _Sem(nc.alloc_semaphore("SE"))

    O = _Ops()
    O.self_sem = {"v": VE, "s": SE, "g": GE}

    for s_w, w in ((s_wq, wqT), (s_wk, wkT), (s_wv, wvT), (s_wg, wgT), (s_wo, woT)):
        O.dma("g", GQ, lambda g, s_w=s_w, w=w: g.dma_start(
            out=s_w[:].rearrange("p (i e) -> p i e", i=8),
            in_=w.ap().rearrange("(i p) e -> p i e", p=TILE)))
    O.dma("g", GQ, lambda g: g.dma_start(out=s_gamT[:], in_=gamT.ap()))
    O.dma("g", GQ, lambda g: g.dma_start(out=s_betT[:], in_=betT.ap()))
    O.dma("g", GQ, lambda g: g.dma_start(out=s_mask[:], in_=maskt.ap()))
    O.dma("g", GQ, lambda g: g.dma_start(out=s_one[:], in_=ones_r.ap()))
    O.dma("g", GQ, lambda g: g.dma_start(out=s_onec[:], in_=ones_c.ap()))
    w_loaded = O.close_epoch(GQ)

    O.add("v", lambda v: v.memzero(s_h[0][:]), inc=VE)
    O.add("v", lambda v: v.memzero(s_eps[:]), inc=VE)
    O.add("v", lambda v: v.tensor_scalar_add(s_eps[:], s_eps[:], EPS), inc=VE)

    def row_times_wo(lhs_col_fn, gate):
        for hf in range(2):
            for j in range(8):
                O.add("t", lambda t, hf=hf, j=j: t.matmul(
                    p_pp[hf][0:1, :],
                    lhs_col_fn(j),
                    s_wo[:, 1024 * j + 512 * hf:1024 * j + 512 * (hf + 1)],
                    start=(j == 0), stop=(j == 7)),
                    wait=(gate if (hf == 0 and j == 0) else ()),
                    inc=(TE if j == 7 else None))
        te = TE.n
        for hf in range(2):
            O.add("v", lambda v, hf=hf: v.tensor_copy(
                s_row[0:1, 512 * hf:512 * (hf + 1)], p_pp[hf][0:1, :]),
                wait=([(TE, te)] if hf == 0 else ()), inc=VE)
        return te, VE.n

    def broadcast_row(dst, gate_ve):
        for hf in range(2):
            O.add("t", lambda t, hf=hf: t.matmul(
                p_pp[hf][:],
                s_one[0:1, 0:TILE],
                s_row[0:1, 512 * hf:512 * (hf + 1)],
                start=True, stop=True),
                wait=([(VE, gate_ve)] if hf == 0 else ()),
                inc=(TE if hf == 1 else None))
        te = TE.n
        for hf in range(2):
            O.add("v", lambda v, dst=dst, hf=hf: v.tensor_copy(
                dst[:, 512 * hf:512 * (hf + 1)], p_pp[hf][:]),
                wait=([(TE, te)] if hf == 0 else ()), inc=VE)
        return VE.n

    bw_te, bw_ve = row_times_wo(lambda j: s_betT[:, j:j + 1], [(GQ, w_loaded)])
    bwB_ve = broadcast_row(s_bwB, bw_ve)
    for j in range(8):
        O.add("v", lambda v, j=j: v.tensor_scalar_mul(
            s_wo[:, 1024 * j:1024 * (j + 1)],
            s_wo[:, 1024 * j:1024 * (j + 1)],
            s_gamT[:, j:j + 1]),
            wait=([(TE, bw_te)] if j == 0 else ()), inc=VE)
    fold_ve = VE.n
    ws_te, ws_ve = row_times_wo(lambda j: s_onec[:], [(VE, fold_ve)])
    wsB_ve = broadcast_row(s_wsB, ws_ve)
    init_ve = VE.n

    gc = [0]
    pv = [init_ve]
    pyf = {0: 0, 1: 0}
    pstu = {0: 0, 1: 0}
    vsu = [0]
    upd_ve = {}
    xt_ep = {}
    G = dict(st_last=0,
             y_last=0,
             qk_te=0,
             e_wo_te=0,
             fr_se={},
             ln1_se={},
             xr_store={},
             fr_ve={},
             e_wo_te2=0,
             scan0_ve=0, e0_done=0,
             yt_slot={},
             yst_slot={})

    def dump(name, src_ap, wait):
        if not DEBUG:
            return
        t = dbg[name]
        r, cn = src_ap.shape[0], src_ap.shape[1]
        O.dma("g", DQ, lambda g, t=t, s=src_ap, r=r, cn=cn: g.dma_start(
            out=t.ap()[0:r, 0:cn], in_=s), wait=wait)
        O.close_epoch(DQ)
        O.add("v", lambda v: v.engine_nop(), wait=[(DQ, DQ.n)])

    def ln_stats(src_ap, c0, wait_src):
        st = lambda i: s_stats[:, c0 + i:c0 + i + 1]
        O.add("s", lambda s: s.activation(
            s_nL[:, 1024:2048], src_ap, AF.Copy, accum_out=st(0)),
            wait=list(wait_src) + [(VE, init_ve)], inc=SE)
        O.add("s", lambda s: s.activation(
            s_nL[:, 1024:2048], src_ap, AF.Square, accum_out=st(1)), inc=SE)
        O.add("s", lambda s: s.activation(
            st(2), st(0), AF.Copy, scale=1.0 / D), inc=SE)
        O.add("s", lambda s: s.activation(
            st(3), st(2), AF.Square), inc=SE)
        O.add("s", lambda s: s.activation(
            st(4), st(3), AF.Identity, scale=-1.0, bias=s_eps[:]), inc=SE)
        O.add("s", lambda s: s.activation(
            st(5), st(1), AF.Identity, scale=1.0 / D, bias=st(4)), inc=SE)
        O.add("s", lambda s: s.activation(
            st(6), st(5), AF.Ln), inc=SE)
        O.add("s", lambda s: s.activation(
            st(7), st(6), AF.Exp, scale=-0.5), inc=SE)
        O.add("s", lambda s: s.activation(
            st(8), st(2), AF.Copy, scale=st(7)), inc=SE)
        O.add("s", lambda s: s.activation(
            st(9), st(8), AF.Copy, scale=-1.0), inc=SE)
        return SE.n

    def layernorm(src_ap, dst_ap, c0, wait_src):
        ln_stats(src_ap, c0, wait_src)
        st = lambda i: s_stats[:, c0 + i:c0 + i + 1]
        O.add("s", lambda s: s.activation(
            dst_ap, src_ap, AF.Identity, scale=st(7), bias=st(9)), inc=SE)
        return SE.n

    def bank_ap(b, n):
        return (p_pp[b][:, 0:n] if b < 2
                else p_py[:, 512 * (b - 2):512 * (b - 2) + n])

    def prep_front(tb, nt, wu, gp2):
        ntok = nt * TILE
        nch = ntok // C

        x_eps = {}

        def emit_loads():
            lw = [(SE, max([G["ln1_se"].get(tt, 0) for tt in range(nt)]))] + \
                 ([(DQ, DQ.n)] if DEBUG else [])
            for tt in range(nt):
                O.dma("g", SX, lambda g, tb=tb, tt=tt: g.dma_start(
                    out=s_x[:, 1024 * tt:1024 * (tt + 1)],
                    in_=xs.ap()[TILE * (tb + tt):TILE * (tb + tt + 1), :]),
                    wait=(lw if tt == 0 else ()))
            ep = O.close_epoch(SX)
            for tt in range(nt):
                x_eps[tt] = ep

        def u_tile(tt):
            def run():
                ep = x_eps[tt]
                d = layernorm(s_x[:, 1024 * tt:1024 * (tt + 1)],
                              s_xln[:, 1024 * tt:1024 * (tt + 1)],
                              64 + 16 * tt, [(SX, ep), (ST, xt_ep.get(0, 0))])
                G["ln1_se"][tt] = d
            return run

        def u_xt():
            for tt in range(nt):
                O.dma("sy", ST, lambda sy, tt=tt: sy.dma_start(
                    out=s_xlnT[:].rearrange("p (j q c) -> p j q c",
                                            j=8, q=4, c=TILE)[:, :, tt, :],
                    in_=s_xln[:, 1024 * tt:1024 * (tt + 1)].rearrange(
                        "p (j c) -> p j c", j=8),
                    transpose=True),
                    wait=([(SE, SE.n), (TE, G["qk_te"])] if tt == 0 else ()))
            ep = O.close_epoch(ST)
            for tt in range(nt):
                xt_ep[tt] = ep

        def u_gblk(p):
            def run():
                bank = p % 2
                w0 = [(ST, xt_ep[nt - 1]), (GQ, w_loaded),
                      (SE, G["fr_se"].get(bank, 0)),
                      (VE, G["fr_ve"].get(bank, 0))]
                for j in range(8):
                    O.add("t", lambda t, p=p, j=j, bank=bank: t.matmul(
                        p_fr[bank][:, 0:ntok],
                        s_wg[:, 1024 * j + TILE * p:1024 * j + TILE * (p + 1)],
                        s_xlnT[:, 512 * j:512 * j + ntok],
                        start=(j == 0), stop=(j == 7)),
                        wait=(w0 if j == 0 else ()), inc=(TE if j == 7 else None))
                O.add("s", lambda s, bank=bank: s.activation(
                    s_nL[:, 1024:1024 + ntok], p_fr[bank][:, 0:ntok],
                    AF.Exp, scale=-1.0),
                    wait=[(TE, TE.n)], inc=SE)
                O.add("s", lambda s, p=p: s.activation(
                    s_S[:, 512 * p:512 * p + ntok], s_nL[:, 1024:1024 + ntok],
                    AF.Ln, bias=1.0), inc=SE)
                G["fr_se"][bank] = SE.n
            return run

        def u_scan0():
            sd = SE.n
            first = True
            for cc in range(min(4, nch)):
                for p in range(8):
                    O.add("v", lambda v, cc=cc, p=p: v.tensor_tensor_scan(
                        s_nL[:, 512 * cc + C * p:512 * cc + C * (p + 1)],
                        s_S[:, 512 * p + C * cc:512 * p + C * (cc + 1)],
                        s_S[:, 512 * p + C * cc:512 * p + C * (cc + 1)],
                        0.0, ALU.add, ALU.bypass),
                        wait=([(SE, sd)] if first else ()))
                    first = False
            G["scan0_ve"] = VE.n
            O.add("s", lambda s: s.activation(
                s_E[:, 0:2048].rearrange("p (c q o) -> p c q o",
                                         q=8, o=C)[:, :, 0:6, :],
                s_nL[:, 0:2048].rearrange("p (c q o) -> p c q o",
                                          q=8, o=C)[:, :, 0:6, :],
                AF.Exp, scale=-1.0),
                wait=[(VE, G["scan0_ve"]), (TE, G["e_wo_te2"])], inc=SE)
            O.add("s", lambda s: s.activation(
                s_Er[:, 0:2048], s_nL[:, 0:2048], AF.Exp, scale=1.0), inc=SE)
            G["e0_done"] = SE.n

        class _U(list):
            pass
        units = _U([u_tile(tt) for tt in range(nt)] + [u_xt] +
                   [u_gblk(p) for p in range(8)] + [u_scan0])
        units.loads = emit_loads

        def finish(dbg_this=False):
            s_done = SE.n

            evac = {}

            def bank6(b, n):
                return p_fr[b - 4][:, 0:n] if b >= 4 else bank_ap(b, n)

            def first_use(bk):
                if bk < 2:
                    return [(VE, pv[0])]
                if bk < 4:
                    return [(SE, pyf[bk - 2])]
                return [(SE, G["fr_se"].get(bk - 4, 0)),
                        (VE, G["fr_ve"].get(bk - 4, 0))]

            for tt in range(nt):
                for hf in range(2):
                    bk = 2 + (2 * tt + hf) % 4
                    w0 = ([(VE, evac[bk])] if bk in evac else first_use(bk))
                    for j in range(8):
                        O.add("t", lambda t, tt=tt, j=j, hf=hf, bk=bk: t.matmul(
                            bank6(bk, 512),
                            s_xlnT[:, 512 * j + TILE * tt:512 * j + TILE * (tt + 1)],
                            s_wv[:, 1024 * j + 512 * hf:1024 * j + 512 * (hf + 1)],
                            start=(j == 0), stop=(j == 7)),
                            wait=(w0 if j == 0 else ()),
                            inc=(TE if j == 7 else None))
                    O.add("v", lambda v, tt=tt, hf=hf, bk=bk: v.tensor_copy(
                        s_V[:, 1024 * tt + 512 * hf:1024 * tt + 512 * (hf + 1)],
                        bank6(bk, 512)),
                        wait=[(TE, TE.n), (GE, GE.n)], inc=VE)
                    evac[bk] = VE.n
                    if bk >= 4:
                        G["fr_ve"][bk - 4] = VE.n
            v_done = VE.n

            O.add("s", lambda s, nch=nch: s.activation(
                s_E[:, 0:512 * min(4, nch)].rearrange(
                    "p (c q o) -> p c q o", q=8, o=C)[:, :, 6:8, :],
                s_nL[:, 0:512 * min(4, nch)].rearrange(
                    "p (c q o) -> p c q o", q=8, o=C)[:, :, 6:8, :],
                AF.Exp, scale=-1.0),
                wait=[(VE, G["scan0_ve"]), (TE, G["e_wo_te"]),
                      (SE, G["e0_done"])], inc=SE)
            e_done = SE.n
            if nch > 4:
                firstv = True
                for cc in range(4, nch):
                    for p in range(8):
                        O.add("v", lambda e, cc=cc, p=p: e.tensor_tensor_scan(
                            s_nL[:, 512 * (cc % 4) + C * p:512 * (cc % 4) + C * (p + 1)],
                            s_S[:, 512 * p + C * cc:512 * p + C * (cc + 1)],
                            s_S[:, 512 * p + C * cc:512 * p + C * (cc + 1)],
                            0.0, ALU.add, ALU.bypass),
                            wait=([(SE, e_done)] if firstv else ()))
                        firstv = False
                scan_done = VE.n
                O.add("s", lambda s: s.activation(
                    s_E[:, 2048:4096], s_nL[:, 0:2048], AF.Exp, scale=-1.0),
                    wait=[(VE, scan_done)], inc=SE)
                O.add("s", lambda s: s.activation(
                    s_Er[:, 2048:4096], s_nL[:, 0:2048], AF.Exp, scale=1.0),
                    inc=SE)
                e_done = SE.n
            O.add("v", lambda v, nch=nch, gp2=gp2: v.tensor_copy(
                s_ec[:, 64 * gp2:64 * (gp2 + 1)].rearrange(
                    "p (c q) -> p c q", q=8)[:, 0:nch, :],
                s_E[:].rearrange("p (c q o) -> p c q o",
                                 q=8, o=C)[:, 0:nch, :, C - 1]),
                wait=[(SE, e_done)], inc=VE)
            ec_done = VE.n

            if dbg_this:
                dump("d_xln", s_xln[:], [(VE, ec_done)])
                dump("d_S", s_S[:], [(SE, s_done)])
                dump("d_negL", s_nL[:], [(VE, ec_done)])

            rp = [2]

            def proj_qk(s_w, s_dst, s_scale, skip, winmajor):
                if skip:
                    return
                for p in range(8):
                    bank = rp[0] % 6
                    rp[0] += 1
                    w0 = ([(VE, evac[bank])] if bank in evac
                          else first_use(bank))
                    for j in range(8):
                        O.add("t", lambda t, p=p, j=j, bank=bank, s_w=s_w,
                              ntok=ntok: t.matmul(
                            bank6(bank, ntok),
                            s_w[:, 1024 * j + TILE * p:1024 * j + TILE * (p + 1)],
                            s_xlnT[:, 512 * j:512 * j + ntok],
                            start=(j == 0), stop=(j == 7)),
                            wait=(w0 if j == 0 else ()),
                            inc=(TE if j == 7 else None))
                    if winmajor:
                        out_ap = (lambda p=p: s_dst[:].rearrange(
                            "p (w q e o) -> p w q e o",
                            w=4, q=8, e=2, o=C)[:, 0:nch // 2, p, :, :])
                    else:
                        out_ap = (lambda p=p: s_dst[:, 512 * p:512 * p + ntok]
                                  .rearrange("p (c o) -> p c o", o=C))
                    O.add("v", lambda v, p=p, bank=bank, s_scale=s_scale,
                          out_ap=out_ap, ntok=ntok, nch=nch:
                          v.tensor_tensor(
                              out_ap(),
                              bank6(bank, ntok).rearrange("p (c o) -> p c o", o=C),
                              s_scale[:].rearrange("p (c q o) -> p c q o",
                                                   q=8, o=C)[:, 0:nch, p, :],
                              ALU.mult),
                          wait=[(TE, TE.n), (SE, e_done)], inc=VE)
                    evac[bank] = VE.n
                    if bank >= 4:
                        G["fr_ve"][bank - 4] = VE.n
            proj_qk(s_wq, s_qt, s_E, wu, False)
            proj_qk(s_wk, s_kt, s_Er, False, True)
            qk_done = VE.n
            G["qk_te"] = TE.n

            if dbg_this:
                dump("d_qt", s_qt[:], [(VE, qk_done)])
                dump("d_kt", s_kt[:], [(VE, qk_done)])
                dump("d_V", s_V[:], [(VE, v_done)])

            return dict(v_done=v_done, ec_done=ec_done, qk_done=qk_done,
                        nch=nch, gp2=gp2)

        return units, finish

    def emit_yln(w):
        return ln_stats(s_y[:, 1024 * (w % 2):1024 * (w % 2 + 1)],
                        16 * w, [(VE, VE.n)])

    def emit_loop(tb, nt, wu, FR, splice, dbg_this=False):
        nch = FR["nch"]
        gp2 = FR["gp2"]
        half_cols = 512 * nch // 2
        O.dma("sy", KT, lambda sy, half_cols=half_cols: sy.dma_start(
            out=s_ktm[:, 0:half_cols].rearrange("p (m c) -> p m c", c=TILE),
            in_=s_kt[:, 0:half_cols].rearrange("p (m c) -> p m c", c=TILE),
            transpose=True),
            wait=[(VE, FR["qk_done"]), (TE, G["st_last"])])
        ktm_ep0 = O.close_epoch(KT)
        O.dma("sy", KT, lambda sy, half_cols=half_cols: sy.dma_start(
            out=s_ktm[:, half_cols:2 * half_cols].rearrange(
                "p (m c) -> p m c", c=TILE),
            in_=s_kt[:, half_cols:2 * half_cols].rearrange(
                "p (m c) -> p m c", c=TILE),
            transpose=True))
        ktm_ep1 = O.close_epoch(KT)
        yt_eps = {}
        yst = {}
        xr_ep = {}

        def xr_load(w):
            if wu or w >= nt:
                return
            O.dma("g", GX, lambda g, w=w, tb=tb: g.dma_start(
                out=s_xr[:, 1024 * (w % 2):1024 * (w % 2 + 1)],
                in_=xs.ap()[TILE * (tb + w):TILE * (tb + w + 1), :]),
                wait=[(XQ, G["xr_store"].get(w % 2, 0))])
            xr_ep[w] = O.close_epoch(GX)

        xr_load(0)
        xr_load(1)
        if splice:
            splice.loads()
        vsg = {}

        def emit_vs(w2):
            O.add("g", lambda g_, w2=w2: g_.tensor_copy(
                s_VS[0:64, :],
                s_V[64:128, 1024 * w2:1024 * (w2 + 1)]),
                wait=[(VE, FR["v_done"]), (TE, vsu[0])], inc=GE)
            O.add("g", lambda g_, w2=w2: g_.tensor_copy(
                s_VS[64:128, :],
                s_V[0:64, 1024 * w2:1024 * (w2 + 1)]),
                inc=GE)
            vsg[w2] = GE.n

        if not wu:
            emit_vs(0)

        def emit_be_late(w):
            w0 = [(YT, yt_eps[w]),
                  (VE, FR["qk_done"] if w == 0 else pv[0])]
            for j in range(8):
                for hf in range(2):
                    O.add("t", lambda t, w=w, j=j, hf=hf: t.matmul(
                        p_pp[hf][:],
                        s_E[:, 512 * j + TILE * w:512 * j + TILE * (w + 1)],
                        s_wo[:, 1024 * j + 512 * hf:1024 * j + 512 * (hf + 1)],
                        start=(j == 0), stop=(j == 7)),
                        wait=(w0 if (j == 0 and hf == 0) else ()),
                        inc=(TE if (j == 7 and hf == 1) else None))
            G["e_wo_te2"] = G["e_wo_te"]
            G["e_wo_te"] = TE.n
            r_ap = s_stats[:, 16 * w + 7:16 * w + 8]
            nrm_ap = s_stats[:, 16 * w + 9:16 * w + 10]
            xsl = 1024 * (w % 2)
            for hf in range(2):
                sl = slice(512 * hf, 512 * (hf + 1))
                O.add("v", lambda v, hf=hf, sl=sl, r_ap=r_ap:
                      v.scalar_tensor_tensor(
                          p_pp[hf][:], p_pp[hf][:], r_ap, s_bwB[:, sl],
                          ALU.mult, ALU.add),
                      wait=([(TE, TE.n), (SE, yst[w])] if hf == 0 else ()),
                      inc=VE)
                O.add("v", lambda v, hf=hf, sl=sl, nrm_ap=nrm_ap:
                      v.scalar_tensor_tensor(
                          p_pp[hf][:], s_wsB[:, sl], nrm_ap,
                          p_pp[hf][:], ALU.mult, ALU.add),
                      inc=VE)
                O.add("v", lambda v, hf=hf, sl=sl, xsl=xsl: v.tensor_add(
                    s_xr[:, xsl + sl.start:xsl + sl.stop],
                    s_xr[:, xsl + sl.start:xsl + sl.stop],
                    p_pp[hf][:]),
                    wait=([(GX, xr_ep[w])] if hf == 0 else ()), inc=VE)
            pv[0] = VE.n
            O.dma("g", XQ, lambda g, w=w, tb=tb, xsl=xsl: g.dma_start(
                out=x2d.ap()[TILE * (tb + w - 2):TILE * (tb + w - 1), :],
                in_=s_xr[:, xsl:xsl + 1024]),
                wait=[(VE, VE.n)])
            G["xr_store"][w % 2] = O.close_epoch(XQ)
            xr_load(w + 2)
            if dbg_this and w == 0:
                dump("d_x2", s_xr[:, 0:1024], [(VE, VE.n)])

        npair = nch // 2
        for cc in range(nch):
            par = cc % 2
            w = cc // 2
            hs = gc[0] % 3
            hn = (gc[0] + 1) % 3
            ysl = 1024 * (w % 2)
            vs_done = vsg.get(w, 0)

            if not wu:
                for h in range(NH):
                    hp, hq = h % 2, h // 2
                    O.add("t", lambda t, w=w, h=h, hp=hp, hq=hq, par=par, cc=cc:
                          t.matmul(
                        p_py[64 * hp:64 * (hp + 1),
                             512 * par + 64 * hq:512 * par + 64 * (hq + 1)],
                        s_kt[64 * hp:64 * (hp + 1),
                             1024 * w + TILE * hq + C * par:
                             1024 * w + TILE * hq + C * (par + 1)],
                        s_qt[64 * hp:64 * (hp + 1),
                             512 * hq + C * cc:512 * hq + C * (cc + 1)],
                        start=True, stop=True),
                        wait=([(VE, FR["qk_done"]), (SE, pyf[par])]
                              if h == 0 else ()),
                        inc=(TE if h == NH - 1 else None))
                p_mm_done = TE.n
                O.add("v", lambda v, par=par: v.tensor_tensor(
                    s_P[:, 512 * par:512 * (par + 1)],
                    p_py[:, 512 * par:512 * (par + 1)],
                    s_mask[:, 512 * par:512 * (par + 1)], ALU.mult),
                    wait=[(TE, p_mm_done)], inc=VE)
                pmask_done = VE.n

            for h in range(NH):
                hp, hq = h % 2, h // 2
                O.add("t", lambda t, w=w, h=h, hp=hp, hq=hq, par=par:
                      t.matmul(
                          p_st[64 * par:64 * (par + 1), 64 * h:64 * (h + 1)],
                          s_ktm[64 * par:64 * (par + 1),
                                1024 * w + TILE * hq + 64 * hp:
                                1024 * w + TILE * hq + 64 * (hp + 1)],
                          s_V[64 * par:64 * (par + 1),
                              1024 * w + 64 * h:1024 * w + 64 * (h + 1)],
                          start=True, stop=True),
                      wait=([(KT, ktm_ep0 if w < (nch + 3) // 4 else ktm_ep1),
                             (VE, pstu[par])] if h == 0 else ()),
                      inc=(TE if h == NH - 1 else None))
            st_mm = TE.n
            G["st_last"] = st_mm

            def emit_update(cc=cc, par=par, hs=hs, hn=hn, st_mm=st_mm):
                for hp in range(2):
                    O.add("v", lambda v, par=par, hp=hp, hs=hs: v.tensor_tensor(
                        s_ht[64 * hp:64 * (hp + 1), :].rearrange(
                            "p (q o) -> p q o", o=C),
                        p_st[64 * par:64 * (par + 1), :].rearrange(
                            "p (q two o) -> p q two o", two=2, o=C)[:, :, hp, :],
                        s_h[hs][64 * hp:64 * (hp + 1), :].rearrange(
                            "p (q o) -> p q o", o=C),
                        ALU.add),
                        wait=([(TE, st_mm), (VE, FR["ec_done"])]
                              if hp == 0 else ()), inc=VE)
                pstu[par] = VE.n
                O.add("v", lambda v, cc=cc, hn=hn: v.tensor_tensor(
                    s_h[hn][:].rearrange("p (q o) -> p q o", o=C),
                    s_ht[:].rearrange("p (q o) -> p q o", o=C),
                    s_ec[:, 64 * gp2 + 8 * cc:64 * gp2 + 8 * cc + 8]
                    .unsqueeze(2).broadcast_to([TILE, 8, C]),
                    ALU.mult), inc=VE)
                upd_ve[hn] = VE.n
            if wu or par == 0:
                emit_update()

            if (not wu) and par == 1:
                ug = max(upd_ve.get((gc[0] - 1) % 3, 0),
                         upd_ve.get(gc[0] % 3, 0))
                for c2 in (cc - 1, cc):
                    par2 = c2 % 2
                    hs2 = (gc[0] - (cc - c2)) % 3
                    for h in range(NH):
                        hp, hq = h % 2, h // 2
                        rhs_v = (
                            (lambda t, c2=c2, hp=hp, h=h:
                             s_V[64 * hp:64 * (hp + 1),
                                 1024 * (c2 // 2) + 64 * h:1024 * (c2 // 2) + 64 * (h + 1)])
                            if hp == par2 else
                            (lambda t, hp=hp, h=h:
                             s_VS[64 * hp:64 * (hp + 1), 64 * h:64 * (h + 1)])
                        )
                        O.add("t", lambda t, c2=c2, par2=par2, h=h, hp=hp, hq=hq,
                              rhs_v=rhs_v: t.matmul(
                                  p_py[64 * hp:64 * (hp + 1),
                                       512 * par2 + 64 * hq:512 * par2 + 64 * (hq + 1)],
                                  s_P[64 * hp:64 * (hp + 1),
                                      512 * par2 + 64 * hq:512 * par2 + 64 * (hq + 1)],
                                  rhs_v(t),
                                  start=True, stop=False),
                              wait=(([(VE, max(pmask_done, ug)), (GE, vs_done)])
                                    if h == 0 else ()))
                        O.add("t", lambda t, c2=c2, par2=par2, h=h, hp=hp, hq=hq,
                              hs2=hs2: t.matmul(
                                  p_py[64 * hp:64 * (hp + 1),
                                       512 * par2 + 64 * hq:512 * par2 + 64 * (hq + 1)],
                                  s_qt[64 * hp:64 * (hp + 1),
                                       512 * hq + C * c2:512 * hq + C * (c2 + 1)],
                                  s_h[hs2][64 * hp:64 * (hp + 1),
                                           64 * hq:64 * (hq + 1)],
                                  start=False, stop=True),
                              inc=(TE if (h == NH - 1 and c2 == cc) else None))
                G["y_last"] = TE.n
                vsu[0] = TE.n
                if w + 1 < npair:
                    emit_vs(w + 1)
                for par2 in (0, 1):
                    for hp in (0, 1):
                        O.add("s", lambda s, ysl=ysl, par2=par2, hp=hp:
                              s.activation(
                            s_y[64 * par2:64 * (par2 + 1),
                                ysl:ysl + 1024].rearrange(
                                "p (q two o) -> p q two o", two=2, o=C)[:, :, hp, :],
                            p_py[64 * hp:64 * (hp + 1),
                                 512 * par2:512 * (par2 + 1)].rearrange(
                                "p (q o) -> p q o", o=C),
                            AF.Copy),
                            wait=(([(TE, TE.n),
                                    (YT, G["yt_slot"].get(w % 2, 0))])
                                  if (par2 == 0 and hp == 0) else ()),
                            inc=SE)
                    pyf[par2] = SE.n
                O.dma("sy", YT, lambda sy, w=w, ysl=ysl: sy.dma_start(
                    out=s_E[:].rearrange("p (j q c) -> p j q c",
                                         j=8, q=4, c=TILE)[:, :, w, :],
                    in_=s_y[:, ysl:ysl + 1024].rearrange(
                        "p (j c) -> p j c", j=8),
                    transpose=True),
                    wait=[(SE, SE.n)])
                yt_eps[w] = O.close_epoch(YT)
                G["yt_slot"][w % 2] = yt_eps[w]
                emit_update()
                if splice:
                    sched = {4: [5, 4, 4, 0], 2: [6, 7]}[npair]
                    for _ in range(min(sched[w], len(splice))):
                        splice.pop(0)()
                yst[w] = emit_yln(w)
                G["yst_slot"][w % 2] = yst[w]
                if w > 0:
                    emit_be_late(w - 1)
                if splice and w == npair - 1:
                    while splice:
                        splice.pop(0)()
            elif splice and wu and par == 1:
                sched = {2: [6, 7]}[npair]
                for _ in range(min(sched[w], len(splice))):
                    splice.pop(0)()
                if w == npair - 1:
                    while splice:
                        splice.pop(0)()
            gc[0] += 1

        if not wu:
            emit_be_late(nt - 1)
        if dbg_this:
            dump("d_y", s_y[:], [(VE, VE.n)])
            dump("d_h", s_h[gc[0] % 3][:], [(VE, VE.n)])

    ngrp = 0 if TRUNC in (1, 2) else (1 if TRUNC == 3 else 4)
    if TRUNC != 1:
        units, fin = prep_front(0, 2, True, 0)
        units.loads()
        for u in units:
            u()
        FR = fin()
        if ngrp:
            units, fin = prep_front(2, 4, False, 1)
        else:
            units = []
        emit_loop(0, 2, True, FR, units)
    for gi in range(ngrp):
        FR = fin(dbg_this=(DEBUG and gi == 0))
        if gi < ngrp - 1:
            units, fin = prep_front(2 + 4 * (gi + 1), 4, False, gi % 2)
        else:
            units = []
        emit_loop(2 + 4 * gi, 4, False, FR, units,
                  dbg_this=(DEBUG and gi == 0))
    a_sx, a_st, a_kt, a_yt = SX.n, ST.n, KT.n, YT.n
    a_gq, a_xq, a_dq, a_gx, a_ge = GQ.n, XQ.n, DQ.n, GX.n, GE.n
    a_te, a_ve, a_se = TE.n, VE.n, SE.n

    es.close()

    es2 = ExitStack()
    E2 = es2.enter_context
    s_x2T = E2(nc.sbuf_tensor("s_x2T", [TILE, 16 * D], BF16))
    s_hff = E2(nc.sbuf_tensor("s_hff", [TILE, NHC * 2048], BF16))
    s_wd = E2(nc.sbuf_tensor("s_wd", [TILE, NHC * 1024], BF16))
    s_wgu = [E2(nc.sbuf_tensor(f"s_wgu{i}", [TILE, 2048], BF16)) for i in range(2)]
    s_x2 = E2(nc.sbuf_tensor("s_x2", [TILE, 2048], F32))
    s_st2 = E2(nc.sbuf_tensor("s_st2", [TILE, 16], F32))
    s_sq2 = E2(nc.sbuf_tensor("s_sq2", [TILE, 1024], F32))
    s_a2 = E2(nc.sbuf_tensor("s_a2", [TILE, 1024], BF16))
    s_act2 = E2(nc.sbuf_tensor("s_act2", [TILE, 512], F32))
    p_g = [E2(nc.psum_tensor(f"p_g{i}", [TILE, 512], F32)) for i in range(2)]
    p_u = [E2(nc.psum_tensor(f"p_u{i}", [TILE, 512], F32)) for i in range(2)]
    p_d = [E2(nc.psum_tensor(f"p_d{i}", [TILE, 1024], F32)) for i in range(2)]

    barrier = [(TE, a_te), (VE, a_ve), (SE, a_se), (SX, a_sx), (ST, a_st),
               (KT, a_kt), (YT, a_yt), (GQ, a_gq), (XQ, a_xq), (DQ, a_dq),
               (GX, a_gx), (GE, a_ge)]
    first = {k: True for k in ("t", "v", "s", "sy", "g")}

    def addD(eng, fn, wait=(), inc=None, inc_by=1):
        w = list(wait)
        if first[eng]:
            w = barrier + w
            first[eng] = False
        O.add(eng, fn, wait=w, inc=inc, inc_by=inc_by)

    def dmaD(eng, sem, fn, wait=()):
        w = list(wait)
        if first[eng]:
            w = barrier + w
            first[eng] = False
        if sem.pending is not None:
            w.append((sem, sem.pending))
            sem.pending = None
        O.add(eng, fn, wait=w, inc=sem, inc_by=16)

    a2_done = {}
    ln2q = {}
    for tt in range(16 if TRUNC == 0 else 0):
        buf = tt % 2
        dmaD("g", GX, lambda g, tt=tt, buf=buf: g.dma_start(
            out=s_x2[:, 1024 * buf:1024 * (buf + 1)],
            in_=x2d.ap()[TILE * tt:TILE * (tt + 1), :]),
            wait=([(VE, a2_done[tt - 2])] if tt >= 2 else []))
        ld = O.close_epoch(GX)
        xt = s_x2[:, 1024 * buf:1024 * (buf + 1)]
        addD("v", lambda v, xt=xt: v.reduce_sum(
            s_st2[:, 0:1], xt, axis=AX.X), wait=[(GX, ld)], inc=VE)
        addD("s", lambda s, xt=xt: s.activation(
            s_sq2[:], xt, AF.Square, accum_out=s_st2[:, 1:2]),
            wait=[(GX, ld), (VE, VE.n)], inc=SE)
        addD("v", lambda v: v.tensor_scalar(
            s_st2[:, 2:3], s_st2[:, 0:1], 1.0 / D, None, ALU.mult), inc=VE)
        addD("v", lambda v: v.tensor_mul(
            s_st2[:, 3:4], s_st2[:, 2:3], s_st2[:, 2:3]), inc=VE)
        addD("v", lambda v: v.tensor_scalar(
            s_st2[:, 4:5], s_st2[:, 1:2], 1.0 / D, s_st2[:, 3:4],
            ALU.mult, ALU.subtract), wait=[(SE, SE.n)], inc=VE)
        addD("v", lambda v: v.tensor_scalar_add(
            s_st2[:, 4:5], s_st2[:, 4:5], EPS), inc=VE)
        addD("s", lambda s: s.activation(
            s_st2[:, 5:6], s_st2[:, 4:5], AF.Ln), wait=[(VE, VE.n)], inc=SE)
        addD("s", lambda s: s.activation(
            s_st2[:, 6:7], s_st2[:, 5:6], AF.Exp, scale=-0.5), inc=SE)
        addD("v", lambda v, xt=xt: v.tensor_scalar(
            s_a2[:], xt, s_st2[:, 2:3], s_st2[:, 6:7], ALU.subtract, ALU.mult),
            wait=[(SE, SE.n), (ST, ST.n)], inc=VE)
        a2_done[tt] = VE.n
        dmaD("sy", ST, lambda sy, tt=tt: sy.dma_start(
            out=s_x2T[:].rearrange("p (j q c) -> p j q c",
                                   j=8, q=16, c=TILE)[:, :, tt, :],
            in_=s_a2[:].rearrange("p (j c) -> p j c", j=8), transpose=True),
            wait=[(VE, a2_done[tt])])
        if tt % 4 == 3:
            ln2q[tt // 4] = O.close_epoch(ST)
    ln2_tp = ST.n

    if TRUNC == 0:
        dmaD("g", GQ, lambda g: g.dma_start(
            out=s_wd[:].rearrange("p (hc d) -> p hc d", hc=NHC),
            in_=wdi.ap().rearrange("(hc p) d -> p hc d", p=TILE)))
    wd_done = O.close_epoch(GQ)

    wgu_users = {}
    silu_of = {}
    hff_of = {}
    for hc in range(NHC if TRUNC == 0 else 0):
        buf = hc % 2
        dmaD("g", GQ, lambda g, hc=hc, buf=buf: g.dma_start(
            out=s_wgu[buf][:, 0:1024].rearrange("p (i h) -> p i h", i=8),
            in_=wgi.ap()[:, TILE * hc:TILE * (hc + 1)].rearrange(
                "(i p) h -> p i h", p=TILE)),
            wait=[(TE, wgu_users.get(buf, 0))])
        dmaD("g", GQ, lambda g, hc=hc, buf=buf: g.dma_start(
            out=s_wgu[buf][:, 1024:2048].rearrange("p (i h) -> p i h", i=8),
            in_=wui.ap()[:, TILE * hc:TILE * (hc + 1)].rearrange(
                "(i p) h -> p i h", p=TILE)))
        gu_ld = O.close_epoch(GQ)
        for t4 in range(4):
            pb = t4 % 2
            w0 = [(GQ, gu_ld), (ST, ln2q.get(t4, ln2_tp)),
                  (SE, silu_of.get(pb, 0))]
            for j in range(8):
                addD("t", lambda t, buf=buf, t4=t4, j=j, pb=pb: t.matmul(
                    p_g[pb][:],
                    s_wgu[buf][:, TILE * j:TILE * (j + 1)],
                    s_x2T[:, 2048 * j + 512 * t4:2048 * j + 512 * (t4 + 1)],
                    start=(j == 0), stop=(j == 7)),
                    wait=(w0 if j == 0 else ()), inc=(TE if j == 7 else None))
            addD("s", lambda s, pb=pb: s.activation(
                s_act2[:], p_g[pb][:], AF.Silu, scale=SCL),
                wait=[(TE, TE.n), (VE, hff_of.get(1 - pb, 0))], inc=SE)
            silu_of[pb] = SE.n
            silu_done = SE.n
            w1 = [(VE, hff_of.get(pb, 0))]
            for j in range(8):
                addD("t", lambda t, buf=buf, t4=t4, j=j, pb=pb: t.matmul(
                    p_u[pb][:],
                    s_wgu[buf][:, 1024 + TILE * j:1024 + TILE * (j + 1)],
                    s_x2T[:, 2048 * j + 512 * t4:2048 * j + 512 * (t4 + 1)],
                    start=(j == 0), stop=(j == 7)),
                    wait=(w1 if j == 0 else ()), inc=(TE if j == 7 else None))
            addD("v", lambda v, hc=hc, t4=t4, pb=pb: v.scalar_tensor_tensor(
                s_hff[:, 2048 * hc + 512 * t4:2048 * hc + 512 * (t4 + 1)],
                p_u[pb][:], SCL * SCL, s_act2[:], ALU.mult, ALU.mult),
                wait=[(TE, TE.n), (SE, silu_done)], inc=VE)
            hff_of[pb] = VE.n
        wgu_users[buf] = TE.n
    hff_done = VE.n

    resid_of = {}
    out_st = {}
    for ts in range(16 if TRUNC == 0 else 0):
        buf = ts % 2
        dmaD("g", GX, lambda g, ts=ts, buf=buf: g.dma_start(
            out=s_x2[:, 1024 * buf:1024 * (buf + 1)],
            in_=x2d.ap()[TILE * ts:TILE * (ts + 1), :]),
            wait=[(VE, resid_of.get(buf, 0)), (XQ, out_st.get(buf, 0))])
        x2_ld = O.close_epoch(GX)
        w0 = [(VE, hff_done if ts == 0 else resid_of.get(buf, 0)), (GQ, wd_done)]
        for hf in range(2):
            for hcc in range(NHC):
                addD("t", lambda t, ts=ts, buf=buf, hf=hf, hcc=hcc: t.matmul(
                    p_d[buf][:, 512 * hf:512 * (hf + 1)],
                    s_hff[:, 2048 * hcc + TILE * ts:2048 * hcc + TILE * (ts + 1)],
                    s_wd[:, 1024 * hcc + 512 * hf:1024 * hcc + 512 * (hf + 1)],
                    start=(hcc == 0), stop=(hcc == NHC - 1)),
                    wait=(w0 if (hf == 0 and hcc == 0) else ()),
                    inc=(TE if (hf == 1 and hcc == NHC - 1) else None))
        addD("v", lambda v, buf=buf: v.tensor_add(
            s_x2[:, 1024 * buf:1024 * (buf + 1)],
            s_x2[:, 1024 * buf:1024 * (buf + 1)], p_d[buf][:]),
            wait=[(TE, TE.n), (GX, x2_ld)], inc=VE)
        resid_of[buf] = VE.n
        dmaD("g", XQ, lambda g, ts=ts, buf=buf: g.dma_start(
            out=out.ap()[TILE * ts:TILE * (ts + 1), :],
            in_=s_x2[:, 1024 * buf:1024 * (buf + 1)]),
            wait=[(VE, VE.n)])
        out_st[buf] = O.close_epoch(XQ)
        if DEBUG and ts == 0:
            dump("d_hff", s_hff[:, 0:2048], [(VE, hff_done)])

    final_xq = XQ.n
    final_dq = DQ.n

    with nc.Block() as block:
        @block.tensor
        def _(t):
            for f in O.q["t"]:
                f(t)

        @block.vector
        def _(v):
            for f in O.q["v"]:
                f(v)

        @block.scalar
        def _(s):
            for f in O.q["s"]:
                f(s)

        @block.sync
        def _(sy):
            for f in O.q["sy"]:
                f(sy)

        @block.gpsimd
        def _(g):
            for f in O.q["g"]:
                f(g)
            g.wait_ge(XQ.h, final_xq)
            if final_dq:
                g.wait_ge(DQ.h, final_dq)

    es2.close()
    return nc


def _shard_inputs(inputs):
    x = np.asarray(inputs["L0_attn_x"], np.float32).reshape(16384, D)
    wqT = np.ascontiguousarray(np.asarray(inputs["L0_attn_Wq"], np.float32).T)
    wkT = np.ascontiguousarray(np.asarray(inputs["L0_attn_Wk"], np.float32).T)
    wvT = np.ascontiguousarray(np.asarray(inputs["L0_attn_Wv"], np.float32).T)
    wgT = np.ascontiguousarray(np.asarray(inputs["L0_attn_Wg"], np.float32).T)
    woT = np.ascontiguousarray(np.asarray(inputs["L0_attn_Wo"], np.float32).T)
    gamT = np.ascontiguousarray(
        np.asarray(inputs["L0_attn_gamma"], np.float32).reshape(8, TILE).T)
    betT = np.ascontiguousarray(
        np.asarray(inputs["L0_attn_beta"], np.float32).reshape(8, TILE).T)
    wgi = np.ascontiguousarray(np.asarray(inputs["L0_ffn_w_gate"], np.int32).T)
    wui = np.ascontiguousarray(np.asarray(inputs["L0_ffn_w_up"], np.int32).T)
    wdi = np.ascontiguousarray(np.asarray(inputs["L0_ffn_w_down"], np.int32).T)
    in_maps = []
    for c in range(NCORES):
        lo = c * T_OWN
        if c == 0:
            xsl = np.concatenate([np.zeros((WU, D), np.float32), x[0:T_OWN]], 0)
        else:
            xsl = np.ascontiguousarray(x[lo - WU:lo + T_OWN])
        in_maps.append(dict(xs=xsl, wqT=wqT, wkT=wkT, wvT=wvT, wgT=wgT, woT=woT,
                            gamT=gamT, betT=betT, wgi=wgi, wui=wui, wdi=wdi))
    return in_maps


def kernel(**inputs):
    in_maps = _shard_inputs(inputs)
    nc = build_program()
    res = run_bass_kernel_spmd(nc, in_maps, core_ids=list(range(NCORES)))
    outs = [np.asarray(res.results[c]["out"]) for c in range(NCORES)]
    full = np.concatenate(outs, axis=0).reshape(1, 16384, D)
    if DEBUG:
        kernel.last_results = res
    return full.astype(np.float32)


# version 21
# speedup vs baseline: 1.0443x; 1.0443x over previous
import numpy as np
from contextlib import ExitStack

import concourse.bass as bass
import concourse.mybir as mybir
from concourse.bass_utils import run_bass_kernel_spmd

F32 = mybir.dt.float32
BF16 = mybir.dt.bfloat16
I32 = mybir.dt.int32
AF = mybir.ActivationFunctionType
ALU = mybir.AluOpType
AX = mybir.AxisListType

NCORES = 8
D = 1024
NH = 16
C = 64
TILE = 128
T_OWN = 2048
WU = 256
T_ALL = T_OWN + WU
HID = 3072
NHC = HID // TILE
SCL = 1.0 / 127.0
EPS = 1e-5

DEBUG = False
TRUNC = 0


class _Sem:
    def __init__(self, h):
        self.h = h
        self.n = 0
        self.pending = None


class _Ops:

    def __init__(self):
        self.q = {k: [] for k in ("t", "v", "s", "sy", "g")}
        self.self_sem = {}

    def add(self, eng, fn, wait=(), inc=None, inc_by=1):
        wait = list(wait)
        ss = self.self_sem.get(eng)
        if ss is not None:
            wait.append((ss, ss.n))
            if inc is None:
                inc, inc_by = ss, 1
        if inc is not None:
            inc.n += inc_by
        w = tuple((s, v) for (s, v) in wait if v > 0)

        def run(e, fn=fn, w=w, inc=inc, inc_by=inc_by):
            for sem, val in w:
                e.wait_ge(sem.h, val)
            ins = fn(e)
            if inc is not None and ins is not None:
                ins.then_inc(inc.h, inc_by)
        self.q[eng].append(run)

    def dma(self, eng, sem, fn, wait=()):
        w = list(wait)
        if sem.pending is not None:
            w.append((sem, sem.pending))
            sem.pending = None
        self.add(eng, fn, wait=w, inc=sem, inc_by=16)

    def close_epoch(self, sem):
        sem.pending = sem.n
        return sem.n


def build_program():
    nc = bass.Bass("TRN2", target_bir_lowering=False)

    xs = nc.dram_tensor("xs", [T_ALL, D], F32, kind="ExternalInput")
    wqT = nc.dram_tensor("wqT", [D, D], F32, kind="ExternalInput")
    wkT = nc.dram_tensor("wkT", [D, D], F32, kind="ExternalInput")
    wvT = nc.dram_tensor("wvT", [D, D], F32, kind="ExternalInput")
    wgT = nc.dram_tensor("wgT", [D, D], F32, kind="ExternalInput")
    woT = nc.dram_tensor("woT", [D, D], F32, kind="ExternalInput")
    gamT = nc.dram_tensor("gamT", [TILE, 8], F32, kind="ExternalInput")
    betT = nc.dram_tensor("betT", [TILE, 8], F32, kind="ExternalInput")
    wgi = nc.dram_tensor("wgi", [D, HID], I32, kind="ExternalInput")
    wui = nc.dram_tensor("wui", [D, HID], I32, kind="ExternalInput")
    wdi = nc.dram_tensor("wdi", [HID, D], I32, kind="ExternalInput")
    out = nc.dram_tensor("out", [T_OWN, D], F32, kind="ExternalOutput")

    x2d = nc.dram_tensor("x2d", [T_OWN, D], F32)

    mask_np = np.tile(np.triu(np.ones((C, C), np.float32)), (2, NH))
    maskt = nc.inline_tensor(mask_np, "mask_rep")
    ones_r = nc.inline_tensor(np.ones((1, TILE), np.float32), "ones_r")
    ones_c = nc.inline_tensor(np.ones((TILE, 1), np.float32), "ones_c")

    dbg = {}
    if DEBUG:
        for name, shape in [
            ("d_xln", [TILE, 4096]), ("d_S", [TILE, 4096]), ("d_negL", [TILE, 2048]),
            ("d_qt", [TILE, 4096]), ("d_kt", [TILE, 4096]), ("d_V", [TILE, 4096]),
            ("d_h", [TILE, 512]), ("d_y", [TILE, 4096]),
            ("d_x2", [TILE, 1024]), ("d_hff", [TILE, 2048]),
        ]:
            dbg[name] = nc.dram_tensor(name, shape, F32, kind="ExternalOutput")

    es = ExitStack()
    E_ = es.enter_context
    s_wq = E_(nc.sbuf_tensor("s_wq", [TILE, 8 * D], BF16))
    s_wk = E_(nc.sbuf_tensor("s_wk", [TILE, 8 * D], BF16))
    s_wv = E_(nc.sbuf_tensor("s_wv", [TILE, 8 * D], BF16))
    s_wg = E_(nc.sbuf_tensor("s_wg", [TILE, 8 * D], BF16))
    s_wo = E_(nc.sbuf_tensor("s_wo", [TILE, 8 * D], BF16))
    s_gamT = E_(nc.sbuf_tensor("s_gamT", [TILE, 8], F32))
    s_betT = E_(nc.sbuf_tensor("s_betT", [TILE, 8], BF16))
    s_row = E_(nc.sbuf_tensor("s_row", [1, D], BF16))
    s_one = E_(nc.sbuf_tensor("s_one", [1, TILE], BF16))
    s_onec = E_(nc.sbuf_tensor("s_onec", [TILE, 1], BF16))
    s_bwB = E_(nc.sbuf_tensor("s_bwB", [TILE, D], BF16))
    s_wsB = E_(nc.sbuf_tensor("s_wsB", [TILE, D], BF16))
    s_eps = E_(nc.sbuf_tensor("s_eps", [TILE, 1], F32))
    s_mask = E_(nc.sbuf_tensor("s_mask", [TILE, 1024], BF16))
    s_x = E_(nc.sbuf_tensor("s_x", [TILE, 4096], F32))
    s_xln = E_(nc.sbuf_tensor("s_xln", [TILE, 4096], BF16))
    s_xlnT = E_(nc.sbuf_tensor("s_xlnT", [TILE, 4096], BF16))
    s_S = E_(nc.sbuf_tensor("s_S", [TILE, 4096], BF16))
    s_nL = E_(nc.sbuf_tensor("s_nL", [TILE, 2048], F32))
    s_E = E_(nc.sbuf_tensor("s_E", [TILE, 4096], BF16))
    s_Er = E_(nc.sbuf_tensor("s_Er", [TILE, 4096], BF16))
    s_ec = E_(nc.sbuf_tensor("s_ec", [TILE, 128], F32))
    s_qt = E_(nc.sbuf_tensor("s_qt", [TILE, 4096], BF16))
    s_kt = E_(nc.sbuf_tensor("s_kt", [TILE, 4096], BF16))
    s_V = E_(nc.sbuf_tensor("s_V", [TILE, 4096], BF16))
    s_ktm = E_(nc.sbuf_tensor("s_ktm", [TILE, 4096], BF16))
    s_P = E_(nc.sbuf_tensor("s_P", [TILE, 1024], BF16))
    s_h = [E_(nc.sbuf_tensor(f"s_h{i}", [TILE, 512], BF16)) for i in range(3)]
    s_y = E_(nc.sbuf_tensor("s_y", [TILE, 2048], BF16))
    s_ht = E_(nc.sbuf_tensor("s_ht", [TILE, 512], F32))
    s_VS = E_(nc.sbuf_tensor("s_VS", [TILE, 1024], BF16))
    s_xr = E_(nc.sbuf_tensor("s_xr", [TILE, 2048], F32))
    s_stats = E_(nc.sbuf_tensor("s_stats", [TILE, 128], F32))
    p_pp = [E_(nc.psum_tensor(f"p_pp{i}", [TILE, 512], F32)) for i in range(2)]
    p_py = E_(nc.psum_tensor("p_py", [TILE, 1024], F32))
    p_st = E_(nc.psum_tensor("p_st", [TILE, 1024], F32))
    p_fr = [E_(nc.psum_tensor(f"p_fr{i}", [TILE, 512], F32)) for i in range(2)]

    SX = _Sem(nc.alloc_semaphore("SX"))
    ST = _Sem(nc.alloc_semaphore("ST"))
    KT = _Sem(nc.alloc_semaphore("KT"))
    YT = _Sem(nc.alloc_semaphore("YT"))
    GQ = _Sem(nc.alloc_semaphore("GQ"))
    GX = _Sem(nc.alloc_semaphore("GX"))
    XQ = _Sem(nc.alloc_semaphore("XQ"))
    DQ = _Sem(nc.alloc_semaphore("DQ"))
    GE = _Sem(nc.alloc_semaphore("GE"))
    TE = _Sem(nc.alloc_semaphore("TE"))
    VE = _Sem(nc.alloc_semaphore("VE"))
    SE = _Sem(nc.alloc_semaphore("SE"))

    O = _Ops()
    O.self_sem = {"v": VE, "s": SE, "g": GE}

    for s_w, w in ((s_wq, wqT), (s_wk, wkT), (s_wv, wvT), (s_wg, wgT), (s_wo, woT)):
        O.dma("g", GQ, lambda g, s_w=s_w, w=w: g.dma_start(
            out=s_w[:].rearrange("p (i e) -> p i e", i=8),
            in_=w.ap().rearrange("(i p) e -> p i e", p=TILE)))
    O.dma("g", GQ, lambda g: g.dma_start(out=s_gamT[:], in_=gamT.ap()))
    O.dma("g", GQ, lambda g: g.dma_start(out=s_betT[:], in_=betT.ap()))
    O.dma("g", GQ, lambda g: g.dma_start(out=s_mask[:], in_=maskt.ap()))
    O.dma("g", GQ, lambda g: g.dma_start(out=s_one[:], in_=ones_r.ap()))
    O.dma("g", GQ, lambda g: g.dma_start(out=s_onec[:], in_=ones_c.ap()))
    w_loaded = O.close_epoch(GQ)

    O.add("v", lambda v: v.memzero(s_h[0][:]), inc=VE)
    O.add("v", lambda v: v.memzero(s_eps[:]), inc=VE)
    O.add("v", lambda v: v.tensor_scalar_add(s_eps[:], s_eps[:], EPS), inc=VE)

    def row_times_wo(lhs_col_fn, gate):
        for hf in range(2):
            for j in range(8):
                O.add("t", lambda t, hf=hf, j=j: t.matmul(
                    p_pp[hf][0:1, :],
                    lhs_col_fn(j),
                    s_wo[:, 1024 * j + 512 * hf:1024 * j + 512 * (hf + 1)],
                    start=(j == 0), stop=(j == 7)),
                    wait=(gate if (hf == 0 and j == 0) else ()),
                    inc=(TE if j == 7 else None))
        te = TE.n
        for hf in range(2):
            O.add("v", lambda v, hf=hf: v.tensor_copy(
                s_row[0:1, 512 * hf:512 * (hf + 1)], p_pp[hf][0:1, :]),
                wait=([(TE, te)] if hf == 0 else ()), inc=VE)
        return te, VE.n

    def broadcast_row(dst, gate_ve):
        for hf in range(2):
            O.add("t", lambda t, hf=hf: t.matmul(
                p_pp[hf][:],
                s_one[0:1, 0:TILE],
                s_row[0:1, 512 * hf:512 * (hf + 1)],
                start=True, stop=True),
                wait=([(VE, gate_ve)] if hf == 0 else ()),
                inc=(TE if hf == 1 else None))
        te = TE.n
        for hf in range(2):
            O.add("v", lambda v, dst=dst, hf=hf: v.tensor_copy(
                dst[:, 512 * hf:512 * (hf + 1)], p_pp[hf][:]),
                wait=([(TE, te)] if hf == 0 else ()), inc=VE)
        return VE.n

    bw_te, bw_ve = row_times_wo(lambda j: s_betT[:, j:j + 1], [(GQ, w_loaded)])
    bwB_ve = broadcast_row(s_bwB, bw_ve)
    for j in range(8):
        O.add("v", lambda v, j=j: v.tensor_scalar_mul(
            s_wo[:, 1024 * j:1024 * (j + 1)],
            s_wo[:, 1024 * j:1024 * (j + 1)],
            s_gamT[:, j:j + 1]),
            wait=([(TE, bw_te)] if j == 0 else ()), inc=VE)
    fold_ve = VE.n
    ws_te, ws_ve = row_times_wo(lambda j: s_onec[:], [(VE, fold_ve)])
    wsB_ve = broadcast_row(s_wsB, ws_ve)
    init_ve = VE.n

    gc = [0]
    pv = [init_ve]
    pyf = {0: 0, 1: 0}
    pstu = {0: 0, 1: 0}
    vsu = [0]
    upd_ve = {}
    xt_ep = {}
    G = dict(st_last=0,
             y_last=0,
             qk_te=0,
             e_wo_te=0,
             fr_se={},
             ln1_se={},
             xr_store={},
             fr_ve={},
             e_wo_te2=0,
             scan0_ve=0, e0_done=0,
             yt_slot={},
             yst_slot={})

    def dump(name, src_ap, wait):
        if not DEBUG:
            return
        t = dbg[name]
        r, cn = src_ap.shape[0], src_ap.shape[1]
        O.dma("g", DQ, lambda g, t=t, s=src_ap, r=r, cn=cn: g.dma_start(
            out=t.ap()[0:r, 0:cn], in_=s), wait=wait)
        O.close_epoch(DQ)
        O.add("v", lambda v: v.engine_nop(), wait=[(DQ, DQ.n)])

    def ln_stats(src_ap, c0, wait_src):
        st = lambda i: s_stats[:, c0 + i:c0 + i + 1]
        O.add("s", lambda s: s.activation(
            s_nL[:, 1024:2048], src_ap, AF.Copy, accum_out=st(0)),
            wait=list(wait_src) + [(VE, init_ve)], inc=SE)
        O.add("s", lambda s: s.activation(
            s_nL[:, 1024:2048], src_ap, AF.Square, accum_out=st(1)), inc=SE)
        O.add("s", lambda s: s.activation(
            st(2), st(0), AF.Copy, scale=1.0 / D), inc=SE)
        O.add("s", lambda s: s.activation(
            st(3), st(2), AF.Square), inc=SE)
        O.add("s", lambda s: s.activation(
            st(4), st(3), AF.Identity, scale=-1.0, bias=s_eps[:]), inc=SE)
        O.add("s", lambda s: s.activation(
            st(5), st(1), AF.Identity, scale=1.0 / D, bias=st(4)), inc=SE)
        O.add("s", lambda s: s.activation(
            st(6), st(5), AF.Ln), inc=SE)
        O.add("s", lambda s: s.activation(
            st(7), st(6), AF.Exp, scale=-0.5), inc=SE)
        O.add("s", lambda s: s.activation(
            st(8), st(2), AF.Copy, scale=st(7)), inc=SE)
        O.add("s", lambda s: s.activation(
            st(9), st(8), AF.Copy, scale=-1.0), inc=SE)
        return SE.n

    def layernorm(src_ap, dst_ap, c0, wait_src):
        ln_stats(src_ap, c0, wait_src)
        st = lambda i: s_stats[:, c0 + i:c0 + i + 1]
        O.add("s", lambda s: s.activation(
            dst_ap, src_ap, AF.Identity, scale=st(7), bias=st(9)), inc=SE)
        return SE.n

    def bank_ap(b, n):
        return (p_pp[b][:, 0:n] if b < 2
                else p_py[:, 512 * (b - 2):512 * (b - 2) + n])

    def prep_front(tb, nt, wu, gp2):
        ntok = nt * TILE
        nch = ntok // C

        x_eps = {}

        def emit_loads():
            lw = [(SE, max([G["ln1_se"].get(tt, 0) for tt in range(nt)]))] + \
                 ([(DQ, DQ.n)] if DEBUG else [])
            for tt in range(nt):
                O.dma("g", SX, lambda g, tb=tb, tt=tt: g.dma_start(
                    out=s_x[:, 1024 * tt:1024 * (tt + 1)],
                    in_=xs.ap()[TILE * (tb + tt):TILE * (tb + tt + 1), :]),
                    wait=(lw if tt == 0 else ()))
            ep = O.close_epoch(SX)
            for tt in range(nt):
                x_eps[tt] = ep

        def u_tile(tt):
            def run():
                ep = x_eps[tt]
                d = layernorm(s_x[:, 1024 * tt:1024 * (tt + 1)],
                              s_xln[:, 1024 * tt:1024 * (tt + 1)],
                              64 + 16 * tt, [(SX, ep), (ST, xt_ep.get(0, 0))])
                G["ln1_se"][tt] = d
            return run

        def u_xt():
            for tt in range(nt):
                O.dma("sy", ST, lambda sy, tt=tt: sy.dma_start(
                    out=s_xlnT[:].rearrange("p (j q c) -> p j q c",
                                            j=8, q=4, c=TILE)[:, :, tt, :],
                    in_=s_xln[:, 1024 * tt:1024 * (tt + 1)].rearrange(
                        "p (j c) -> p j c", j=8),
                    transpose=True),
                    wait=([(SE, SE.n), (TE, G["qk_te"])] if tt == 0 else ()))
            ep = O.close_epoch(ST)
            for tt in range(nt):
                xt_ep[tt] = ep

        def u_gblk(p):
            def run():
                bank = p % 2
                w0 = [(ST, xt_ep[nt - 1]), (GQ, w_loaded),
                      (SE, G["fr_se"].get(bank, 0)),
                      (VE, G["fr_ve"].get(bank, 0))]
                for j in range(8):
                    O.add("t", lambda t, p=p, j=j, bank=bank: t.matmul(
                        p_fr[bank][:, 0:ntok],
                        s_wg[:, 1024 * j + TILE * p:1024 * j + TILE * (p + 1)],
                        s_xlnT[:, 512 * j:512 * j + ntok],
                        start=(j == 0), stop=(j == 7)),
                        wait=(w0 if j == 0 else ()), inc=(TE if j == 7 else None))
                O.add("s", lambda s, bank=bank: s.activation(
                    s_nL[:, 1024:1024 + ntok], p_fr[bank][:, 0:ntok],
                    AF.Exp, scale=-1.0),
                    wait=[(TE, TE.n)], inc=SE)
                O.add("s", lambda s, p=p: s.activation(
                    s_S[:, 512 * p:512 * p + ntok], s_nL[:, 1024:1024 + ntok],
                    AF.Ln, bias=1.0), inc=SE)
                G["fr_se"][bank] = SE.n
            return run

        def u_scan0():
            sd = SE.n
            first = True
            for cc in range(min(4, nch)):
                for p in range(8):
                    O.add("v", lambda v, cc=cc, p=p: v.tensor_tensor_scan(
                        s_nL[:, 512 * cc + C * p:512 * cc + C * (p + 1)],
                        s_S[:, 512 * p + C * cc:512 * p + C * (cc + 1)],
                        s_S[:, 512 * p + C * cc:512 * p + C * (cc + 1)],
                        0.0, ALU.add, ALU.bypass),
                        wait=([(SE, sd)] if first else ()))
                    first = False
            G["scan0_ve"] = VE.n
            O.add("s", lambda s: s.activation(
                s_E[:, 0:2048].rearrange("p (c q o) -> p c q o",
                                         q=8, o=C)[:, :, 0:6, :],
                s_nL[:, 0:2048].rearrange("p (c q o) -> p c q o",
                                          q=8, o=C)[:, :, 0:6, :],
                AF.Exp, scale=-1.0),
                wait=[(VE, G["scan0_ve"]), (TE, G["e_wo_te2"])], inc=SE)
            O.add("s", lambda s: s.activation(
                s_Er[:, 0:2048], s_nL[:, 0:2048], AF.Exp, scale=1.0), inc=SE)
            G["e0_done"] = SE.n

        class _U(list):
            pass
        units = _U([u_tile(tt) for tt in range(nt)] + [u_xt] +
                   [u_gblk(p) for p in range(8)] + [u_scan0])
        units.loads = emit_loads

        def finish(dbg_this=False):
            s_done = SE.n

            evac = {}

            def bank6(b, n):
                return p_fr[b - 4][:, 0:n] if b >= 4 else bank_ap(b, n)

            def first_use(bk):
                if bk < 2:
                    return [(VE, pv[0])]
                if bk < 4:
                    return [(VE, pyf[bk - 2])]
                return [(SE, G["fr_se"].get(bk - 4, 0)),
                        (VE, G["fr_ve"].get(bk - 4, 0))]

            for tt in range(nt):
                for hf in range(2):
                    bk = 2 + (2 * tt + hf) % 4
                    w0 = ([(VE, evac[bk])] if bk in evac else first_use(bk))
                    for j in range(8):
                        O.add("t", lambda t, tt=tt, j=j, hf=hf, bk=bk: t.matmul(
                            bank6(bk, 512),
                            s_xlnT[:, 512 * j + TILE * tt:512 * j + TILE * (tt + 1)],
                            s_wv[:, 1024 * j + 512 * hf:1024 * j + 512 * (hf + 1)],
                            start=(j == 0), stop=(j == 7)),
                            wait=(w0 if j == 0 else ()),
                            inc=(TE if j == 7 else None))
                    O.add("v", lambda v, tt=tt, hf=hf, bk=bk: v.tensor_copy(
                        s_V[:, 1024 * tt + 512 * hf:1024 * tt + 512 * (hf + 1)],
                        bank6(bk, 512)),
                        wait=[(TE, TE.n), (GE, GE.n)], inc=VE)
                    evac[bk] = VE.n
                    if bk >= 4:
                        G["fr_ve"][bk - 4] = VE.n
            v_done = VE.n

            O.add("s", lambda s, nch=nch: s.activation(
                s_E[:, 0:512 * min(4, nch)].rearrange(
                    "p (c q o) -> p c q o", q=8, o=C)[:, :, 6:8, :],
                s_nL[:, 0:512 * min(4, nch)].rearrange(
                    "p (c q o) -> p c q o", q=8, o=C)[:, :, 6:8, :],
                AF.Exp, scale=-1.0),
                wait=[(VE, G["scan0_ve"]), (TE, G["e_wo_te"]),
                      (SE, G["e0_done"])], inc=SE)
            e_done = SE.n
            if nch > 4:
                firstv = True
                for cc in range(4, nch):
                    for p in range(8):
                        O.add("v", lambda e, cc=cc, p=p: e.tensor_tensor_scan(
                            s_nL[:, 512 * (cc % 4) + C * p:512 * (cc % 4) + C * (p + 1)],
                            s_S[:, 512 * p + C * cc:512 * p + C * (cc + 1)],
                            s_S[:, 512 * p + C * cc:512 * p + C * (cc + 1)],
                            0.0, ALU.add, ALU.bypass),
                            wait=([(SE, e_done)] if firstv else ()))
                        firstv = False
                scan_done = VE.n
                O.add("s", lambda s: s.activation(
                    s_E[:, 2048:4096], s_nL[:, 0:2048], AF.Exp, scale=-1.0),
                    wait=[(VE, scan_done)], inc=SE)
                O.add("s", lambda s: s.activation(
                    s_Er[:, 2048:4096], s_nL[:, 0:2048], AF.Exp, scale=1.0),
                    inc=SE)
                e_done = SE.n
            O.add("v", lambda v, nch=nch, gp2=gp2: v.tensor_copy(
                s_ec[:, 64 * gp2:64 * (gp2 + 1)].rearrange(
                    "p (c q) -> p c q", q=8)[:, 0:nch, :],
                s_E[:].rearrange("p (c q o) -> p c q o",
                                 q=8, o=C)[:, 0:nch, :, C - 1]),
                wait=[(SE, e_done)], inc=VE)
            ec_done = VE.n

            if dbg_this:
                dump("d_xln", s_xln[:], [(VE, ec_done)])
                dump("d_S", s_S[:], [(SE, s_done)])
                dump("d_negL", s_nL[:], [(VE, ec_done)])

            rp = [2]

            def proj_qk(s_w, s_dst, s_scale, skip, winmajor):
                if skip:
                    return
                for p in range(8):
                    bank = rp[0] % 6
                    rp[0] += 1
                    w0 = ([(VE, evac[bank])] if bank in evac
                          else first_use(bank))
                    for j in range(8):
                        O.add("t", lambda t, p=p, j=j, bank=bank, s_w=s_w,
                              ntok=ntok: t.matmul(
                            bank6(bank, ntok),
                            s_w[:, 1024 * j + TILE * p:1024 * j + TILE * (p + 1)],
                            s_xlnT[:, 512 * j:512 * j + ntok],
                            start=(j == 0), stop=(j == 7)),
                            wait=(w0 if j == 0 else ()),
                            inc=(TE if j == 7 else None))
                    if winmajor:
                        out_ap = (lambda p=p: s_dst[:].rearrange(
                            "p (w q e o) -> p w q e o",
                            w=4, q=8, e=2, o=C)[:, 0:nch // 2, p, :, :])
                    else:
                        out_ap = (lambda p=p: s_dst[:, 512 * p:512 * p + ntok]
                                  .rearrange("p (c o) -> p c o", o=C))
                    O.add("v", lambda v, p=p, bank=bank, s_scale=s_scale,
                          out_ap=out_ap, ntok=ntok, nch=nch:
                          v.tensor_tensor(
                              out_ap(),
                              bank6(bank, ntok).rearrange("p (c o) -> p c o", o=C),
                              s_scale[:].rearrange("p (c q o) -> p c q o",
                                                   q=8, o=C)[:, 0:nch, p, :],
                              ALU.mult),
                          wait=[(TE, TE.n), (SE, e_done)], inc=VE)
                    evac[bank] = VE.n
                    if bank >= 4:
                        G["fr_ve"][bank - 4] = VE.n
            proj_qk(s_wq, s_qt, s_E, wu, False)
            proj_qk(s_wk, s_kt, s_Er, False, True)
            qk_done = VE.n
            G["qk_te"] = TE.n

            if dbg_this:
                dump("d_qt", s_qt[:], [(VE, qk_done)])
                dump("d_kt", s_kt[:], [(VE, qk_done)])
                dump("d_V", s_V[:], [(VE, v_done)])

            return dict(v_done=v_done, ec_done=ec_done, qk_done=qk_done,
                        nch=nch, gp2=gp2)

        return units, finish

    def emit_yln(w):
        return ln_stats(s_y[:, 1024 * (w % 2):1024 * (w % 2 + 1)],
                        16 * w, [(VE, VE.n)])

    def emit_loop(tb, nt, wu, FR, splice, dbg_this=False):
        nch = FR["nch"]
        gp2 = FR["gp2"]
        half_cols = 512 * nch // 2
        O.dma("sy", KT, lambda sy, half_cols=half_cols: sy.dma_start(
            out=s_ktm[:, 0:half_cols].rearrange("p (m c) -> p m c", c=TILE),
            in_=s_kt[:, 0:half_cols].rearrange("p (m c) -> p m c", c=TILE),
            transpose=True),
            wait=[(VE, FR["qk_done"]), (TE, G["st_last"])])
        ktm_ep0 = O.close_epoch(KT)
        O.dma("sy", KT, lambda sy, half_cols=half_cols: sy.dma_start(
            out=s_ktm[:, half_cols:2 * half_cols].rearrange(
                "p (m c) -> p m c", c=TILE),
            in_=s_kt[:, half_cols:2 * half_cols].rearrange(
                "p (m c) -> p m c", c=TILE),
            transpose=True))
        ktm_ep1 = O.close_epoch(KT)
        yt_eps = {}
        yst = {}
        xr_ep = {}

        def xr_load(w):
            if wu or w >= nt:
                return
            O.dma("g", GX, lambda g, w=w, tb=tb: g.dma_start(
                out=s_xr[:, 1024 * (w % 2):1024 * (w % 2 + 1)],
                in_=xs.ap()[TILE * (tb + w):TILE * (tb + w + 1), :]),
                wait=[(XQ, G["xr_store"].get(w % 2, 0))])
            xr_ep[w] = O.close_epoch(GX)

        xr_load(0)
        xr_load(1)
        if splice:
            splice.loads()
            for _ in range(min(3, len(splice) - 9)):
                splice.pop(0)()
        vsg = {}

        def emit_vs(w2):
            O.add("g", lambda g_, w2=w2: g_.tensor_copy(
                s_VS[0:64, :],
                s_V[64:128, 1024 * w2:1024 * (w2 + 1)]),
                wait=[(VE, FR["v_done"]), (TE, vsu[0])], inc=GE)
            O.add("g", lambda g_, w2=w2: g_.tensor_copy(
                s_VS[64:128, :],
                s_V[0:64, 1024 * w2:1024 * (w2 + 1)]),
                inc=GE)
            vsg[w2] = GE.n

        if not wu:
            emit_vs(0)

        def emit_be_late(w):
            w0 = [(YT, yt_eps[w]),
                  (VE, FR["qk_done"] if w == 0 else pv[0])]
            for j in range(8):
                for hf in range(2):
                    O.add("t", lambda t, w=w, j=j, hf=hf: t.matmul(
                        p_pp[hf][:],
                        s_E[:, 512 * j + TILE * w:512 * j + TILE * (w + 1)],
                        s_wo[:, 1024 * j + 512 * hf:1024 * j + 512 * (hf + 1)],
                        start=(j == 0), stop=(j == 7)),
                        wait=(w0 if (j == 0 and hf == 0) else ()),
                        inc=(TE if (j == 7 and hf == 1) else None))
            G["e_wo_te2"] = G["e_wo_te"]
            G["e_wo_te"] = TE.n
            r_ap = s_stats[:, 16 * w + 7:16 * w + 8]
            nrm_ap = s_stats[:, 16 * w + 9:16 * w + 10]
            xsl = 1024 * (w % 2)
            for hf in range(2):
                sl = slice(512 * hf, 512 * (hf + 1))
                O.add("v", lambda v, hf=hf, sl=sl, r_ap=r_ap:
                      v.scalar_tensor_tensor(
                          p_pp[hf][:], p_pp[hf][:], r_ap, s_bwB[:, sl],
                          ALU.mult, ALU.add),
                      wait=([(TE, TE.n), (SE, yst[w])] if hf == 0 else ()),
                      inc=VE)
                O.add("v", lambda v, hf=hf, sl=sl, nrm_ap=nrm_ap:
                      v.scalar_tensor_tensor(
                          p_pp[hf][:], s_wsB[:, sl], nrm_ap,
                          p_pp[hf][:], ALU.mult, ALU.add),
                      inc=VE)
                O.add("v", lambda v, hf=hf, sl=sl, xsl=xsl: v.tensor_add(
                    s_xr[:, xsl + sl.start:xsl + sl.stop],
                    s_xr[:, xsl + sl.start:xsl + sl.stop],
                    p_pp[hf][:]),
                    wait=([(GX, xr_ep[w])] if hf == 0 else ()), inc=VE)
            pv[0] = VE.n
            O.dma("g", XQ, lambda g, w=w, tb=tb, xsl=xsl: g.dma_start(
                out=x2d.ap()[TILE * (tb + w - 2):TILE * (tb + w - 1), :],
                in_=s_xr[:, xsl:xsl + 1024]),
                wait=[(VE, VE.n)])
            G["xr_store"][w % 2] = O.close_epoch(XQ)
            xr_load(w + 2)
            if dbg_this and w == 0:
                dump("d_x2", s_xr[:, 0:1024], [(VE, VE.n)])

        npair = nch // 2
        for cc in range(nch):
            par = cc % 2
            w = cc // 2
            hs = gc[0] % 3
            hn = (gc[0] + 1) % 3
            ysl = 1024 * (w % 2)
            vs_done = vsg.get(w, 0)

            if not wu:
                for h in range(NH):
                    hp, hq = h % 2, h // 2
                    O.add("t", lambda t, w=w, h=h, hp=hp, hq=hq, par=par, cc=cc:
                          t.matmul(
                        p_py[64 * hp:64 * (hp + 1),
                             512 * par + 64 * hq:512 * par + 64 * (hq + 1)],
                        s_kt[64 * hp:64 * (hp + 1),
                             1024 * w + TILE * hq + C * par:
                             1024 * w + TILE * hq + C * (par + 1)],
                        s_qt[64 * hp:64 * (hp + 1),
                             512 * hq + C * cc:512 * hq + C * (cc + 1)],
                        start=True, stop=True),
                        wait=([(VE, max(FR["qk_done"], pyf[par]))]
                              if h == 0 else ()),
                        inc=(TE if h == NH - 1 else None))
                p_mm_done = TE.n
                O.add("v", lambda v, par=par: v.tensor_tensor(
                    s_P[:, 512 * par:512 * (par + 1)],
                    p_py[:, 512 * par:512 * (par + 1)],
                    s_mask[:, 512 * par:512 * (par + 1)], ALU.mult),
                    wait=[(TE, p_mm_done)], inc=VE)
                pmask_done = VE.n

            for h in range(NH):
                hp, hq = h % 2, h // 2
                O.add("t", lambda t, w=w, h=h, hp=hp, hq=hq, par=par:
                      t.matmul(
                          p_st[64 * par:64 * (par + 1), 64 * h:64 * (h + 1)],
                          s_ktm[64 * par:64 * (par + 1),
                                1024 * w + TILE * hq + 64 * hp:
                                1024 * w + TILE * hq + 64 * (hp + 1)],
                          s_V[64 * par:64 * (par + 1),
                              1024 * w + 64 * h:1024 * w + 64 * (h + 1)],
                          start=True, stop=True),
                      wait=([(KT, ktm_ep0 if w < (nch + 3) // 4 else ktm_ep1),
                             (VE, pstu[par])] if h == 0 else ()),
                      inc=(TE if h == NH - 1 else None))
            st_mm = TE.n
            G["st_last"] = st_mm

            def emit_update(cc=cc, par=par, hs=hs, hn=hn, st_mm=st_mm):
                for hp in range(2):
                    O.add("v", lambda v, par=par, hp=hp, hs=hs: v.tensor_tensor(
                        s_ht[64 * hp:64 * (hp + 1), :].rearrange(
                            "p (q o) -> p q o", o=C),
                        p_st[64 * par:64 * (par + 1), :].rearrange(
                            "p (q two o) -> p q two o", two=2, o=C)[:, :, hp, :],
                        s_h[hs][64 * hp:64 * (hp + 1), :].rearrange(
                            "p (q o) -> p q o", o=C),
                        ALU.add),
                        wait=([(TE, st_mm), (VE, FR["ec_done"])]
                              if hp == 0 else ()), inc=VE)
                pstu[par] = VE.n
                O.add("v", lambda v, cc=cc, hn=hn: v.tensor_tensor(
                    s_h[hn][:].rearrange("p (q o) -> p q o", o=C),
                    s_ht[:].rearrange("p (q o) -> p q o", o=C),
                    s_ec[:, 64 * gp2 + 8 * cc:64 * gp2 + 8 * cc + 8]
                    .unsqueeze(2).broadcast_to([TILE, 8, C]),
                    ALU.mult), inc=VE)
                upd_ve[hn] = VE.n
            if wu or par == 0:
                emit_update()

            if (not wu) and par == 1:
                ug = max(upd_ve.get((gc[0] - 1) % 3, 0),
                         upd_ve.get(gc[0] % 3, 0))
                for c2 in (cc - 1, cc):
                    par2 = c2 % 2
                    hs2 = (gc[0] - (cc - c2)) % 3
                    for h in range(NH):
                        hp, hq = h % 2, h // 2
                        rhs_v = (
                            (lambda t, c2=c2, hp=hp, h=h:
                             s_V[64 * hp:64 * (hp + 1),
                                 1024 * (c2 // 2) + 64 * h:1024 * (c2 // 2) + 64 * (h + 1)])
                            if hp == par2 else
                            (lambda t, hp=hp, h=h:
                             s_VS[64 * hp:64 * (hp + 1), 64 * h:64 * (h + 1)])
                        )
                        O.add("t", lambda t, c2=c2, par2=par2, h=h, hp=hp, hq=hq,
                              rhs_v=rhs_v: t.matmul(
                                  p_py[64 * hp:64 * (hp + 1),
                                       512 * par2 + 64 * hq:512 * par2 + 64 * (hq + 1)],
                                  s_P[64 * hp:64 * (hp + 1),
                                      512 * par2 + 64 * hq:512 * par2 + 64 * (hq + 1)],
                                  rhs_v(t),
                                  start=True, stop=False),
                              wait=(([(VE, max(pmask_done, ug)), (GE, vs_done)])
                                    if h == 0 else ()))
                        O.add("t", lambda t, c2=c2, par2=par2, h=h, hp=hp, hq=hq,
                              hs2=hs2: t.matmul(
                                  p_py[64 * hp:64 * (hp + 1),
                                       512 * par2 + 64 * hq:512 * par2 + 64 * (hq + 1)],
                                  s_qt[64 * hp:64 * (hp + 1),
                                       512 * hq + C * c2:512 * hq + C * (c2 + 1)],
                                  s_h[hs2][64 * hp:64 * (hp + 1),
                                           64 * hq:64 * (hq + 1)],
                                  start=False, stop=True),
                              inc=(TE if (h == NH - 1 and c2 == cc) else None))
                G["y_last"] = TE.n
                vsu[0] = TE.n
                if w + 1 < npair:
                    emit_vs(w + 1)
                for par2 in (0, 1):
                    for hp in (0, 1):
                        O.add("v", lambda v, ysl=ysl, par2=par2, hp=hp:
                              v.tensor_copy(
                            s_y[64 * par2:64 * (par2 + 1),
                                ysl:ysl + 1024].rearrange(
                                "p (q two o) -> p q two o", two=2, o=C)[:, :, hp, :],
                            p_py[64 * hp:64 * (hp + 1),
                                 512 * par2:512 * (par2 + 1)].rearrange(
                                "p (q o) -> p q o", o=C)),
                            wait=(([(TE, TE.n),
                                    (YT, G["yt_slot"].get(w % 2, 0)),
                                    (SE, G["yst_slot"].get(w % 2, 0))])
                                  if (par2 == 0 and hp == 0) else ()),
                            inc=VE)
                    pyf[par2] = VE.n
                O.dma("sy", YT, lambda sy, w=w, ysl=ysl: sy.dma_start(
                    out=s_E[:].rearrange("p (j q c) -> p j q c",
                                         j=8, q=4, c=TILE)[:, :, w, :],
                    in_=s_y[:, ysl:ysl + 1024].rearrange(
                        "p (j c) -> p j c", j=8),
                    transpose=True),
                    wait=[(VE, VE.n)])
                yt_eps[w] = O.close_epoch(YT)
                G["yt_slot"][w % 2] = yt_eps[w]
                emit_update()
                if splice:
                    sched = {4: [2, 4, 4, 0], 2: [3, 7]}[npair]
                    for _ in range(min(sched[w], len(splice))):
                        splice.pop(0)()
                yst[w] = emit_yln(w)
                G["yst_slot"][w % 2] = yst[w]
                if w > 0:
                    emit_be_late(w - 1)
                if splice and w == npair - 1:
                    while splice:
                        splice.pop(0)()
            elif splice and wu and par == 1:
                sched = {2: [3, 7]}[npair]
                for _ in range(min(sched[w], len(splice))):
                    splice.pop(0)()
                if w == npair - 1:
                    while splice:
                        splice.pop(0)()
            gc[0] += 1

        if not wu:
            emit_be_late(nt - 1)
        if dbg_this:
            dump("d_y", s_y[:], [(VE, VE.n)])
            dump("d_h", s_h[gc[0] % 3][:], [(VE, VE.n)])

    ngrp = 0 if TRUNC in (1, 2) else (1 if TRUNC == 3 else 4)
    if TRUNC != 1:
        units, fin = prep_front(0, 2, True, 0)
        units.loads()
        for u in units:
            u()
        FR = fin()
        if ngrp:
            units, fin = prep_front(2, 4, False, 1)
        else:
            units = []
        emit_loop(0, 2, True, FR, units)
    for gi in range(ngrp):
        FR = fin(dbg_this=(DEBUG and gi == 0))
        if gi < ngrp - 1:
            units, fin = prep_front(2 + 4 * (gi + 1), 4, False, gi % 2)
        else:
            units = []
        emit_loop(2 + 4 * gi, 4, False, FR, units,
                  dbg_this=(DEBUG and gi == 0))
    a_sx, a_st, a_kt, a_yt = SX.n, ST.n, KT.n, YT.n
    a_gq, a_xq, a_dq, a_gx, a_ge = GQ.n, XQ.n, DQ.n, GX.n, GE.n
    a_te, a_ve, a_se = TE.n, VE.n, SE.n

    es.close()

    es2 = ExitStack()
    E2 = es2.enter_context
    s_x2T = E2(nc.sbuf_tensor("s_x2T", [TILE, 16 * D], BF16))
    s_hff = E2(nc.sbuf_tensor("s_hff", [TILE, NHC * 2048], BF16))
    s_wd = E2(nc.sbuf_tensor("s_wd", [TILE, NHC * 1024], BF16))
    s_wgu = [E2(nc.sbuf_tensor(f"s_wgu{i}", [TILE, 2048], BF16)) for i in range(2)]
    s_x2 = E2(nc.sbuf_tensor("s_x2", [TILE, 2048], F32))
    s_st2 = E2(nc.sbuf_tensor("s_st2", [TILE, 16], F32))
    s_sq2 = E2(nc.sbuf_tensor("s_sq2", [TILE, 1024], F32))
    s_a2 = E2(nc.sbuf_tensor("s_a2", [TILE, 1024], BF16))
    s_act2 = E2(nc.sbuf_tensor("s_act2", [TILE, 512], F32))
    p_g = [E2(nc.psum_tensor(f"p_g{i}", [TILE, 512], F32)) for i in range(2)]
    p_u = [E2(nc.psum_tensor(f"p_u{i}", [TILE, 512], F32)) for i in range(2)]
    p_d = [E2(nc.psum_tensor(f"p_d{i}", [TILE, 1024], F32)) for i in range(2)]

    barrier = [(TE, a_te), (VE, a_ve), (SE, a_se), (SX, a_sx), (ST, a_st),
               (KT, a_kt), (YT, a_yt), (GQ, a_gq), (XQ, a_xq), (DQ, a_dq),
               (GX, a_gx), (GE, a_ge)]
    first = {k: True for k in ("t", "v", "s", "sy", "g")}

    def addD(eng, fn, wait=(), inc=None, inc_by=1):
        w = list(wait)
        if first[eng]:
            w = barrier + w
            first[eng] = False
        O.add(eng, fn, wait=w, inc=inc, inc_by=inc_by)

    def dmaD(eng, sem, fn, wait=()):
        w = list(wait)
        if first[eng]:
            w = barrier + w
            first[eng] = False
        if sem.pending is not None:
            w.append((sem, sem.pending))
            sem.pending = None
        O.add(eng, fn, wait=w, inc=sem, inc_by=16)

    a2_done = {}
    for tt in range(16 if TRUNC == 0 else 0):
        buf = tt % 2
        dmaD("g", GX, lambda g, tt=tt, buf=buf: g.dma_start(
            out=s_x2[:, 1024 * buf:1024 * (buf + 1)],
            in_=x2d.ap()[TILE * tt:TILE * (tt + 1), :]),
            wait=([(VE, a2_done[tt - 2])] if tt >= 2 else []))
        ld = O.close_epoch(GX)
        xt = s_x2[:, 1024 * buf:1024 * (buf + 1)]
        addD("v", lambda v, xt=xt: v.reduce_sum(
            s_st2[:, 0:1], xt, axis=AX.X), wait=[(GX, ld)], inc=VE)
        addD("s", lambda s, xt=xt: s.activation(
            s_sq2[:], xt, AF.Square, accum_out=s_st2[:, 1:2]),
            wait=[(GX, ld), (VE, VE.n)], inc=SE)
        addD("v", lambda v: v.tensor_scalar(
            s_st2[:, 2:3], s_st2[:, 0:1], 1.0 / D, None, ALU.mult), inc=VE)
        addD("v", lambda v: v.tensor_mul(
            s_st2[:, 3:4], s_st2[:, 2:3], s_st2[:, 2:3]), inc=VE)
        addD("v", lambda v: v.tensor_scalar(
            s_st2[:, 4:5], s_st2[:, 1:2], 1.0 / D, s_st2[:, 3:4],
            ALU.mult, ALU.subtract), wait=[(SE, SE.n)], inc=VE)
        addD("v", lambda v: v.tensor_scalar_add(
            s_st2[:, 4:5], s_st2[:, 4:5], EPS), inc=VE)
        addD("s", lambda s: s.activation(
            s_st2[:, 5:6], s_st2[:, 4:5], AF.Ln), wait=[(VE, VE.n)], inc=SE)
        addD("s", lambda s: s.activation(
            s_st2[:, 6:7], s_st2[:, 5:6], AF.Exp, scale=-0.5), inc=SE)
        addD("v", lambda v, xt=xt: v.tensor_scalar(
            s_a2[:], xt, s_st2[:, 2:3], s_st2[:, 6:7], ALU.subtract, ALU.mult),
            wait=[(SE, SE.n), (ST, ST.n)], inc=VE)
        a2_done[tt] = VE.n
        dmaD("sy", ST, lambda sy, tt=tt: sy.dma_start(
            out=s_x2T[:].rearrange("p (j q c) -> p j q c",
                                   j=8, q=16, c=TILE)[:, :, tt, :],
            in_=s_a2[:].rearrange("p (j c) -> p j c", j=8), transpose=True),
            wait=[(VE, a2_done[tt])])
        O.close_epoch(ST)
    ln2_tp = ST.n

    if TRUNC == 0:
        dmaD("g", GQ, lambda g: g.dma_start(
            out=s_wd[:].rearrange("p (hc d) -> p hc d", hc=NHC),
            in_=wdi.ap().rearrange("(hc p) d -> p hc d", p=TILE)))
    wd_done = O.close_epoch(GQ)

    wgu_users = {}
    silu_of = {}
    hff_of = {}
    for hc in range(NHC if TRUNC == 0 else 0):
        buf = hc % 2
        dmaD("g", GQ, lambda g, hc=hc, buf=buf: g.dma_start(
            out=s_wgu[buf][:, 0:1024].rearrange("p (i h) -> p i h", i=8),
            in_=wgi.ap()[:, TILE * hc:TILE * (hc + 1)].rearrange(
                "(i p) h -> p i h", p=TILE)),
            wait=[(TE, wgu_users.get(buf, 0))])
        dmaD("g", GQ, lambda g, hc=hc, buf=buf: g.dma_start(
            out=s_wgu[buf][:, 1024:2048].rearrange("p (i h) -> p i h", i=8),
            in_=wui.ap()[:, TILE * hc:TILE * (hc + 1)].rearrange(
                "(i p) h -> p i h", p=TILE)))
        gu_ld = O.close_epoch(GQ)
        for t4 in range(4):
            pb = t4 % 2
            w0 = [(GQ, gu_ld), (ST, ln2_tp), (SE, silu_of.get(pb, 0))]
            for j in range(8):
                addD("t", lambda t, buf=buf, t4=t4, j=j, pb=pb: t.matmul(
                    p_g[pb][:],
                    s_wgu[buf][:, TILE * j:TILE * (j + 1)],
                    s_x2T[:, 2048 * j + 512 * t4:2048 * j + 512 * (t4 + 1)],
                    start=(j == 0), stop=(j == 7)),
                    wait=(w0 if j == 0 else ()), inc=(TE if j == 7 else None))
            addD("s", lambda s, pb=pb: s.activation(
                s_act2[:], p_g[pb][:], AF.Silu, scale=SCL),
                wait=[(TE, TE.n), (VE, hff_of.get(1 - pb, 0))], inc=SE)
            silu_of[pb] = SE.n
            silu_done = SE.n
            w1 = [(VE, hff_of.get(pb, 0))]
            for j in range(8):
                addD("t", lambda t, buf=buf, t4=t4, j=j, pb=pb: t.matmul(
                    p_u[pb][:],
                    s_wgu[buf][:, 1024 + TILE * j:1024 + TILE * (j + 1)],
                    s_x2T[:, 2048 * j + 512 * t4:2048 * j + 512 * (t4 + 1)],
                    start=(j == 0), stop=(j == 7)),
                    wait=(w1 if j == 0 else ()), inc=(TE if j == 7 else None))
            addD("v", lambda v, hc=hc, t4=t4, pb=pb: v.scalar_tensor_tensor(
                s_hff[:, 2048 * hc + 512 * t4:2048 * hc + 512 * (t4 + 1)],
                p_u[pb][:], SCL * SCL, s_act2[:], ALU.mult, ALU.mult),
                wait=[(TE, TE.n), (SE, silu_done)], inc=VE)
            hff_of[pb] = VE.n
        wgu_users[buf] = TE.n
    hff_done = VE.n

    resid_of = {}
    out_st = {}
    for ts in range(16 if TRUNC == 0 else 0):
        buf = ts % 2
        dmaD("g", GX, lambda g, ts=ts, buf=buf: g.dma_start(
            out=s_x2[:, 1024 * buf:1024 * (buf + 1)],
            in_=x2d.ap()[TILE * ts:TILE * (ts + 1), :]),
            wait=[(VE, resid_of.get(buf, 0)), (XQ, out_st.get(buf, 0))])
        x2_ld = O.close_epoch(GX)
        w0 = [(VE, hff_done if ts == 0 else resid_of.get(buf, 0)), (GQ, wd_done)]
        for hf in range(2):
            for hcc in range(NHC):
                addD("t", lambda t, ts=ts, buf=buf, hf=hf, hcc=hcc: t.matmul(
                    p_d[buf][:, 512 * hf:512 * (hf + 1)],
                    s_hff[:, 2048 * hcc + TILE * ts:2048 * hcc + TILE * (ts + 1)],
                    s_wd[:, 1024 * hcc + 512 * hf:1024 * hcc + 512 * (hf + 1)],
                    start=(hcc == 0), stop=(hcc == NHC - 1)),
                    wait=(w0 if (hf == 0 and hcc == 0) else ()),
                    inc=(TE if (hf == 1 and hcc == NHC - 1) else None))
        addD("v", lambda v, buf=buf: v.tensor_add(
            s_x2[:, 1024 * buf:1024 * (buf + 1)],
            s_x2[:, 1024 * buf:1024 * (buf + 1)], p_d[buf][:]),
            wait=[(TE, TE.n), (GX, x2_ld)], inc=VE)
        resid_of[buf] = VE.n
        dmaD("g", XQ, lambda g, ts=ts, buf=buf: g.dma_start(
            out=out.ap()[TILE * ts:TILE * (ts + 1), :],
            in_=s_x2[:, 1024 * buf:1024 * (buf + 1)]),
            wait=[(VE, VE.n)])
        out_st[buf] = O.close_epoch(XQ)
        if DEBUG and ts == 0:
            dump("d_hff", s_hff[:, 0:2048], [(VE, hff_done)])

    final_xq = XQ.n
    final_dq = DQ.n

    with nc.Block() as block:
        @block.tensor
        def _(t):
            for f in O.q["t"]:
                f(t)

        @block.vector
        def _(v):
            for f in O.q["v"]:
                f(v)

        @block.scalar
        def _(s):
            for f in O.q["s"]:
                f(s)

        @block.sync
        def _(sy):
            for f in O.q["sy"]:
                f(sy)

        @block.gpsimd
        def _(g):
            for f in O.q["g"]:
                f(g)
            g.wait_ge(XQ.h, final_xq)
            if final_dq:
                g.wait_ge(DQ.h, final_dq)

    es2.close()
    return nc


def _shard_inputs(inputs):
    x = np.asarray(inputs["L0_attn_x"], np.float32).reshape(16384, D)
    wqT = np.ascontiguousarray(np.asarray(inputs["L0_attn_Wq"], np.float32).T)
    wkT = np.ascontiguousarray(np.asarray(inputs["L0_attn_Wk"], np.float32).T)
    wvT = np.ascontiguousarray(np.asarray(inputs["L0_attn_Wv"], np.float32).T)
    wgT = np.ascontiguousarray(np.asarray(inputs["L0_attn_Wg"], np.float32).T)
    woT = np.ascontiguousarray(np.asarray(inputs["L0_attn_Wo"], np.float32).T)
    gamT = np.ascontiguousarray(
        np.asarray(inputs["L0_attn_gamma"], np.float32).reshape(8, TILE).T)
    betT = np.ascontiguousarray(
        np.asarray(inputs["L0_attn_beta"], np.float32).reshape(8, TILE).T)
    wgi = np.ascontiguousarray(np.asarray(inputs["L0_ffn_w_gate"], np.int32).T)
    wui = np.ascontiguousarray(np.asarray(inputs["L0_ffn_w_up"], np.int32).T)
    wdi = np.ascontiguousarray(np.asarray(inputs["L0_ffn_w_down"], np.int32).T)
    in_maps = []
    for c in range(NCORES):
        lo = c * T_OWN
        if c == 0:
            xsl = np.concatenate([np.zeros((WU, D), np.float32), x[0:T_OWN]], 0)
        else:
            xsl = np.ascontiguousarray(x[lo - WU:lo + T_OWN])
        in_maps.append(dict(xs=xsl, wqT=wqT, wkT=wkT, wvT=wvT, wgT=wgT, woT=woT,
                            gamT=gamT, betT=betT, wgi=wgi, wui=wui, wdi=wdi))
    return in_maps


def kernel(**inputs):
    in_maps = _shard_inputs(inputs)
    nc = build_program()
    res = run_bass_kernel_spmd(nc, in_maps, core_ids=list(range(NCORES)))
    outs = [np.asarray(res.results[c]["out"]) for c in range(NCORES)]
    full = np.concatenate(outs, axis=0).reshape(1, 16384, D)
    if DEBUG:
        kernel.last_results = res
    return full.astype(np.float32)


# version 22
# speedup vs baseline: 1.0561x; 1.0113x over previous
import numpy as np
from contextlib import ExitStack

import concourse.bass as bass
import concourse.mybir as mybir
from concourse.bass_utils import run_bass_kernel_spmd

F32 = mybir.dt.float32
BF16 = mybir.dt.bfloat16
I32 = mybir.dt.int32
AF = mybir.ActivationFunctionType
ALU = mybir.AluOpType
AX = mybir.AxisListType

NCORES = 8
D = 1024
NH = 16
C = 64
TILE = 128
T_OWN = 2048
WU = 256
T_ALL = T_OWN + WU
HID = 3072
NHC = HID // TILE
SCL = 1.0 / 127.0
EPS = 1e-5

DEBUG = False
TRUNC = 0


class _Sem:
    def __init__(self, h):
        self.h = h
        self.n = 0
        self.pending = None


class _Ops:

    def __init__(self):
        self.q = {k: [] for k in ("t", "v", "s", "sy", "g")}
        self.self_sem = {}

    def add(self, eng, fn, wait=(), inc=None, inc_by=1):
        wait = list(wait)
        ss = self.self_sem.get(eng)
        if ss is not None:
            wait.append((ss, ss.n))
            if inc is None:
                inc, inc_by = ss, 1
        if inc is not None:
            inc.n += inc_by
        w = tuple((s, v) for (s, v) in wait if v > 0)

        def run(e, fn=fn, w=w, inc=inc, inc_by=inc_by):
            for sem, val in w:
                e.wait_ge(sem.h, val)
            ins = fn(e)
            if inc is not None and ins is not None:
                ins.then_inc(inc.h, inc_by)
        self.q[eng].append(run)

    def dma(self, eng, sem, fn, wait=()):
        w = list(wait)
        if sem.pending is not None:
            w.append((sem, sem.pending))
            sem.pending = None
        self.add(eng, fn, wait=w, inc=sem, inc_by=16)

    def close_epoch(self, sem):
        sem.pending = sem.n
        return sem.n


def build_program():
    nc = bass.Bass("TRN2", target_bir_lowering=False)

    xs = nc.dram_tensor("xs", [T_ALL, D], F32, kind="ExternalInput")
    wqT = nc.dram_tensor("wqT", [D, D], F32, kind="ExternalInput")
    wkT = nc.dram_tensor("wkT", [D, D], F32, kind="ExternalInput")
    wvT = nc.dram_tensor("wvT", [D, D], F32, kind="ExternalInput")
    wgT = nc.dram_tensor("wgT", [D, D], F32, kind="ExternalInput")
    woT = nc.dram_tensor("woT", [D, D], F32, kind="ExternalInput")
    gamT = nc.dram_tensor("gamT", [TILE, 8], F32, kind="ExternalInput")
    betT = nc.dram_tensor("betT", [TILE, 8], F32, kind="ExternalInput")
    wgi = nc.dram_tensor("wgi", [D, HID], I32, kind="ExternalInput")
    wui = nc.dram_tensor("wui", [D, HID], I32, kind="ExternalInput")
    wdi = nc.dram_tensor("wdi", [HID, D], I32, kind="ExternalInput")
    out = nc.dram_tensor("out", [T_OWN, D], F32, kind="ExternalOutput")

    x2d = nc.dram_tensor("x2d", [T_OWN, D], F32)

    mask_np = np.tile(np.triu(np.ones((C, C), np.float32)), (2, NH))
    maskt = nc.inline_tensor(mask_np, "mask_rep")
    ones_r = nc.inline_tensor(np.ones((1, TILE), np.float32), "ones_r")
    ones_c = nc.inline_tensor(np.ones((TILE, 1), np.float32), "ones_c")

    dbg = {}
    if DEBUG:
        for name, shape in [
            ("d_xln", [TILE, 4096]), ("d_S", [TILE, 4096]), ("d_negL", [TILE, 2048]),
            ("d_qt", [TILE, 4096]), ("d_kt", [TILE, 4096]), ("d_V", [TILE, 4096]),
            ("d_h", [TILE, 512]), ("d_y", [TILE, 4096]),
            ("d_x2", [TILE, 1024]), ("d_hff", [TILE, 2048]),
        ]:
            dbg[name] = nc.dram_tensor(name, shape, F32, kind="ExternalOutput")

    es = ExitStack()
    E_ = es.enter_context
    s_wq = E_(nc.sbuf_tensor("s_wq", [TILE, 8 * D], BF16))
    s_wk = E_(nc.sbuf_tensor("s_wk", [TILE, 8 * D], BF16))
    s_wv = E_(nc.sbuf_tensor("s_wv", [TILE, 8 * D], BF16))
    s_wg = E_(nc.sbuf_tensor("s_wg", [TILE, 8 * D], BF16))
    s_wo = E_(nc.sbuf_tensor("s_wo", [TILE, 8 * D], BF16))
    s_gamT = E_(nc.sbuf_tensor("s_gamT", [TILE, 8], F32))
    s_betT = E_(nc.sbuf_tensor("s_betT", [TILE, 8], BF16))
    s_row = E_(nc.sbuf_tensor("s_row", [1, D], BF16))
    s_one = E_(nc.sbuf_tensor("s_one", [1, TILE], BF16))
    s_onec = E_(nc.sbuf_tensor("s_onec", [TILE, 1], BF16))
    s_bwB = E_(nc.sbuf_tensor("s_bwB", [TILE, D], BF16))
    s_wsB = E_(nc.sbuf_tensor("s_wsB", [TILE, D], BF16))
    s_eps = E_(nc.sbuf_tensor("s_eps", [TILE, 1], F32))
    s_mask = E_(nc.sbuf_tensor("s_mask", [TILE, 1024], BF16))
    s_x = E_(nc.sbuf_tensor("s_x", [TILE, 4096], F32))
    s_xln = E_(nc.sbuf_tensor("s_xln", [TILE, 4096], BF16))
    s_xlnT = E_(nc.sbuf_tensor("s_xlnT", [TILE, 4096], BF16))
    s_S = E_(nc.sbuf_tensor("s_S", [TILE, 4096], BF16))
    s_nL = E_(nc.sbuf_tensor("s_nL", [TILE, 2048], F32))
    s_E = E_(nc.sbuf_tensor("s_E", [TILE, 4096], BF16))
    s_Er = E_(nc.sbuf_tensor("s_Er", [TILE, 4096], BF16))
    s_ec = E_(nc.sbuf_tensor("s_ec", [TILE, 128], F32))
    s_qt = E_(nc.sbuf_tensor("s_qt", [TILE, 4096], BF16))
    s_kt = E_(nc.sbuf_tensor("s_kt", [TILE, 4096], BF16))
    s_V = E_(nc.sbuf_tensor("s_V", [TILE, 4096], BF16))
    s_ktm = E_(nc.sbuf_tensor("s_ktm", [TILE, 4096], BF16))
    s_P = E_(nc.sbuf_tensor("s_P", [TILE, 1024], BF16))
    s_h = [E_(nc.sbuf_tensor(f"s_h{i}", [TILE, 512], BF16)) for i in range(3)]
    s_y = E_(nc.sbuf_tensor("s_y", [TILE, 2048], BF16))
    s_ht = E_(nc.sbuf_tensor("s_ht", [TILE, 512], F32))
    s_VS = E_(nc.sbuf_tensor("s_VS", [TILE, 1024], BF16))
    s_xr = E_(nc.sbuf_tensor("s_xr", [TILE, 2048], F32))
    s_stats = E_(nc.sbuf_tensor("s_stats", [TILE, 128], F32))
    p_pp = [E_(nc.psum_tensor(f"p_pp{i}", [TILE, 512], F32)) for i in range(2)]
    p_py = E_(nc.psum_tensor("p_py", [TILE, 1024], F32))
    p_st = E_(nc.psum_tensor("p_st", [TILE, 1024], F32))
    p_fr = [E_(nc.psum_tensor(f"p_fr{i}", [TILE, 512], F32)) for i in range(2)]

    SX = _Sem(nc.alloc_semaphore("SX"))
    ST = _Sem(nc.alloc_semaphore("ST"))
    KT = _Sem(nc.alloc_semaphore("KT"))
    YT = _Sem(nc.alloc_semaphore("YT"))
    GQ = _Sem(nc.alloc_semaphore("GQ"))
    GX = _Sem(nc.alloc_semaphore("GX"))
    XQ = _Sem(nc.alloc_semaphore("XQ"))
    DQ = _Sem(nc.alloc_semaphore("DQ"))
    GE = _Sem(nc.alloc_semaphore("GE"))
    TE = _Sem(nc.alloc_semaphore("TE"))
    VE = _Sem(nc.alloc_semaphore("VE"))
    SE = _Sem(nc.alloc_semaphore("SE"))

    O = _Ops()
    O.self_sem = {"v": VE, "s": SE, "g": GE}

    for s_w, w in ((s_wq, wqT), (s_wk, wkT), (s_wv, wvT), (s_wg, wgT), (s_wo, woT)):
        O.dma("g", GQ, lambda g, s_w=s_w, w=w: g.dma_start(
            out=s_w[:].rearrange("p (i e) -> p i e", i=8),
            in_=w.ap().rearrange("(i p) e -> p i e", p=TILE)))
    O.dma("g", GQ, lambda g: g.dma_start(out=s_gamT[:], in_=gamT.ap()))
    O.dma("g", GQ, lambda g: g.dma_start(out=s_betT[:], in_=betT.ap()))
    O.dma("g", GQ, lambda g: g.dma_start(out=s_mask[:], in_=maskt.ap()))
    O.dma("g", GQ, lambda g: g.dma_start(out=s_one[:], in_=ones_r.ap()))
    O.dma("g", GQ, lambda g: g.dma_start(out=s_onec[:], in_=ones_c.ap()))
    w_loaded = O.close_epoch(GQ)

    O.add("v", lambda v: v.memzero(s_h[0][:]), inc=VE)
    O.add("v", lambda v: v.memzero(s_eps[:]), inc=VE)
    O.add("v", lambda v: v.tensor_scalar_add(s_eps[:], s_eps[:], EPS), inc=VE)

    def row_times_wo(lhs_col_fn, gate):
        for hf in range(2):
            for j in range(8):
                O.add("t", lambda t, hf=hf, j=j: t.matmul(
                    p_pp[hf][0:1, :],
                    lhs_col_fn(j),
                    s_wo[:, 1024 * j + 512 * hf:1024 * j + 512 * (hf + 1)],
                    start=(j == 0), stop=(j == 7)),
                    wait=(gate if (hf == 0 and j == 0) else ()),
                    inc=(TE if j == 7 else None))
        te = TE.n
        for hf in range(2):
            O.add("v", lambda v, hf=hf: v.tensor_copy(
                s_row[0:1, 512 * hf:512 * (hf + 1)], p_pp[hf][0:1, :]),
                wait=([(TE, te)] if hf == 0 else ()), inc=VE)
        return te, VE.n

    def broadcast_row(dst, gate_ve):
        for hf in range(2):
            O.add("t", lambda t, hf=hf: t.matmul(
                p_pp[hf][:],
                s_one[0:1, 0:TILE],
                s_row[0:1, 512 * hf:512 * (hf + 1)],
                start=True, stop=True),
                wait=([(VE, gate_ve)] if hf == 0 else ()),
                inc=(TE if hf == 1 else None))
        te = TE.n
        for hf in range(2):
            O.add("v", lambda v, dst=dst, hf=hf: v.tensor_copy(
                dst[:, 512 * hf:512 * (hf + 1)], p_pp[hf][:]),
                wait=([(TE, te)] if hf == 0 else ()), inc=VE)
        return VE.n

    bw_te, bw_ve = row_times_wo(lambda j: s_betT[:, j:j + 1], [(GQ, w_loaded)])
    bwB_ve = broadcast_row(s_bwB, bw_ve)
    for j in range(8):
        O.add("v", lambda v, j=j: v.tensor_scalar_mul(
            s_wo[:, 1024 * j:1024 * (j + 1)],
            s_wo[:, 1024 * j:1024 * (j + 1)],
            s_gamT[:, j:j + 1]),
            wait=([(TE, bw_te)] if j == 0 else ()), inc=VE)
    fold_ve = VE.n
    ws_te, ws_ve = row_times_wo(lambda j: s_onec[:], [(VE, fold_ve)])
    wsB_ve = broadcast_row(s_wsB, ws_ve)
    init_ve = VE.n

    gc = [0]
    pv = [init_ve]
    pyf = {0: 0, 1: 0}
    pstu = {0: 0, 1: 0}
    vsu = [0]
    upd_ve = {}
    xt_ep = {}
    G = dict(st_last=0,
             y_last=0,
             qk_te=0,
             e_wo_te=0,
             fr_se={},
             ln1_se={},
             xr_store={},
             fr_ve={},
             e_wo_te2=0,
             scan0_ve=0, e0_done=0,
             yt_slot={},
             yst_slot={})

    def dump(name, src_ap, wait):
        if not DEBUG:
            return
        t = dbg[name]
        r, cn = src_ap.shape[0], src_ap.shape[1]
        O.dma("g", DQ, lambda g, t=t, s=src_ap, r=r, cn=cn: g.dma_start(
            out=t.ap()[0:r, 0:cn], in_=s), wait=wait)
        O.close_epoch(DQ)
        O.add("v", lambda v: v.engine_nop(), wait=[(DQ, DQ.n)])

    def ln_stats(src_ap, c0, wait_src):
        st = lambda i: s_stats[:, c0 + i:c0 + i + 1]
        O.add("s", lambda s: s.activation(
            s_nL[:, 1024:2048], src_ap, AF.Copy, accum_out=st(0)),
            wait=list(wait_src) + [(VE, init_ve)], inc=SE)
        O.add("s", lambda s: s.activation(
            s_nL[:, 1024:2048], src_ap, AF.Square, accum_out=st(1)), inc=SE)
        O.add("s", lambda s: s.activation(
            st(2), st(0), AF.Copy, scale=1.0 / D), inc=SE)
        O.add("s", lambda s: s.activation(
            st(3), st(2), AF.Square), inc=SE)
        O.add("s", lambda s: s.activation(
            st(4), st(3), AF.Identity, scale=-1.0, bias=s_eps[:]), inc=SE)
        O.add("s", lambda s: s.activation(
            st(5), st(1), AF.Identity, scale=1.0 / D, bias=st(4)), inc=SE)
        O.add("s", lambda s: s.activation(
            st(6), st(5), AF.Ln), inc=SE)
        O.add("s", lambda s: s.activation(
            st(7), st(6), AF.Exp, scale=-0.5), inc=SE)
        O.add("s", lambda s: s.activation(
            st(8), st(2), AF.Copy, scale=st(7)), inc=SE)
        O.add("s", lambda s: s.activation(
            st(9), st(8), AF.Copy, scale=-1.0), inc=SE)
        return SE.n

    def layernorm(src_ap, dst_ap, c0, wait_src):
        ln_stats(src_ap, c0, wait_src)
        st = lambda i: s_stats[:, c0 + i:c0 + i + 1]
        O.add("s", lambda s: s.activation(
            dst_ap, src_ap, AF.Identity, scale=st(7), bias=st(9)), inc=SE)
        return SE.n

    def bank_ap(b, n):
        return (p_pp[b][:, 0:n] if b < 2
                else p_py[:, 512 * (b - 2):512 * (b - 2) + n])

    def prep_front(tb, nt, wu, gp2):
        ntok = nt * TILE
        nch = ntok // C

        x_eps = {}

        def emit_loads():
            lw = [(SE, max([G["ln1_se"].get(tt, 0) for tt in range(nt)]))] + \
                 ([(DQ, DQ.n)] if DEBUG else [])
            for tt in range(nt):
                O.dma("g", SX, lambda g, tb=tb, tt=tt: g.dma_start(
                    out=s_x[:, 1024 * tt:1024 * (tt + 1)],
                    in_=xs.ap()[TILE * (tb + tt):TILE * (tb + tt + 1), :]),
                    wait=(lw if tt == 0 else ()))
            ep = O.close_epoch(SX)
            for tt in range(nt):
                x_eps[tt] = ep

        def u_tile(tt):
            def run():
                ep = x_eps[tt]
                d = layernorm(s_x[:, 1024 * tt:1024 * (tt + 1)],
                              s_xln[:, 1024 * tt:1024 * (tt + 1)],
                              64 + 16 * tt, [(SX, ep), (ST, xt_ep.get(0, 0))])
                G["ln1_se"][tt] = d
            return run

        def u_xt():
            for tt in range(nt):
                O.dma("sy", ST, lambda sy, tt=tt: sy.dma_start(
                    out=s_xlnT[:].rearrange("p (j q c) -> p j q c",
                                            j=8, q=4, c=TILE)[:, :, tt, :],
                    in_=s_xln[:, 1024 * tt:1024 * (tt + 1)].rearrange(
                        "p (j c) -> p j c", j=8),
                    transpose=True),
                    wait=([(SE, SE.n), (TE, G["qk_te"])] if tt == 0 else ()))
            ep = O.close_epoch(ST)
            for tt in range(nt):
                xt_ep[tt] = ep

        def u_gblk(p):
            def run():
                bank = p % 2
                w0 = [(ST, xt_ep[nt - 1]), (GQ, w_loaded),
                      (SE, G["fr_se"].get(bank, 0)),
                      (VE, G["fr_ve"].get(bank, 0))]
                for j in range(8):
                    O.add("t", lambda t, p=p, j=j, bank=bank: t.matmul(
                        p_fr[bank][:, 0:ntok],
                        s_wg[:, 1024 * j + TILE * p:1024 * j + TILE * (p + 1)],
                        s_xlnT[:, 512 * j:512 * j + ntok],
                        start=(j == 0), stop=(j == 7)),
                        wait=(w0 if j == 0 else ()), inc=(TE if j == 7 else None))
                O.add("s", lambda s, bank=bank: s.activation(
                    s_nL[:, 1024:1024 + ntok], p_fr[bank][:, 0:ntok],
                    AF.Exp, scale=-1.0),
                    wait=[(TE, TE.n)], inc=SE)
                O.add("s", lambda s, p=p: s.activation(
                    s_S[:, 512 * p:512 * p + ntok], s_nL[:, 1024:1024 + ntok],
                    AF.Ln, bias=1.0), inc=SE)
                G["fr_se"][bank] = SE.n
            return run

        def u_scan0():
            sd = SE.n
            first = True
            for cc in range(min(4, nch)):
                for p in range(8):
                    O.add("v", lambda v, cc=cc, p=p: v.tensor_tensor_scan(
                        s_nL[:, 512 * cc + C * p:512 * cc + C * (p + 1)],
                        s_S[:, 512 * p + C * cc:512 * p + C * (cc + 1)],
                        s_S[:, 512 * p + C * cc:512 * p + C * (cc + 1)],
                        0.0, ALU.add, ALU.bypass),
                        wait=([(SE, sd)] if first else ()))
                    first = False
            G["scan0_ve"] = VE.n
            O.add("s", lambda s: s.activation(
                s_E[:, 0:2048].rearrange("p (c q o) -> p c q o",
                                         q=8, o=C)[:, :, 0:6, :],
                s_nL[:, 0:2048].rearrange("p (c q o) -> p c q o",
                                          q=8, o=C)[:, :, 0:6, :],
                AF.Exp, scale=-1.0),
                wait=[(VE, G["scan0_ve"]), (TE, G["e_wo_te2"])], inc=SE)
            O.add("s", lambda s: s.activation(
                s_Er[:, 0:2048], s_nL[:, 0:2048], AF.Exp, scale=1.0), inc=SE)
            G["e0_done"] = SE.n

        class _U(list):
            pass
        units = _U([u_tile(tt) for tt in range(nt)] + [u_xt] +
                   [u_gblk(p) for p in range(8)] + [u_scan0])
        units.loads = emit_loads

        def finish(dbg_this=False):
            s_done = SE.n

            evac = {}

            def bank6(b, n):
                return p_fr[b - 4][:, 0:n] if b >= 4 else bank_ap(b, n)

            def first_use(bk):
                if bk < 2:
                    return [(VE, pv[0])]
                if bk < 4:
                    return [(VE, pyf[bk - 2])]
                return [(SE, G["fr_se"].get(bk - 4, 0)),
                        (VE, G["fr_ve"].get(bk - 4, 0))]

            for tt in range(nt):
                for hf in range(2):
                    bk = 2 + (2 * tt + hf) % 4
                    w0 = ([(VE, evac[bk])] if bk in evac else first_use(bk))
                    for j in range(8):
                        O.add("t", lambda t, tt=tt, j=j, hf=hf, bk=bk: t.matmul(
                            bank6(bk, 512),
                            s_xlnT[:, 512 * j + TILE * tt:512 * j + TILE * (tt + 1)],
                            s_wv[:, 1024 * j + 512 * hf:1024 * j + 512 * (hf + 1)],
                            start=(j == 0), stop=(j == 7)),
                            wait=(w0 if j == 0 else ()),
                            inc=(TE if j == 7 else None))
                    O.add("v", lambda v, tt=tt, hf=hf, bk=bk: v.tensor_copy(
                        s_V[:, 1024 * tt + 512 * hf:1024 * tt + 512 * (hf + 1)],
                        bank6(bk, 512)),
                        wait=[(TE, TE.n), (GE, GE.n)], inc=VE)
                    evac[bk] = VE.n
                    if bk >= 4:
                        G["fr_ve"][bk - 4] = VE.n
            v_done = VE.n

            O.add("s", lambda s, nch=nch: s.activation(
                s_E[:, 0:512 * min(4, nch)].rearrange(
                    "p (c q o) -> p c q o", q=8, o=C)[:, :, 6:8, :],
                s_nL[:, 0:512 * min(4, nch)].rearrange(
                    "p (c q o) -> p c q o", q=8, o=C)[:, :, 6:8, :],
                AF.Exp, scale=-1.0),
                wait=[(VE, G["scan0_ve"]), (TE, G["e_wo_te"]),
                      (SE, G["e0_done"])], inc=SE)
            e_done = SE.n
            if nch > 4:
                firstv = True
                for cc in range(4, nch):
                    for p in range(8):
                        O.add("v", lambda e, cc=cc, p=p: e.tensor_tensor_scan(
                            s_nL[:, 512 * (cc % 4) + C * p:512 * (cc % 4) + C * (p + 1)],
                            s_S[:, 512 * p + C * cc:512 * p + C * (cc + 1)],
                            s_S[:, 512 * p + C * cc:512 * p + C * (cc + 1)],
                            0.0, ALU.add, ALU.bypass),
                            wait=([(SE, e_done)] if firstv else ()))
                        firstv = False
                scan_done = VE.n
                O.add("s", lambda s: s.activation(
                    s_E[:, 2048:4096], s_nL[:, 0:2048], AF.Exp, scale=-1.0),
                    wait=[(VE, scan_done)], inc=SE)
                O.add("s", lambda s: s.activation(
                    s_Er[:, 2048:4096], s_nL[:, 0:2048], AF.Exp, scale=1.0),
                    inc=SE)
                e_done = SE.n
            O.add("v", lambda v, nch=nch, gp2=gp2: v.tensor_copy(
                s_ec[:, 64 * gp2:64 * (gp2 + 1)].rearrange(
                    "p (c q) -> p c q", q=8)[:, 0:nch, :],
                s_E[:].rearrange("p (c q o) -> p c q o",
                                 q=8, o=C)[:, 0:nch, :, C - 1]),
                wait=[(SE, e_done)], inc=VE)
            ec_done = VE.n

            if dbg_this:
                dump("d_xln", s_xln[:], [(VE, ec_done)])
                dump("d_S", s_S[:], [(SE, s_done)])
                dump("d_negL", s_nL[:], [(VE, ec_done)])

            rp = [2]

            def proj_qk(s_w, s_dst, s_scale, skip, winmajor):
                if skip:
                    return
                for p in range(8):
                    bank = rp[0] % 6
                    rp[0] += 1
                    w0 = ([(VE, evac[bank])] if bank in evac
                          else first_use(bank))
                    for j in range(8):
                        O.add("t", lambda t, p=p, j=j, bank=bank, s_w=s_w,
                              ntok=ntok: t.matmul(
                            bank6(bank, ntok),
                            s_w[:, 1024 * j + TILE * p:1024 * j + TILE * (p + 1)],
                            s_xlnT[:, 512 * j:512 * j + ntok],
                            start=(j == 0), stop=(j == 7)),
                            wait=(w0 if j == 0 else ()),
                            inc=(TE if j == 7 else None))
                    if winmajor:
                        out_ap = (lambda p=p: s_dst[:].rearrange(
                            "p (w q e o) -> p w q e o",
                            w=4, q=8, e=2, o=C)[:, 0:nch // 2, p, :, :])
                    else:
                        out_ap = (lambda p=p: s_dst[:, 512 * p:512 * p + ntok]
                                  .rearrange("p (c o) -> p c o", o=C))
                    O.add("v", lambda v, p=p, bank=bank, s_scale=s_scale,
                          out_ap=out_ap, ntok=ntok, nch=nch:
                          v.tensor_tensor(
                              out_ap(),
                              bank6(bank, ntok).rearrange("p (c o) -> p c o", o=C),
                              s_scale[:].rearrange("p (c q o) -> p c q o",
                                                   q=8, o=C)[:, 0:nch, p, :],
                              ALU.mult),
                          wait=[(TE, TE.n), (SE, e_done)], inc=VE)
                    evac[bank] = VE.n
                    if bank >= 4:
                        G["fr_ve"][bank - 4] = VE.n
            proj_qk(s_wq, s_qt, s_E, wu, False)
            proj_qk(s_wk, s_kt, s_Er, False, True)
            qk_done = VE.n
            G["qk_te"] = TE.n

            if dbg_this:
                dump("d_qt", s_qt[:], [(VE, qk_done)])
                dump("d_kt", s_kt[:], [(VE, qk_done)])
                dump("d_V", s_V[:], [(VE, v_done)])

            return dict(v_done=v_done, ec_done=ec_done, qk_done=qk_done,
                        nch=nch, gp2=gp2)

        return units, finish

    def emit_yln(w):
        return ln_stats(s_y[:, 1024 * (w % 2):1024 * (w % 2 + 1)],
                        16 * w, [(VE, VE.n)])

    def emit_loop(tb, nt, wu, FR, splice, dbg_this=False):
        nch = FR["nch"]
        gp2 = FR["gp2"]
        half_cols = 512 * nch // 2
        O.dma("sy", KT, lambda sy, half_cols=half_cols: sy.dma_start(
            out=s_ktm[:, 0:half_cols].rearrange("p (m c) -> p m c", c=TILE),
            in_=s_kt[:, 0:half_cols].rearrange("p (m c) -> p m c", c=TILE),
            transpose=True),
            wait=[(VE, FR["qk_done"]), (TE, G["st_last"])])
        ktm_ep0 = O.close_epoch(KT)
        O.dma("sy", KT, lambda sy, half_cols=half_cols: sy.dma_start(
            out=s_ktm[:, half_cols:2 * half_cols].rearrange(
                "p (m c) -> p m c", c=TILE),
            in_=s_kt[:, half_cols:2 * half_cols].rearrange(
                "p (m c) -> p m c", c=TILE),
            transpose=True))
        ktm_ep1 = O.close_epoch(KT)
        yt_eps = {}
        yst = {}
        xr_ep = {}

        def xr_load(w):
            if wu or w >= nt:
                return
            O.dma("g", GX, lambda g, w=w, tb=tb: g.dma_start(
                out=s_xr[:, 1024 * (w % 2):1024 * (w % 2 + 1)],
                in_=xs.ap()[TILE * (tb + w):TILE * (tb + w + 1), :]),
                wait=[(XQ, G["xr_store"].get(w % 2, 0))])
            xr_ep[w] = O.close_epoch(GX)

        xr_load(0)
        xr_load(1)
        if splice:
            splice.loads()
            for _ in range(min(4, len(splice) - 9)):
                splice.pop(0)()
        vsg = {}

        def emit_vs(w2):
            O.add("g", lambda g_, w2=w2: g_.tensor_copy(
                s_VS[0:64, :],
                s_V[64:128, 1024 * w2:1024 * (w2 + 1)]),
                wait=[(VE, FR["v_done"]), (TE, vsu[0])], inc=GE)
            O.add("g", lambda g_, w2=w2: g_.tensor_copy(
                s_VS[64:128, :],
                s_V[0:64, 1024 * w2:1024 * (w2 + 1)]),
                inc=GE)
            vsg[w2] = GE.n

        if not wu:
            emit_vs(0)

        def emit_be_late(w):
            w0 = [(YT, yt_eps[w]),
                  (VE, FR["qk_done"] if w == 0 else pv[0])]
            for j in range(8):
                for hf in range(2):
                    O.add("t", lambda t, w=w, j=j, hf=hf: t.matmul(
                        p_pp[hf][:],
                        s_E[:, 512 * j + TILE * w:512 * j + TILE * (w + 1)],
                        s_wo[:, 1024 * j + 512 * hf:1024 * j + 512 * (hf + 1)],
                        start=(j == 0), stop=(j == 7)),
                        wait=(w0 if (j == 0 and hf == 0) else ()),
                        inc=(TE if (j == 7 and hf == 1) else None))
            G["e_wo_te2"] = G["e_wo_te"]
            G["e_wo_te"] = TE.n
            r_ap = s_stats[:, 16 * w + 7:16 * w + 8]
            nrm_ap = s_stats[:, 16 * w + 9:16 * w + 10]
            xsl = 1024 * (w % 2)
            for hf in range(2):
                sl = slice(512 * hf, 512 * (hf + 1))
                O.add("v", lambda v, hf=hf, sl=sl, r_ap=r_ap:
                      v.scalar_tensor_tensor(
                          p_pp[hf][:], p_pp[hf][:], r_ap, s_bwB[:, sl],
                          ALU.mult, ALU.add),
                      wait=([(TE, TE.n), (SE, yst[w])] if hf == 0 else ()),
                      inc=VE)
                O.add("v", lambda v, hf=hf, sl=sl, nrm_ap=nrm_ap:
                      v.scalar_tensor_tensor(
                          p_pp[hf][:], s_wsB[:, sl], nrm_ap,
                          p_pp[hf][:], ALU.mult, ALU.add),
                      inc=VE)
                O.add("v", lambda v, hf=hf, sl=sl, xsl=xsl: v.tensor_add(
                    s_xr[:, xsl + sl.start:xsl + sl.stop],
                    s_xr[:, xsl + sl.start:xsl + sl.stop],
                    p_pp[hf][:]),
                    wait=([(GX, xr_ep[w])] if hf == 0 else ()), inc=VE)
            pv[0] = VE.n
            O.dma("g", XQ, lambda g, w=w, tb=tb, xsl=xsl: g.dma_start(
                out=x2d.ap()[TILE * (tb + w - 2):TILE * (tb + w - 1), :],
                in_=s_xr[:, xsl:xsl + 1024]),
                wait=[(VE, VE.n)])
            G["xr_store"][w % 2] = O.close_epoch(XQ)
            xr_load(w + 2)
            if dbg_this and w == 0:
                dump("d_x2", s_xr[:, 0:1024], [(VE, VE.n)])

        npair = nch // 2
        for cc in range(nch):
            par = cc % 2
            w = cc // 2
            hs = gc[0] % 3
            hn = (gc[0] + 1) % 3
            ysl = 1024 * (w % 2)
            vs_done = vsg.get(w, 0)

            if not wu:
                for h in range(NH):
                    hp, hq = h % 2, h // 2
                    O.add("t", lambda t, w=w, h=h, hp=hp, hq=hq, par=par, cc=cc:
                          t.matmul(
                        p_py[64 * hp:64 * (hp + 1),
                             512 * par + 64 * hq:512 * par + 64 * (hq + 1)],
                        s_kt[64 * hp:64 * (hp + 1),
                             1024 * w + TILE * hq + C * par:
                             1024 * w + TILE * hq + C * (par + 1)],
                        s_qt[64 * hp:64 * (hp + 1),
                             512 * hq + C * cc:512 * hq + C * (cc + 1)],
                        start=True, stop=True),
                        wait=([(VE, max(FR["qk_done"], pyf[par]))]
                              if h == 0 else ()),
                        inc=(TE if h == NH - 1 else None))
                p_mm_done = TE.n
                O.add("v", lambda v, par=par: v.tensor_tensor(
                    s_P[:, 512 * par:512 * (par + 1)],
                    p_py[:, 512 * par:512 * (par + 1)],
                    s_mask[:, 512 * par:512 * (par + 1)], ALU.mult),
                    wait=[(TE, p_mm_done)], inc=VE)
                pmask_done = VE.n

            for h in range(NH):
                hp, hq = h % 2, h // 2
                O.add("t", lambda t, w=w, h=h, hp=hp, hq=hq, par=par:
                      t.matmul(
                          p_st[64 * par:64 * (par + 1), 64 * h:64 * (h + 1)],
                          s_ktm[64 * par:64 * (par + 1),
                                1024 * w + TILE * hq + 64 * hp:
                                1024 * w + TILE * hq + 64 * (hp + 1)],
                          s_V[64 * par:64 * (par + 1),
                              1024 * w + 64 * h:1024 * w + 64 * (h + 1)],
                          start=True, stop=True),
                      wait=([(KT, ktm_ep0 if w < (nch + 3) // 4 else ktm_ep1),
                             (VE, pstu[par])] if h == 0 else ()),
                      inc=(TE if h == NH - 1 else None))
            st_mm = TE.n
            G["st_last"] = st_mm

            def emit_update(cc=cc, par=par, hs=hs, hn=hn, st_mm=st_mm):
                for hp in range(2):
                    O.add("v", lambda v, par=par, hp=hp, hs=hs: v.tensor_tensor(
                        s_ht[64 * hp:64 * (hp + 1), :].rearrange(
                            "p (q o) -> p q o", o=C),
                        p_st[64 * par:64 * (par + 1), :].rearrange(
                            "p (q two o) -> p q two o", two=2, o=C)[:, :, hp, :],
                        s_h[hs][64 * hp:64 * (hp + 1), :].rearrange(
                            "p (q o) -> p q o", o=C),
                        ALU.add),
                        wait=([(TE, st_mm), (VE, FR["ec_done"])]
                              if hp == 0 else ()), inc=VE)
                pstu[par] = VE.n
                O.add("v", lambda v, cc=cc, hn=hn: v.tensor_tensor(
                    s_h[hn][:].rearrange("p (q o) -> p q o", o=C),
                    s_ht[:].rearrange("p (q o) -> p q o", o=C),
                    s_ec[:, 64 * gp2 + 8 * cc:64 * gp2 + 8 * cc + 8]
                    .unsqueeze(2).broadcast_to([TILE, 8, C]),
                    ALU.mult), inc=VE)
                upd_ve[hn] = VE.n
            if wu or par == 0:
                emit_update()

            if (not wu) and par == 1:
                ug = max(upd_ve.get((gc[0] - 1) % 3, 0),
                         upd_ve.get(gc[0] % 3, 0))
                for c2 in (cc - 1, cc):
                    par2 = c2 % 2
                    hs2 = (gc[0] - (cc - c2)) % 3
                    for h in range(NH):
                        hp, hq = h % 2, h // 2
                        rhs_v = (
                            (lambda t, c2=c2, hp=hp, h=h:
                             s_V[64 * hp:64 * (hp + 1),
                                 1024 * (c2 // 2) + 64 * h:1024 * (c2 // 2) + 64 * (h + 1)])
                            if hp == par2 else
                            (lambda t, hp=hp, h=h:
                             s_VS[64 * hp:64 * (hp + 1), 64 * h:64 * (h + 1)])
                        )
                        O.add("t", lambda t, c2=c2, par2=par2, h=h, hp=hp, hq=hq,
                              rhs_v=rhs_v: t.matmul(
                                  p_py[64 * hp:64 * (hp + 1),
                                       512 * par2 + 64 * hq:512 * par2 + 64 * (hq + 1)],
                                  s_P[64 * hp:64 * (hp + 1),
                                      512 * par2 + 64 * hq:512 * par2 + 64 * (hq + 1)],
                                  rhs_v(t),
                                  start=True, stop=False),
                              wait=(([(VE, max(pmask_done, ug)), (GE, vs_done)])
                                    if h == 0 else ()))
                        O.add("t", lambda t, c2=c2, par2=par2, h=h, hp=hp, hq=hq,
                              hs2=hs2: t.matmul(
                                  p_py[64 * hp:64 * (hp + 1),
                                       512 * par2 + 64 * hq:512 * par2 + 64 * (hq + 1)],
                                  s_qt[64 * hp:64 * (hp + 1),
                                       512 * hq + C * c2:512 * hq + C * (c2 + 1)],
                                  s_h[hs2][64 * hp:64 * (hp + 1),
                                           64 * hq:64 * (hq + 1)],
                                  start=False, stop=True),
                              inc=(TE if (h == NH - 1 and c2 == cc) else None))
                G["y_last"] = TE.n
                vsu[0] = TE.n
                if w + 1 < npair:
                    emit_vs(w + 1)
                for par2 in (0, 1):
                    for hp in (0, 1):
                        O.add("v", lambda v, ysl=ysl, par2=par2, hp=hp:
                              v.tensor_copy(
                            s_y[64 * par2:64 * (par2 + 1),
                                ysl:ysl + 1024].rearrange(
                                "p (q two o) -> p q two o", two=2, o=C)[:, :, hp, :],
                            p_py[64 * hp:64 * (hp + 1),
                                 512 * par2:512 * (par2 + 1)].rearrange(
                                "p (q o) -> p q o", o=C)),
                            wait=(([(TE, TE.n),
                                    (YT, G["yt_slot"].get(w % 2, 0)),
                                    (SE, G["yst_slot"].get(w % 2, 0))])
                                  if (par2 == 0 and hp == 0) else ()),
                            inc=VE)
                    pyf[par2] = VE.n
                O.dma("sy", YT, lambda sy, w=w, ysl=ysl: sy.dma_start(
                    out=s_E[:].rearrange("p (j q c) -> p j q c",
                                         j=8, q=4, c=TILE)[:, :, w, :],
                    in_=s_y[:, ysl:ysl + 1024].rearrange(
                        "p (j c) -> p j c", j=8),
                    transpose=True),
                    wait=[(VE, VE.n)])
                yt_eps[w] = O.close_epoch(YT)
                G["yt_slot"][w % 2] = yt_eps[w]
                emit_update()
                if splice:
                    sched = {4: [1, 4, 4, 0], 2: [2, 7]}[npair]
                    for _ in range(min(sched[w], len(splice))):
                        splice.pop(0)()
                yst[w] = emit_yln(w)
                G["yst_slot"][w % 2] = yst[w]
                if w > 0:
                    emit_be_late(w - 1)
                if splice and w == npair - 1:
                    while splice:
                        splice.pop(0)()
            elif splice and wu and par == 1:
                sched = {2: [2, 7]}[npair]
                for _ in range(min(sched[w], len(splice))):
                    splice.pop(0)()
                if w == npair - 1:
                    while splice:
                        splice.pop(0)()
            gc[0] += 1

        if not wu:
            emit_be_late(nt - 1)
        if dbg_this:
            dump("d_y", s_y[:], [(VE, VE.n)])
            dump("d_h", s_h[gc[0] % 3][:], [(VE, VE.n)])

    ngrp = 0 if TRUNC in (1, 2) else (1 if TRUNC == 3 else 4)
    if TRUNC != 1:
        units, fin = prep_front(0, 2, True, 0)
        units.loads()
        for u in units:
            u()
        FR = fin()
        if ngrp:
            units, fin = prep_front(2, 4, False, 1)
        else:
            units = []
        emit_loop(0, 2, True, FR, units)
    for gi in range(ngrp):
        FR = fin(dbg_this=(DEBUG and gi == 0))
        if gi < ngrp - 1:
            units, fin = prep_front(2 + 4 * (gi + 1), 4, False, gi % 2)
        else:
            units = []
        emit_loop(2 + 4 * gi, 4, False, FR, units,
                  dbg_this=(DEBUG and gi == 0))
    a_sx, a_st, a_kt, a_yt = SX.n, ST.n, KT.n, YT.n
    a_gq, a_xq, a_dq, a_gx, a_ge = GQ.n, XQ.n, DQ.n, GX.n, GE.n
    a_te, a_ve, a_se = TE.n, VE.n, SE.n

    es.close()

    es2 = ExitStack()
    E2 = es2.enter_context
    s_x2T = E2(nc.sbuf_tensor("s_x2T", [TILE, 16 * D], BF16))
    s_hff = E2(nc.sbuf_tensor("s_hff", [TILE, NHC * 2048], BF16))
    s_wd = E2(nc.sbuf_tensor("s_wd", [TILE, NHC * 1024], BF16))
    s_wgu = [E2(nc.sbuf_tensor(f"s_wgu{i}", [TILE, 2048], BF16)) for i in range(2)]
    s_x2 = E2(nc.sbuf_tensor("s_x2", [TILE, 2048], F32))
    s_st2 = E2(nc.sbuf_tensor("s_st2", [TILE, 16], F32))
    s_sq2 = E2(nc.sbuf_tensor("s_sq2", [TILE, 1024], F32))
    s_a2 = E2(nc.sbuf_tensor("s_a2", [TILE, 1024], BF16))
    s_act2 = E2(nc.sbuf_tensor("s_act2", [TILE, 512], F32))
    p_g = [E2(nc.psum_tensor(f"p_g{i}", [TILE, 512], F32)) for i in range(2)]
    p_u = [E2(nc.psum_tensor(f"p_u{i}", [TILE, 512], F32)) for i in range(2)]
    p_d = [E2(nc.psum_tensor(f"p_d{i}", [TILE, 1024], F32)) for i in range(2)]

    barrier = [(TE, a_te), (VE, a_ve), (SE, a_se), (SX, a_sx), (ST, a_st),
               (KT, a_kt), (YT, a_yt), (GQ, a_gq), (XQ, a_xq), (DQ, a_dq),
               (GX, a_gx), (GE, a_ge)]
    first = {k: True for k in ("t", "v", "s", "sy", "g")}

    def addD(eng, fn, wait=(), inc=None, inc_by=1):
        w = list(wait)
        if first[eng]:
            w = barrier + w
            first[eng] = False
        O.add(eng, fn, wait=w, inc=inc, inc_by=inc_by)

    def dmaD(eng, sem, fn, wait=()):
        w = list(wait)
        if first[eng]:
            w = barrier + w
            first[eng] = False
        if sem.pending is not None:
            w.append((sem, sem.pending))
            sem.pending = None
        O.add(eng, fn, wait=w, inc=sem, inc_by=16)

    a2_done = {}
    for tt in range(16 if TRUNC == 0 else 0):
        buf = tt % 2
        dmaD("g", GX, lambda g, tt=tt, buf=buf: g.dma_start(
            out=s_x2[:, 1024 * buf:1024 * (buf + 1)],
            in_=x2d.ap()[TILE * tt:TILE * (tt + 1), :]),
            wait=([(VE, a2_done[tt - 2])] if tt >= 2 else []))
        ld = O.close_epoch(GX)
        xt = s_x2[:, 1024 * buf:1024 * (buf + 1)]
        addD("v", lambda v, xt=xt: v.reduce_sum(
            s_st2[:, 0:1], xt, axis=AX.X), wait=[(GX, ld)], inc=VE)
        addD("s", lambda s, xt=xt: s.activation(
            s_sq2[:], xt, AF.Square, accum_out=s_st2[:, 1:2]),
            wait=[(GX, ld), (VE, VE.n)], inc=SE)
        addD("v", lambda v: v.tensor_scalar(
            s_st2[:, 2:3], s_st2[:, 0:1], 1.0 / D, None, ALU.mult), inc=VE)
        addD("v", lambda v: v.tensor_mul(
            s_st2[:, 3:4], s_st2[:, 2:3], s_st2[:, 2:3]), inc=VE)
        addD("v", lambda v: v.tensor_scalar(
            s_st2[:, 4:5], s_st2[:, 1:2], 1.0 / D, s_st2[:, 3:4],
            ALU.mult, ALU.subtract), wait=[(SE, SE.n)], inc=VE)
        addD("v", lambda v: v.tensor_scalar_add(
            s_st2[:, 4:5], s_st2[:, 4:5], EPS), inc=VE)
        addD("s", lambda s: s.activation(
            s_st2[:, 5:6], s_st2[:, 4:5], AF.Ln), wait=[(VE, VE.n)], inc=SE)
        addD("s", lambda s: s.activation(
            s_st2[:, 6:7], s_st2[:, 5:6], AF.Exp, scale=-0.5), inc=SE)
        addD("v", lambda v, xt=xt: v.tensor_scalar(
            s_a2[:], xt, s_st2[:, 2:3], s_st2[:, 6:7], ALU.subtract, ALU.mult),
            wait=[(SE, SE.n), (ST, ST.n)], inc=VE)
        a2_done[tt] = VE.n
        dmaD("sy", ST, lambda sy, tt=tt: sy.dma_start(
            out=s_x2T[:].rearrange("p (j q c) -> p j q c",
                                   j=8, q=16, c=TILE)[:, :, tt, :],
            in_=s_a2[:].rearrange("p (j c) -> p j c", j=8), transpose=True),
            wait=[(VE, a2_done[tt])])
        O.close_epoch(ST)
    ln2_tp = ST.n

    if TRUNC == 0:
        dmaD("g", GQ, lambda g: g.dma_start(
            out=s_wd[:].rearrange("p (hc d) -> p hc d", hc=NHC),
            in_=wdi.ap().rearrange("(hc p) d -> p hc d", p=TILE)))
    wd_done = O.close_epoch(GQ)

    wgu_users = {}
    silu_of = {}
    hff_of = {}
    for hc in range(NHC if TRUNC == 0 else 0):
        buf = hc % 2
        dmaD("g", GQ, lambda g, hc=hc, buf=buf: g.dma_start(
            out=s_wgu[buf][:, 0:1024].rearrange("p (i h) -> p i h", i=8),
            in_=wgi.ap()[:, TILE * hc:TILE * (hc + 1)].rearrange(
                "(i p) h -> p i h", p=TILE)),
            wait=[(TE, wgu_users.get(buf, 0))])
        dmaD("g", GQ, lambda g, hc=hc, buf=buf: g.dma_start(
            out=s_wgu[buf][:, 1024:2048].rearrange("p (i h) -> p i h", i=8),
            in_=wui.ap()[:, TILE * hc:TILE * (hc + 1)].rearrange(
                "(i p) h -> p i h", p=TILE)))
        gu_ld = O.close_epoch(GQ)
        for t4 in range(4):
            pb = t4 % 2
            w0 = [(GQ, gu_ld), (ST, ln2_tp), (SE, silu_of.get(pb, 0))]
            for j in range(8):
                addD("t", lambda t, buf=buf, t4=t4, j=j, pb=pb: t.matmul(
                    p_g[pb][:],
                    s_wgu[buf][:, TILE * j:TILE * (j + 1)],
                    s_x2T[:, 2048 * j + 512 * t4:2048 * j + 512 * (t4 + 1)],
                    start=(j == 0), stop=(j == 7)),
                    wait=(w0 if j == 0 else ()), inc=(TE if j == 7 else None))
            addD("s", lambda s, pb=pb: s.activation(
                s_act2[:], p_g[pb][:], AF.Silu, scale=SCL),
                wait=[(TE, TE.n), (VE, hff_of.get(1 - pb, 0))], inc=SE)
            silu_of[pb] = SE.n
            silu_done = SE.n
            w1 = [(VE, hff_of.get(pb, 0))]
            for j in range(8):
                addD("t", lambda t, buf=buf, t4=t4, j=j, pb=pb: t.matmul(
                    p_u[pb][:],
                    s_wgu[buf][:, 1024 + TILE * j:1024 + TILE * (j + 1)],
                    s_x2T[:, 2048 * j + 512 * t4:2048 * j + 512 * (t4 + 1)],
                    start=(j == 0), stop=(j == 7)),
                    wait=(w1 if j == 0 else ()), inc=(TE if j == 7 else None))
            addD("v", lambda v, hc=hc, t4=t4, pb=pb: v.scalar_tensor_tensor(
                s_hff[:, 2048 * hc + 512 * t4:2048 * hc + 512 * (t4 + 1)],
                p_u[pb][:], SCL * SCL, s_act2[:], ALU.mult, ALU.mult),
                wait=[(TE, TE.n), (SE, silu_done)], inc=VE)
            hff_of[pb] = VE.n
        wgu_users[buf] = TE.n
    hff_done = VE.n

    resid_of = {}
    out_st = {}
    for ts in range(16 if TRUNC == 0 else 0):
        buf = ts % 2
        dmaD("g", GX, lambda g, ts=ts, buf=buf: g.dma_start(
            out=s_x2[:, 1024 * buf:1024 * (buf + 1)],
            in_=x2d.ap()[TILE * ts:TILE * (ts + 1), :]),
            wait=[(VE, resid_of.get(buf, 0)), (XQ, out_st.get(buf, 0))])
        x2_ld = O.close_epoch(GX)
        w0 = [(VE, hff_done if ts == 0 else resid_of.get(buf, 0)), (GQ, wd_done)]
        for hf in range(2):
            for hcc in range(NHC):
                addD("t", lambda t, ts=ts, buf=buf, hf=hf, hcc=hcc: t.matmul(
                    p_d[buf][:, 512 * hf:512 * (hf + 1)],
                    s_hff[:, 2048 * hcc + TILE * ts:2048 * hcc + TILE * (ts + 1)],
                    s_wd[:, 1024 * hcc + 512 * hf:1024 * hcc + 512 * (hf + 1)],
                    start=(hcc == 0), stop=(hcc == NHC - 1)),
                    wait=(w0 if (hf == 0 and hcc == 0) else ()),
                    inc=(TE if (hf == 1 and hcc == NHC - 1) else None))
        addD("v", lambda v, buf=buf: v.tensor_add(
            s_x2[:, 1024 * buf:1024 * (buf + 1)],
            s_x2[:, 1024 * buf:1024 * (buf + 1)], p_d[buf][:]),
            wait=[(TE, TE.n), (GX, x2_ld)], inc=VE)
        resid_of[buf] = VE.n
        dmaD("g", XQ, lambda g, ts=ts, buf=buf: g.dma_start(
            out=out.ap()[TILE * ts:TILE * (ts + 1), :],
            in_=s_x2[:, 1024 * buf:1024 * (buf + 1)]),
            wait=[(VE, VE.n)])
        out_st[buf] = O.close_epoch(XQ)
        if DEBUG and ts == 0:
            dump("d_hff", s_hff[:, 0:2048], [(VE, hff_done)])

    final_xq = XQ.n
    final_dq = DQ.n

    with nc.Block() as block:
        @block.tensor
        def _(t):
            for f in O.q["t"]:
                f(t)

        @block.vector
        def _(v):
            for f in O.q["v"]:
                f(v)

        @block.scalar
        def _(s):
            for f in O.q["s"]:
                f(s)

        @block.sync
        def _(sy):
            for f in O.q["sy"]:
                f(sy)

        @block.gpsimd
        def _(g):
            for f in O.q["g"]:
                f(g)
            g.wait_ge(XQ.h, final_xq)
            if final_dq:
                g.wait_ge(DQ.h, final_dq)

    es2.close()
    return nc


def _shard_inputs(inputs):
    x = np.asarray(inputs["L0_attn_x"], np.float32).reshape(16384, D)
    wqT = np.ascontiguousarray(np.asarray(inputs["L0_attn_Wq"], np.float32).T)
    wkT = np.ascontiguousarray(np.asarray(inputs["L0_attn_Wk"], np.float32).T)
    wvT = np.ascontiguousarray(np.asarray(inputs["L0_attn_Wv"], np.float32).T)
    wgT = np.ascontiguousarray(np.asarray(inputs["L0_attn_Wg"], np.float32).T)
    woT = np.ascontiguousarray(np.asarray(inputs["L0_attn_Wo"], np.float32).T)
    gamT = np.ascontiguousarray(
        np.asarray(inputs["L0_attn_gamma"], np.float32).reshape(8, TILE).T)
    betT = np.ascontiguousarray(
        np.asarray(inputs["L0_attn_beta"], np.float32).reshape(8, TILE).T)
    wgi = np.ascontiguousarray(np.asarray(inputs["L0_ffn_w_gate"], np.int32).T)
    wui = np.ascontiguousarray(np.asarray(inputs["L0_ffn_w_up"], np.int32).T)
    wdi = np.ascontiguousarray(np.asarray(inputs["L0_ffn_w_down"], np.int32).T)
    in_maps = []
    for c in range(NCORES):
        lo = c * T_OWN
        if c == 0:
            xsl = np.concatenate([np.zeros((WU, D), np.float32), x[0:T_OWN]], 0)
        else:
            xsl = np.ascontiguousarray(x[lo - WU:lo + T_OWN])
        in_maps.append(dict(xs=xsl, wqT=wqT, wkT=wkT, wvT=wvT, wgT=wgT, woT=woT,
                            gamT=gamT, betT=betT, wgi=wgi, wui=wui, wdi=wdi))
    return in_maps


def kernel(**inputs):
    in_maps = _shard_inputs(inputs)
    nc = build_program()
    res = run_bass_kernel_spmd(nc, in_maps, core_ids=list(range(NCORES)))
    outs = [np.asarray(res.results[c]["out"]) for c in range(NCORES)]
    full = np.concatenate(outs, axis=0).reshape(1, 16384, D)
    if DEBUG:
        kernel.last_results = res
    return full.astype(np.float32)


# version 23
# speedup vs baseline: 1.0592x; 1.0029x over previous
import numpy as np
from contextlib import ExitStack

import concourse.bass as bass
import concourse.mybir as mybir
from concourse.bass_utils import run_bass_kernel_spmd

F32 = mybir.dt.float32
BF16 = mybir.dt.bfloat16
I32 = mybir.dt.int32
AF = mybir.ActivationFunctionType
ALU = mybir.AluOpType
AX = mybir.AxisListType

NCORES = 8
D = 1024
NH = 16
C = 64
TILE = 128
T_OWN = 2048
WU = 256
T_ALL = T_OWN + WU
HID = 3072
NHC = HID // TILE
SCL = 1.0 / 127.0
EPS = 1e-5

DEBUG = False
TRUNC = 0


class _Sem:
    def __init__(self, h):
        self.h = h
        self.n = 0
        self.pending = None


class _Ops:

    def __init__(self):
        self.q = {k: [] for k in ("t", "v", "s", "sy", "g")}
        self.self_sem = {}

    def add(self, eng, fn, wait=(), inc=None, inc_by=1):
        wait = list(wait)
        ss = self.self_sem.get(eng)
        if ss is not None:
            wait.append((ss, ss.n))
            if inc is None:
                inc, inc_by = ss, 1
        if inc is not None:
            inc.n += inc_by
        w = tuple((s, v) for (s, v) in wait if v > 0)

        def run(e, fn=fn, w=w, inc=inc, inc_by=inc_by):
            for sem, val in w:
                e.wait_ge(sem.h, val)
            ins = fn(e)
            if inc is not None and ins is not None:
                ins.then_inc(inc.h, inc_by)
        self.q[eng].append(run)

    def dma(self, eng, sem, fn, wait=()):
        w = list(wait)
        if sem.pending is not None:
            w.append((sem, sem.pending))
            sem.pending = None
        self.add(eng, fn, wait=w, inc=sem, inc_by=16)

    def close_epoch(self, sem):
        sem.pending = sem.n
        return sem.n


def build_program():
    nc = bass.Bass("TRN2", target_bir_lowering=False)

    xs = nc.dram_tensor("xs", [T_ALL, D], F32, kind="ExternalInput")
    wqT = nc.dram_tensor("wqT", [D, D], F32, kind="ExternalInput")
    wkT = nc.dram_tensor("wkT", [D, D], F32, kind="ExternalInput")
    wvT = nc.dram_tensor("wvT", [D, D], F32, kind="ExternalInput")
    wgT = nc.dram_tensor("wgT", [D, D], F32, kind="ExternalInput")
    woT = nc.dram_tensor("woT", [D, D], F32, kind="ExternalInput")
    gamT = nc.dram_tensor("gamT", [TILE, 8], F32, kind="ExternalInput")
    betT = nc.dram_tensor("betT", [TILE, 8], F32, kind="ExternalInput")
    wgi = nc.dram_tensor("wgi", [D, HID], I32, kind="ExternalInput")
    wui = nc.dram_tensor("wui", [D, HID], I32, kind="ExternalInput")
    wdi = nc.dram_tensor("wdi", [HID, D], I32, kind="ExternalInput")
    out = nc.dram_tensor("out", [T_OWN, D], F32, kind="ExternalOutput")

    x2d = nc.dram_tensor("x2d", [T_OWN, D], F32)

    mask_np = np.tile(np.triu(np.ones((C, C), np.float32)), (2, NH))
    maskt = nc.inline_tensor(mask_np, "mask_rep")
    ones_r = nc.inline_tensor(np.ones((1, TILE), np.float32), "ones_r")
    ones_c = nc.inline_tensor(np.ones((TILE, 1), np.float32), "ones_c")

    dbg = {}
    if DEBUG:
        for name, shape in [
            ("d_xln", [TILE, 4096]), ("d_S", [TILE, 4096]), ("d_negL", [TILE, 2048]),
            ("d_qt", [TILE, 4096]), ("d_kt", [TILE, 4096]), ("d_V", [TILE, 4096]),
            ("d_h", [TILE, 512]), ("d_y", [TILE, 4096]),
            ("d_x2", [TILE, 1024]), ("d_hff", [TILE, 2048]),
        ]:
            dbg[name] = nc.dram_tensor(name, shape, F32, kind="ExternalOutput")

    es = ExitStack()
    E_ = es.enter_context
    s_wq = E_(nc.sbuf_tensor("s_wq", [TILE, 8 * D], BF16))
    s_wk = E_(nc.sbuf_tensor("s_wk", [TILE, 8 * D], BF16))
    s_wv = E_(nc.sbuf_tensor("s_wv", [TILE, 8 * D], BF16))
    s_wg = E_(nc.sbuf_tensor("s_wg", [TILE, 8 * D], BF16))
    s_wo = E_(nc.sbuf_tensor("s_wo", [TILE, 8 * D], BF16))
    s_gamT = E_(nc.sbuf_tensor("s_gamT", [TILE, 8], F32))
    s_betT = E_(nc.sbuf_tensor("s_betT", [TILE, 8], BF16))
    s_row = E_(nc.sbuf_tensor("s_row", [1, D], BF16))
    s_one = E_(nc.sbuf_tensor("s_one", [1, TILE], BF16))
    s_onec = E_(nc.sbuf_tensor("s_onec", [TILE, 1], BF16))
    s_bwB = E_(nc.sbuf_tensor("s_bwB", [TILE, D], BF16))
    s_wsB = E_(nc.sbuf_tensor("s_wsB", [TILE, D], BF16))
    s_eps = E_(nc.sbuf_tensor("s_eps", [TILE, 1], F32))
    s_mask = E_(nc.sbuf_tensor("s_mask", [TILE, 1024], BF16))
    s_x = E_(nc.sbuf_tensor("s_x", [TILE, 4096], F32))
    s_xln = E_(nc.sbuf_tensor("s_xln", [TILE, 4096], BF16))
    s_xlnT = E_(nc.sbuf_tensor("s_xlnT", [TILE, 4096], BF16))
    s_S = E_(nc.sbuf_tensor("s_S", [TILE, 4096], BF16))
    s_nL = E_(nc.sbuf_tensor("s_nL", [TILE, 2048], F32))
    s_E = E_(nc.sbuf_tensor("s_E", [TILE, 4096], BF16))
    s_Er = E_(nc.sbuf_tensor("s_Er", [TILE, 4096], BF16))
    s_ec = E_(nc.sbuf_tensor("s_ec", [TILE, 128], F32))
    s_qt = E_(nc.sbuf_tensor("s_qt", [TILE, 4096], BF16))
    s_kt = E_(nc.sbuf_tensor("s_kt", [TILE, 4096], BF16))
    s_V = E_(nc.sbuf_tensor("s_V", [TILE, 4096], BF16))
    s_ktm = E_(nc.sbuf_tensor("s_ktm", [TILE, 4096], BF16))
    s_P = E_(nc.sbuf_tensor("s_P", [TILE, 1024], BF16))
    s_h = [E_(nc.sbuf_tensor(f"s_h{i}", [TILE, 512], BF16)) for i in range(3)]
    s_y = E_(nc.sbuf_tensor("s_y", [TILE, 2048], BF16))
    s_ht = E_(nc.sbuf_tensor("s_ht", [TILE, 512], F32))
    s_VS = E_(nc.sbuf_tensor("s_VS", [TILE, 1024], BF16))
    s_xr = E_(nc.sbuf_tensor("s_xr", [TILE, 2048], F32))
    s_stats = E_(nc.sbuf_tensor("s_stats", [TILE, 128], F32))
    p_pp = [E_(nc.psum_tensor(f"p_pp{i}", [TILE, 512], F32)) for i in range(2)]
    p_py = E_(nc.psum_tensor("p_py", [TILE, 1024], F32))
    p_st = E_(nc.psum_tensor("p_st", [TILE, 1024], F32))
    p_fr = [E_(nc.psum_tensor(f"p_fr{i}", [TILE, 512], F32)) for i in range(2)]

    SX = _Sem(nc.alloc_semaphore("SX"))
    ST = _Sem(nc.alloc_semaphore("ST"))
    KT = _Sem(nc.alloc_semaphore("KT"))
    YT = _Sem(nc.alloc_semaphore("YT"))
    GQ = _Sem(nc.alloc_semaphore("GQ"))
    GX = _Sem(nc.alloc_semaphore("GX"))
    XQ = _Sem(nc.alloc_semaphore("XQ"))
    DQ = _Sem(nc.alloc_semaphore("DQ"))
    GE = _Sem(nc.alloc_semaphore("GE"))
    TE = _Sem(nc.alloc_semaphore("TE"))
    VE = _Sem(nc.alloc_semaphore("VE"))
    SE = _Sem(nc.alloc_semaphore("SE"))

    O = _Ops()
    O.self_sem = {"v": VE, "s": SE, "g": GE}

    for s_w, w in ((s_wq, wqT), (s_wk, wkT), (s_wv, wvT), (s_wg, wgT), (s_wo, woT)):
        O.dma("g", GQ, lambda g, s_w=s_w, w=w: g.dma_start(
            out=s_w[:].rearrange("p (i e) -> p i e", i=8),
            in_=w.ap().rearrange("(i p) e -> p i e", p=TILE)))
    O.dma("g", GQ, lambda g: g.dma_start(out=s_gamT[:], in_=gamT.ap()))
    O.dma("g", GQ, lambda g: g.dma_start(out=s_betT[:], in_=betT.ap()))
    O.dma("g", GQ, lambda g: g.dma_start(out=s_mask[:], in_=maskt.ap()))
    O.dma("g", GQ, lambda g: g.dma_start(out=s_one[:], in_=ones_r.ap()))
    O.dma("g", GQ, lambda g: g.dma_start(out=s_onec[:], in_=ones_c.ap()))
    w_loaded = O.close_epoch(GQ)

    O.add("v", lambda v: v.memzero(s_h[0][:]), inc=VE)
    O.add("v", lambda v: v.memzero(s_eps[:]), inc=VE)
    O.add("v", lambda v: v.tensor_scalar_add(s_eps[:], s_eps[:], EPS), inc=VE)

    def row_times_wo(lhs_col_fn, gate):
        for hf in range(2):
            for j in range(8):
                O.add("t", lambda t, hf=hf, j=j: t.matmul(
                    p_pp[hf][0:1, :],
                    lhs_col_fn(j),
                    s_wo[:, 1024 * j + 512 * hf:1024 * j + 512 * (hf + 1)],
                    start=(j == 0), stop=(j == 7)),
                    wait=(gate if (hf == 0 and j == 0) else ()),
                    inc=(TE if j == 7 else None))
        te = TE.n
        for hf in range(2):
            O.add("v", lambda v, hf=hf: v.tensor_copy(
                s_row[0:1, 512 * hf:512 * (hf + 1)], p_pp[hf][0:1, :]),
                wait=([(TE, te)] if hf == 0 else ()), inc=VE)
        return te, VE.n

    def broadcast_row(dst, gate_ve):
        for hf in range(2):
            O.add("t", lambda t, hf=hf: t.matmul(
                p_pp[hf][:],
                s_one[0:1, 0:TILE],
                s_row[0:1, 512 * hf:512 * (hf + 1)],
                start=True, stop=True),
                wait=([(VE, gate_ve)] if hf == 0 else ()),
                inc=(TE if hf == 1 else None))
        te = TE.n
        for hf in range(2):
            O.add("v", lambda v, dst=dst, hf=hf: v.tensor_copy(
                dst[:, 512 * hf:512 * (hf + 1)], p_pp[hf][:]),
                wait=([(TE, te)] if hf == 0 else ()), inc=VE)
        return VE.n

    bw_te, bw_ve = row_times_wo(lambda j: s_betT[:, j:j + 1], [(GQ, w_loaded)])
    bwB_ve = broadcast_row(s_bwB, bw_ve)
    for j in range(8):
        O.add("v", lambda v, j=j: v.tensor_scalar_mul(
            s_wo[:, 1024 * j:1024 * (j + 1)],
            s_wo[:, 1024 * j:1024 * (j + 1)],
            s_gamT[:, j:j + 1]),
            wait=([(TE, bw_te)] if j == 0 else ()), inc=VE)
    fold_ve = VE.n
    ws_te, ws_ve = row_times_wo(lambda j: s_onec[:], [(VE, fold_ve)])
    wsB_ve = broadcast_row(s_wsB, ws_ve)
    init_ve = VE.n

    gc = [0]
    pv = [init_ve]
    pyf = {0: 0, 1: 0}
    pstu = {0: 0, 1: 0}
    vsu = [0]
    upd_ve = {}
    xt_ep = {}
    G = dict(st_last=0,
             y_last=0,
             qk_te=0,
             e_wo_te=0,
             fr_se={},
             ln1_se={},
             xr_store={},
             fr_ve={},
             e_wo_te2=0,
             scan0_ve=0, e0_done=0,
             yt_slot={},
             yst_slot={})

    def dump(name, src_ap, wait):
        if not DEBUG:
            return
        t = dbg[name]
        r, cn = src_ap.shape[0], src_ap.shape[1]
        O.dma("g", DQ, lambda g, t=t, s=src_ap, r=r, cn=cn: g.dma_start(
            out=t.ap()[0:r, 0:cn], in_=s), wait=wait)
        O.close_epoch(DQ)
        O.add("v", lambda v: v.engine_nop(), wait=[(DQ, DQ.n)])

    def ln_stats(src_ap, c0, wait_src):
        st = lambda i: s_stats[:, c0 + i:c0 + i + 1]
        O.add("s", lambda s: s.activation(
            s_nL[:, 1024:2048], src_ap, AF.Copy, accum_out=st(0)),
            wait=list(wait_src) + [(VE, init_ve)], inc=SE)
        O.add("s", lambda s: s.activation(
            s_nL[:, 1024:2048], src_ap, AF.Square, accum_out=st(1)), inc=SE)
        O.add("s", lambda s: s.activation(
            st(2), st(0), AF.Copy, scale=1.0 / D), inc=SE)
        O.add("s", lambda s: s.activation(
            st(3), st(2), AF.Square), inc=SE)
        O.add("s", lambda s: s.activation(
            st(4), st(3), AF.Identity, scale=-1.0, bias=s_eps[:]), inc=SE)
        O.add("s", lambda s: s.activation(
            st(5), st(1), AF.Identity, scale=1.0 / D, bias=st(4)), inc=SE)
        O.add("s", lambda s: s.activation(
            st(6), st(5), AF.Ln), inc=SE)
        O.add("s", lambda s: s.activation(
            st(7), st(6), AF.Exp, scale=-0.5), inc=SE)
        O.add("s", lambda s: s.activation(
            st(8), st(2), AF.Copy, scale=st(7)), inc=SE)
        O.add("s", lambda s: s.activation(
            st(9), st(8), AF.Copy, scale=-1.0), inc=SE)
        return SE.n

    def layernorm(src_ap, dst_ap, c0, wait_src):
        ln_stats(src_ap, c0, wait_src)
        st = lambda i: s_stats[:, c0 + i:c0 + i + 1]
        O.add("s", lambda s: s.activation(
            dst_ap, src_ap, AF.Identity, scale=st(7), bias=st(9)), inc=SE)
        return SE.n

    def bank_ap(b, n):
        return (p_pp[b][:, 0:n] if b < 2
                else p_py[:, 512 * (b - 2):512 * (b - 2) + n])

    def prep_front(tb, nt, wu, gp2):
        ntok = nt * TILE
        nch = ntok // C

        x_eps = {}

        def emit_loads():
            lw = [(SE, SE.n)] + ([(DQ, DQ.n)] if DEBUG else [])
            for tt in range(nt):
                O.dma("g", SX, lambda g, tb=tb, tt=tt: g.dma_start(
                    out=s_x[:, 1024 * tt:1024 * (tt + 1)],
                    in_=xs.ap()[TILE * (tb + tt):TILE * (tb + tt + 1), :]),
                    wait=(lw if tt == 0 else ()))
            ep = O.close_epoch(SX)
            for tt in range(nt):
                x_eps[tt] = ep

        def u_tile(tt):
            def run():
                ep = x_eps[tt]
                d = layernorm(s_x[:, 1024 * tt:1024 * (tt + 1)],
                              s_xln[:, 1024 * tt:1024 * (tt + 1)],
                              64 + 16 * tt, [(SX, ep), (ST, xt_ep.get(0, 0))])
                G["ln1_se"][tt] = d
            return run

        def u_xt():
            for tt in range(nt):
                O.dma("sy", ST, lambda sy, tt=tt: sy.dma_start(
                    out=s_xlnT[:].rearrange("p (j q c) -> p j q c",
                                            j=8, q=4, c=TILE)[:, :, tt, :],
                    in_=s_xln[:, 1024 * tt:1024 * (tt + 1)].rearrange(
                        "p (j c) -> p j c", j=8),
                    transpose=True),
                    wait=([(SE, SE.n), (TE, G["qk_te"])] if tt == 0 else ()))
            ep = O.close_epoch(ST)
            for tt in range(nt):
                xt_ep[tt] = ep

        def u_gblk(p):
            def run():
                bank = p % 2
                w0 = [(ST, xt_ep[nt - 1]), (GQ, w_loaded),
                      (SE, G["fr_se"].get(bank, 0)),
                      (VE, G["fr_ve"].get(bank, 0))]
                for j in range(8):
                    O.add("t", lambda t, p=p, j=j, bank=bank: t.matmul(
                        p_fr[bank][:, 0:ntok],
                        s_wg[:, 1024 * j + TILE * p:1024 * j + TILE * (p + 1)],
                        s_xlnT[:, 512 * j:512 * j + ntok],
                        start=(j == 0), stop=(j == 7)),
                        wait=(w0 if j == 0 else ()), inc=(TE if j == 7 else None))
                O.add("s", lambda s, bank=bank: s.activation(
                    s_nL[:, 1024:1024 + ntok], p_fr[bank][:, 0:ntok],
                    AF.Exp, scale=-1.0),
                    wait=[(TE, TE.n)], inc=SE)
                O.add("s", lambda s, p=p: s.activation(
                    s_S[:, 512 * p:512 * p + ntok], s_nL[:, 1024:1024 + ntok],
                    AF.Ln, bias=1.0), inc=SE)
                G["fr_se"][bank] = SE.n
            return run

        def u_scan0():
            sd = SE.n
            first = True
            for cc in range(nch):
                dst = (s_nL if cc < 4 else s_x)
                c4 = cc % 4
                for p in range(8):
                    O.add("v", lambda v, c4=c4, cc=cc, p=p, dst=dst:
                          v.tensor_tensor_scan(
                        dst[:, 512 * c4 + C * p:512 * c4 + C * (p + 1)],
                        s_S[:, 512 * p + C * cc:512 * p + C * (cc + 1)],
                        s_S[:, 512 * p + C * cc:512 * p + C * (cc + 1)],
                        0.0, ALU.add, ALU.bypass),
                        wait=([(SE, sd)] if first else ()))
                    first = False
            G["scan0_ve"] = VE.n
            O.add("s", lambda s: s.activation(
                s_E[:, 0:2048].rearrange("p (c q o) -> p c q o",
                                         q=8, o=C)[:, :, 0:6, :],
                s_nL[:, 0:2048].rearrange("p (c q o) -> p c q o",
                                          q=8, o=C)[:, :, 0:6, :],
                AF.Exp, scale=-1.0),
                wait=[(VE, G["scan0_ve"]), (TE, G["e_wo_te2"])], inc=SE)
            O.add("s", lambda s: s.activation(
                s_Er[:, 0:2048], s_nL[:, 0:2048], AF.Exp, scale=1.0), inc=SE)
            G["e0_done"] = SE.n

        class _U(list):
            pass
        units = _U([u_tile(tt) for tt in range(nt)] + [u_xt] +
                   [u_gblk(p) for p in range(8)] + [u_scan0])
        units.loads = emit_loads

        def finish(dbg_this=False):
            s_done = SE.n

            evac = {}

            def bank6(b, n):
                return p_fr[b - 4][:, 0:n] if b >= 4 else bank_ap(b, n)

            def first_use(bk):
                if bk < 2:
                    return [(VE, pv[0])]
                if bk < 4:
                    return [(VE, pyf[bk - 2])]
                return [(SE, G["fr_se"].get(bk - 4, 0)),
                        (VE, G["fr_ve"].get(bk - 4, 0))]

            for tt in range(nt):
                for hf in range(2):
                    bk = 2 + (2 * tt + hf) % 4
                    w0 = ([(VE, evac[bk])] if bk in evac else first_use(bk))
                    for j in range(8):
                        O.add("t", lambda t, tt=tt, j=j, hf=hf, bk=bk: t.matmul(
                            bank6(bk, 512),
                            s_xlnT[:, 512 * j + TILE * tt:512 * j + TILE * (tt + 1)],
                            s_wv[:, 1024 * j + 512 * hf:1024 * j + 512 * (hf + 1)],
                            start=(j == 0), stop=(j == 7)),
                            wait=(w0 if j == 0 else ()),
                            inc=(TE if j == 7 else None))
                    O.add("v", lambda v, tt=tt, hf=hf, bk=bk: v.tensor_copy(
                        s_V[:, 1024 * tt + 512 * hf:1024 * tt + 512 * (hf + 1)],
                        bank6(bk, 512)),
                        wait=[(TE, TE.n), (GE, GE.n)], inc=VE)
                    evac[bk] = VE.n
                    if bk >= 4:
                        G["fr_ve"][bk - 4] = VE.n
            v_done = VE.n

            O.add("s", lambda s, nch=nch: s.activation(
                s_E[:, 0:512 * min(4, nch)].rearrange(
                    "p (c q o) -> p c q o", q=8, o=C)[:, :, 6:8, :],
                s_nL[:, 0:512 * min(4, nch)].rearrange(
                    "p (c q o) -> p c q o", q=8, o=C)[:, :, 6:8, :],
                AF.Exp, scale=-1.0),
                wait=[(VE, G["scan0_ve"]), (TE, G["e_wo_te"]),
                      (SE, G["e0_done"])], inc=SE)
            e_done = SE.n
            if nch > 4:
                O.add("s", lambda s: s.activation(
                    s_E[:, 2048:4096], s_x[:, 0:2048], AF.Exp, scale=-1.0),
                    wait=[(VE, G["scan0_ve"])], inc=SE)
                O.add("s", lambda s: s.activation(
                    s_Er[:, 2048:4096], s_x[:, 0:2048], AF.Exp, scale=1.0),
                    inc=SE)
                e_done = SE.n
            O.add("v", lambda v, nch=nch, gp2=gp2: v.tensor_copy(
                s_ec[:, 64 * gp2:64 * (gp2 + 1)].rearrange(
                    "p (c q) -> p c q", q=8)[:, 0:nch, :],
                s_E[:].rearrange("p (c q o) -> p c q o",
                                 q=8, o=C)[:, 0:nch, :, C - 1]),
                wait=[(SE, e_done)], inc=VE)
            ec_done = VE.n

            if dbg_this:
                dump("d_xln", s_xln[:], [(VE, ec_done)])
                dump("d_S", s_S[:], [(SE, s_done)])
                dump("d_negL", s_nL[:], [(VE, ec_done)])

            rp = [2]

            def proj_qk(s_w, s_dst, s_scale, skip, winmajor):
                if skip:
                    return
                for p in range(8):
                    bank = rp[0] % 6
                    rp[0] += 1
                    w0 = ([(VE, evac[bank])] if bank in evac
                          else first_use(bank))
                    for j in range(8):
                        O.add("t", lambda t, p=p, j=j, bank=bank, s_w=s_w,
                              ntok=ntok: t.matmul(
                            bank6(bank, ntok),
                            s_w[:, 1024 * j + TILE * p:1024 * j + TILE * (p + 1)],
                            s_xlnT[:, 512 * j:512 * j + ntok],
                            start=(j == 0), stop=(j == 7)),
                            wait=(w0 if j == 0 else ()),
                            inc=(TE if j == 7 else None))
                    if winmajor:
                        out_ap = (lambda p=p: s_dst[:].rearrange(
                            "p (w q e o) -> p w q e o",
                            w=4, q=8, e=2, o=C)[:, 0:nch // 2, p, :, :])
                    else:
                        out_ap = (lambda p=p: s_dst[:, 512 * p:512 * p + ntok]
                                  .rearrange("p (c o) -> p c o", o=C))
                    O.add("v", lambda v, p=p, bank=bank, s_scale=s_scale,
                          out_ap=out_ap, ntok=ntok, nch=nch:
                          v.tensor_tensor(
                              out_ap(),
                              bank6(bank, ntok).rearrange("p (c o) -> p c o", o=C),
                              s_scale[:].rearrange("p (c q o) -> p c q o",
                                                   q=8, o=C)[:, 0:nch, p, :],
                              ALU.mult),
                          wait=[(TE, TE.n), (SE, e_done)], inc=VE)
                    evac[bank] = VE.n
                    if bank >= 4:
                        G["fr_ve"][bank - 4] = VE.n
            proj_qk(s_wq, s_qt, s_E, wu, False)
            proj_qk(s_wk, s_kt, s_Er, False, True)
            qk_done = VE.n
            G["qk_te"] = TE.n

            if dbg_this:
                dump("d_qt", s_qt[:], [(VE, qk_done)])
                dump("d_kt", s_kt[:], [(VE, qk_done)])
                dump("d_V", s_V[:], [(VE, v_done)])

            return dict(v_done=v_done, ec_done=ec_done, qk_done=qk_done,
                        nch=nch, gp2=gp2)

        return units, finish

    def emit_yln(w):
        return ln_stats(s_y[:, 1024 * (w % 2):1024 * (w % 2 + 1)],
                        16 * w, [(VE, VE.n)])

    def emit_loop(tb, nt, wu, FR, splice, dbg_this=False):
        nch = FR["nch"]
        gp2 = FR["gp2"]
        half_cols = 512 * nch // 2
        O.dma("sy", KT, lambda sy, half_cols=half_cols: sy.dma_start(
            out=s_ktm[:, 0:half_cols].rearrange("p (m c) -> p m c", c=TILE),
            in_=s_kt[:, 0:half_cols].rearrange("p (m c) -> p m c", c=TILE),
            transpose=True),
            wait=[(VE, FR["qk_done"]), (TE, G["st_last"])])
        ktm_ep0 = O.close_epoch(KT)
        O.dma("sy", KT, lambda sy, half_cols=half_cols: sy.dma_start(
            out=s_ktm[:, half_cols:2 * half_cols].rearrange(
                "p (m c) -> p m c", c=TILE),
            in_=s_kt[:, half_cols:2 * half_cols].rearrange(
                "p (m c) -> p m c", c=TILE),
            transpose=True))
        ktm_ep1 = O.close_epoch(KT)
        yt_eps = {}
        yst = {}
        xr_ep = {}

        def xr_load(w):
            if wu or w >= nt:
                return
            O.dma("g", GX, lambda g, w=w, tb=tb: g.dma_start(
                out=s_xr[:, 1024 * (w % 2):1024 * (w % 2 + 1)],
                in_=xs.ap()[TILE * (tb + w):TILE * (tb + w + 1), :]),
                wait=[(XQ, G["xr_store"].get(w % 2, 0))])
            xr_ep[w] = O.close_epoch(GX)

        xr_load(0)
        xr_load(1)
        if splice:
            splice.loads()
            for _ in range(min(4, len(splice) - 9)):
                splice.pop(0)()
        vsg = {}

        def emit_vs(w2):
            O.add("g", lambda g_, w2=w2: g_.tensor_copy(
                s_VS[0:64, :],
                s_V[64:128, 1024 * w2:1024 * (w2 + 1)]),
                wait=[(VE, FR["v_done"]), (TE, vsu[0])], inc=GE)
            O.add("g", lambda g_, w2=w2: g_.tensor_copy(
                s_VS[64:128, :],
                s_V[0:64, 1024 * w2:1024 * (w2 + 1)]),
                inc=GE)
            vsg[w2] = GE.n

        if not wu:
            emit_vs(0)

        def emit_be_late(w):
            w0 = [(YT, yt_eps[w]),
                  (VE, FR["qk_done"] if w == 0 else pv[0])]
            for j in range(8):
                for hf in range(2):
                    O.add("t", lambda t, w=w, j=j, hf=hf: t.matmul(
                        p_pp[hf][:],
                        s_E[:, 512 * j + TILE * w:512 * j + TILE * (w + 1)],
                        s_wo[:, 1024 * j + 512 * hf:1024 * j + 512 * (hf + 1)],
                        start=(j == 0), stop=(j == 7)),
                        wait=(w0 if (j == 0 and hf == 0) else ()),
                        inc=(TE if (j == 7 and hf == 1) else None))
            G["e_wo_te2"] = G["e_wo_te"]
            G["e_wo_te"] = TE.n
            r_ap = s_stats[:, 16 * w + 7:16 * w + 8]
            nrm_ap = s_stats[:, 16 * w + 9:16 * w + 10]
            xsl = 1024 * (w % 2)
            for hf in range(2):
                sl = slice(512 * hf, 512 * (hf + 1))
                O.add("v", lambda v, hf=hf, sl=sl, r_ap=r_ap:
                      v.scalar_tensor_tensor(
                          p_pp[hf][:], p_pp[hf][:], r_ap, s_bwB[:, sl],
                          ALU.mult, ALU.add),
                      wait=([(TE, TE.n), (SE, yst[w])] if hf == 0 else ()),
                      inc=VE)
                O.add("v", lambda v, hf=hf, sl=sl, nrm_ap=nrm_ap:
                      v.scalar_tensor_tensor(
                          p_pp[hf][:], s_wsB[:, sl], nrm_ap,
                          p_pp[hf][:], ALU.mult, ALU.add),
                      inc=VE)
                O.add("v", lambda v, hf=hf, sl=sl, xsl=xsl: v.tensor_add(
                    s_xr[:, xsl + sl.start:xsl + sl.stop],
                    s_xr[:, xsl + sl.start:xsl + sl.stop],
                    p_pp[hf][:]),
                    wait=([(GX, xr_ep[w])] if hf == 0 else ()), inc=VE)
            pv[0] = VE.n
            O.dma("g", XQ, lambda g, w=w, tb=tb, xsl=xsl: g.dma_start(
                out=x2d.ap()[TILE * (tb + w - 2):TILE * (tb + w - 1), :],
                in_=s_xr[:, xsl:xsl + 1024]),
                wait=[(VE, VE.n)])
            G["xr_store"][w % 2] = O.close_epoch(XQ)
            xr_load(w + 2)
            if dbg_this and w == 0:
                dump("d_x2", s_xr[:, 0:1024], [(VE, VE.n)])

        npair = nch // 2
        for cc in range(nch):
            par = cc % 2
            w = cc // 2
            hs = gc[0] % 3
            hn = (gc[0] + 1) % 3
            ysl = 1024 * (w % 2)
            vs_done = vsg.get(w, 0)

            if not wu:
                for h in range(NH):
                    hp, hq = h % 2, h // 2
                    O.add("t", lambda t, w=w, h=h, hp=hp, hq=hq, par=par, cc=cc:
                          t.matmul(
                        p_py[64 * hp:64 * (hp + 1),
                             512 * par + 64 * hq:512 * par + 64 * (hq + 1)],
                        s_kt[64 * hp:64 * (hp + 1),
                             1024 * w + TILE * hq + C * par:
                             1024 * w + TILE * hq + C * (par + 1)],
                        s_qt[64 * hp:64 * (hp + 1),
                             512 * hq + C * cc:512 * hq + C * (cc + 1)],
                        start=True, stop=True),
                        wait=([(VE, max(FR["qk_done"], pyf[par]))]
                              if h == 0 else ()),
                        inc=(TE if h == NH - 1 else None))
                p_mm_done = TE.n
                O.add("v", lambda v, par=par: v.tensor_tensor(
                    s_P[:, 512 * par:512 * (par + 1)],
                    p_py[:, 512 * par:512 * (par + 1)],
                    s_mask[:, 512 * par:512 * (par + 1)], ALU.mult),
                    wait=[(TE, p_mm_done)], inc=VE)
                pmask_done = VE.n

            for h in range(NH):
                hp, hq = h % 2, h // 2
                O.add("t", lambda t, w=w, h=h, hp=hp, hq=hq, par=par:
                      t.matmul(
                          p_st[64 * par:64 * (par + 1), 64 * h:64 * (h + 1)],
                          s_ktm[64 * par:64 * (par + 1),
                                1024 * w + TILE * hq + 64 * hp:
                                1024 * w + TILE * hq + 64 * (hp + 1)],
                          s_V[64 * par:64 * (par + 1),
                              1024 * w + 64 * h:1024 * w + 64 * (h + 1)],
                          start=True, stop=True),
                      wait=([(KT, ktm_ep0 if w < (nch + 3) // 4 else ktm_ep1),
                             (VE, pstu[par])] if h == 0 else ()),
                      inc=(TE if h == NH - 1 else None))
            st_mm = TE.n
            G["st_last"] = st_mm

            def emit_update(cc=cc, par=par, hs=hs, hn=hn, st_mm=st_mm):
                for hp in range(2):
                    O.add("v", lambda v, par=par, hp=hp, hs=hs: v.tensor_tensor(
                        s_ht[64 * hp:64 * (hp + 1), :].rearrange(
                            "p (q o) -> p q o", o=C),
                        p_st[64 * par:64 * (par + 1), :].rearrange(
                            "p (q two o) -> p q two o", two=2, o=C)[:, :, hp, :],
                        s_h[hs][64 * hp:64 * (hp + 1), :].rearrange(
                            "p (q o) -> p q o", o=C),
                        ALU.add),
                        wait=([(TE, st_mm), (VE, FR["ec_done"])]
                              if hp == 0 else ()), inc=VE)
                pstu[par] = VE.n
                O.add("v", lambda v, cc=cc, hn=hn: v.tensor_tensor(
                    s_h[hn][:].rearrange("p (q o) -> p q o", o=C),
                    s_ht[:].rearrange("p (q o) -> p q o", o=C),
                    s_ec[:, 64 * gp2 + 8 * cc:64 * gp2 + 8 * cc + 8]
                    .unsqueeze(2).broadcast_to([TILE, 8, C]),
                    ALU.mult), inc=VE)
                upd_ve[hn] = VE.n
            if wu or par == 0:
                emit_update()

            if (not wu) and par == 1:
                ug = max(upd_ve.get((gc[0] - 1) % 3, 0),
                         upd_ve.get(gc[0] % 3, 0))
                for c2 in (cc - 1, cc):
                    par2 = c2 % 2
                    hs2 = (gc[0] - (cc - c2)) % 3
                    for h in range(NH):
                        hp, hq = h % 2, h // 2
                        rhs_v = (
                            (lambda t, c2=c2, hp=hp, h=h:
                             s_V[64 * hp:64 * (hp + 1),
                                 1024 * (c2 // 2) + 64 * h:1024 * (c2 // 2) + 64 * (h + 1)])
                            if hp == par2 else
                            (lambda t, hp=hp, h=h:
                             s_VS[64 * hp:64 * (hp + 1), 64 * h:64 * (h + 1)])
                        )
                        O.add("t", lambda t, c2=c2, par2=par2, h=h, hp=hp, hq=hq,
                              rhs_v=rhs_v: t.matmul(
                                  p_py[64 * hp:64 * (hp + 1),
                                       512 * par2 + 64 * hq:512 * par2 + 64 * (hq + 1)],
                                  s_P[64 * hp:64 * (hp + 1),
                                      512 * par2 + 64 * hq:512 * par2 + 64 * (hq + 1)],
                                  rhs_v(t),
                                  start=True, stop=False),
                              wait=(([(VE, max(pmask_done, ug)), (GE, vs_done)])
                                    if h == 0 else ()))
                        O.add("t", lambda t, c2=c2, par2=par2, h=h, hp=hp, hq=hq,
                              hs2=hs2: t.matmul(
                                  p_py[64 * hp:64 * (hp + 1),
                                       512 * par2 + 64 * hq:512 * par2 + 64 * (hq + 1)],
                                  s_qt[64 * hp:64 * (hp + 1),
                                       512 * hq + C * c2:512 * hq + C * (c2 + 1)],
                                  s_h[hs2][64 * hp:64 * (hp + 1),
                                           64 * hq:64 * (hq + 1)],
                                  start=False, stop=True),
                              inc=(TE if (h == NH - 1 and c2 == cc) else None))
                G["y_last"] = TE.n
                vsu[0] = TE.n
                if w + 1 < npair:
                    emit_vs(w + 1)
                for par2 in (0, 1):
                    for hp in (0, 1):
                        O.add("v", lambda v, ysl=ysl, par2=par2, hp=hp:
                              v.tensor_copy(
                            s_y[64 * par2:64 * (par2 + 1),
                                ysl:ysl + 1024].rearrange(
                                "p (q two o) -> p q two o", two=2, o=C)[:, :, hp, :],
                            p_py[64 * hp:64 * (hp + 1),
                                 512 * par2:512 * (par2 + 1)].rearrange(
                                "p (q o) -> p q o", o=C)),
                            wait=(([(TE, TE.n),
                                    (YT, G["yt_slot"].get(w % 2, 0)),
                                    (SE, G["yst_slot"].get(w % 2, 0))])
                                  if (par2 == 0 and hp == 0) else ()),
                            inc=VE)
                    pyf[par2] = VE.n
                O.dma("sy", YT, lambda sy, w=w, ysl=ysl: sy.dma_start(
                    out=s_E[:].rearrange("p (j q c) -> p j q c",
                                         j=8, q=4, c=TILE)[:, :, w, :],
                    in_=s_y[:, ysl:ysl + 1024].rearrange(
                        "p (j c) -> p j c", j=8),
                    transpose=True),
                    wait=[(VE, VE.n)])
                yt_eps[w] = O.close_epoch(YT)
                G["yt_slot"][w % 2] = yt_eps[w]
                emit_update()
                if splice:
                    sched = {4: [1, 4, 4, 0], 2: [2, 7]}[npair]
                    for _ in range(min(sched[w], len(splice))):
                        splice.pop(0)()
                yst[w] = emit_yln(w)
                G["yst_slot"][w % 2] = yst[w]
                if w > 0:
                    emit_be_late(w - 1)
                if splice and w == npair - 1:
                    while splice:
                        splice.pop(0)()
            elif splice and wu and par == 1:
                sched = {2: [2, 7]}[npair]
                for _ in range(min(sched[w], len(splice))):
                    splice.pop(0)()
                if w == npair - 1:
                    while splice:
                        splice.pop(0)()
            gc[0] += 1

        if not wu:
            emit_be_late(nt - 1)
        if dbg_this:
            dump("d_y", s_y[:], [(VE, VE.n)])
            dump("d_h", s_h[gc[0] % 3][:], [(VE, VE.n)])

    ngrp = 0 if TRUNC in (1, 2) else (1 if TRUNC == 3 else 4)
    if TRUNC != 1:
        units, fin = prep_front(0, 2, True, 0)
        units.loads()
        for u in units:
            u()
        FR = fin()
        if ngrp:
            units, fin = prep_front(2, 4, False, 1)
        else:
            units = []
        emit_loop(0, 2, True, FR, units)
    for gi in range(ngrp):
        FR = fin(dbg_this=(DEBUG and gi == 0))
        if gi < ngrp - 1:
            units, fin = prep_front(2 + 4 * (gi + 1), 4, False, gi % 2)
        else:
            units = []
        emit_loop(2 + 4 * gi, 4, False, FR, units,
                  dbg_this=(DEBUG and gi == 0))
    a_sx, a_st, a_kt, a_yt = SX.n, ST.n, KT.n, YT.n
    a_gq, a_xq, a_dq, a_gx, a_ge = GQ.n, XQ.n, DQ.n, GX.n, GE.n
    a_te, a_ve, a_se = TE.n, VE.n, SE.n

    es.close()

    es2 = ExitStack()
    E2 = es2.enter_context
    s_x2T = E2(nc.sbuf_tensor("s_x2T", [TILE, 16 * D], BF16))
    s_hff = E2(nc.sbuf_tensor("s_hff", [TILE, NHC * 2048], BF16))
    s_wd = E2(nc.sbuf_tensor("s_wd", [TILE, NHC * 1024], BF16))
    s_wgu = [E2(nc.sbuf_tensor(f"s_wgu{i}", [TILE, 2048], BF16)) for i in range(2)]
    s_x2 = E2(nc.sbuf_tensor("s_x2", [TILE, 2048], F32))
    s_st2 = E2(nc.sbuf_tensor("s_st2", [TILE, 16], F32))
    s_sq2 = E2(nc.sbuf_tensor("s_sq2", [TILE, 1024], F32))
    s_a2 = E2(nc.sbuf_tensor("s_a2", [TILE, 1024], BF16))
    s_act2 = E2(nc.sbuf_tensor("s_act2", [TILE, 512], F32))
    p_g = [E2(nc.psum_tensor(f"p_g{i}", [TILE, 512], F32)) for i in range(2)]
    p_u = [E2(nc.psum_tensor(f"p_u{i}", [TILE, 512], F32)) for i in range(2)]
    p_d = [E2(nc.psum_tensor(f"p_d{i}", [TILE, 1024], F32)) for i in range(2)]

    barrier = [(TE, a_te), (VE, a_ve), (SE, a_se), (SX, a_sx), (ST, a_st),
               (KT, a_kt), (YT, a_yt), (GQ, a_gq), (XQ, a_xq), (DQ, a_dq),
               (GX, a_gx), (GE, a_ge)]
    first = {k: True for k in ("t", "v", "s", "sy", "g")}

    def addD(eng, fn, wait=(), inc=None, inc_by=1):
        w = list(wait)
        if first[eng]:
            w = barrier + w
            first[eng] = False
        O.add(eng, fn, wait=w, inc=inc, inc_by=inc_by)

    def dmaD(eng, sem, fn, wait=()):
        w = list(wait)
        if first[eng]:
            w = barrier + w
            first[eng] = False
        if sem.pending is not None:
            w.append((sem, sem.pending))
            sem.pending = None
        O.add(eng, fn, wait=w, inc=sem, inc_by=16)

    a2_done = {}
    for tt in range(16 if TRUNC == 0 else 0):
        buf = tt % 2
        dmaD("g", GX, lambda g, tt=tt, buf=buf: g.dma_start(
            out=s_x2[:, 1024 * buf:1024 * (buf + 1)],
            in_=x2d.ap()[TILE * tt:TILE * (tt + 1), :]),
            wait=([(VE, a2_done[tt - 2])] if tt >= 2 else []))
        ld = O.close_epoch(GX)
        xt = s_x2[:, 1024 * buf:1024 * (buf + 1)]
        addD("v", lambda v, xt=xt: v.reduce_sum(
            s_st2[:, 0:1], xt, axis=AX.X), wait=[(GX, ld)], inc=VE)
        addD("s", lambda s, xt=xt: s.activation(
            s_sq2[:], xt, AF.Square, accum_out=s_st2[:, 1:2]),
            wait=[(GX, ld), (VE, VE.n)], inc=SE)
        addD("v", lambda v: v.tensor_scalar(
            s_st2[:, 2:3], s_st2[:, 0:1], 1.0 / D, None, ALU.mult), inc=VE)
        addD("v", lambda v: v.tensor_mul(
            s_st2[:, 3:4], s_st2[:, 2:3], s_st2[:, 2:3]), inc=VE)
        addD("v", lambda v: v.tensor_scalar(
            s_st2[:, 4:5], s_st2[:, 1:2], 1.0 / D, s_st2[:, 3:4],
            ALU.mult, ALU.subtract), wait=[(SE, SE.n)], inc=VE)
        addD("v", lambda v: v.tensor_scalar_add(
            s_st2[:, 4:5], s_st2[:, 4:5], EPS), inc=VE)
        addD("s", lambda s: s.activation(
            s_st2[:, 5:6], s_st2[:, 4:5], AF.Ln), wait=[(VE, VE.n)], inc=SE)
        addD("s", lambda s: s.activation(
            s_st2[:, 6:7], s_st2[:, 5:6], AF.Exp, scale=-0.5), inc=SE)
        addD("v", lambda v, xt=xt: v.tensor_scalar(
            s_a2[:], xt, s_st2[:, 2:3], s_st2[:, 6:7], ALU.subtract, ALU.mult),
            wait=[(SE, SE.n), (ST, ST.n)], inc=VE)
        a2_done[tt] = VE.n
        dmaD("sy", ST, lambda sy, tt=tt: sy.dma_start(
            out=s_x2T[:].rearrange("p (j q c) -> p j q c",
                                   j=8, q=16, c=TILE)[:, :, tt, :],
            in_=s_a2[:].rearrange("p (j c) -> p j c", j=8), transpose=True),
            wait=[(VE, a2_done[tt])])
        O.close_epoch(ST)
    ln2_tp = ST.n

    if TRUNC == 0:
        dmaD("g", GQ, lambda g: g.dma_start(
            out=s_wd[:].rearrange("p (hc d) -> p hc d", hc=NHC),
            in_=wdi.ap().rearrange("(hc p) d -> p hc d", p=TILE)))
    wd_done = O.close_epoch(GQ)

    wgu_users = {}
    silu_of = {}
    hff_of = {}
    for hc in range(NHC if TRUNC == 0 else 0):
        buf = hc % 2
        dmaD("g", GQ, lambda g, hc=hc, buf=buf: g.dma_start(
            out=s_wgu[buf][:, 0:1024].rearrange("p (i h) -> p i h", i=8),
            in_=wgi.ap()[:, TILE * hc:TILE * (hc + 1)].rearrange(
                "(i p) h -> p i h", p=TILE)),
            wait=[(TE, wgu_users.get(buf, 0))])
        dmaD("g", GQ, lambda g, hc=hc, buf=buf: g.dma_start(
            out=s_wgu[buf][:, 1024:2048].rearrange("p (i h) -> p i h", i=8),
            in_=wui.ap()[:, TILE * hc:TILE * (hc + 1)].rearrange(
                "(i p) h -> p i h", p=TILE)))
        gu_ld = O.close_epoch(GQ)
        for t4 in range(4):
            pb = t4 % 2
            w0 = [(GQ, gu_ld), (ST, ln2_tp), (SE, silu_of.get(pb, 0))]
            for j in range(8):
                addD("t", lambda t, buf=buf, t4=t4, j=j, pb=pb: t.matmul(
                    p_g[pb][:],
                    s_wgu[buf][:, TILE * j:TILE * (j + 1)],
                    s_x2T[:, 2048 * j + 512 * t4:2048 * j + 512 * (t4 + 1)],
                    start=(j == 0), stop=(j == 7)),
                    wait=(w0 if j == 0 else ()), inc=(TE if j == 7 else None))
            addD("s", lambda s, pb=pb: s.activation(
                s_act2[:], p_g[pb][:], AF.Silu, scale=SCL),
                wait=[(TE, TE.n), (VE, hff_of.get(1 - pb, 0))], inc=SE)
            silu_of[pb] = SE.n
            silu_done = SE.n
            w1 = [(VE, hff_of.get(pb, 0))]
            for j in range(8):
                addD("t", lambda t, buf=buf, t4=t4, j=j, pb=pb: t.matmul(
                    p_u[pb][:],
                    s_wgu[buf][:, 1024 + TILE * j:1024 + TILE * (j + 1)],
                    s_x2T[:, 2048 * j + 512 * t4:2048 * j + 512 * (t4 + 1)],
                    start=(j == 0), stop=(j == 7)),
                    wait=(w1 if j == 0 else ()), inc=(TE if j == 7 else None))
            addD("v", lambda v, hc=hc, t4=t4, pb=pb: v.scalar_tensor_tensor(
                s_hff[:, 2048 * hc + 512 * t4:2048 * hc + 512 * (t4 + 1)],
                p_u[pb][:], SCL * SCL, s_act2[:], ALU.mult, ALU.mult),
                wait=[(TE, TE.n), (SE, silu_done)], inc=VE)
            hff_of[pb] = VE.n
        wgu_users[buf] = TE.n
    hff_done = VE.n

    resid_of = {}
    out_st = {}
    for ts in range(16 if TRUNC == 0 else 0):
        buf = ts % 2
        dmaD("g", GX, lambda g, ts=ts, buf=buf: g.dma_start(
            out=s_x2[:, 1024 * buf:1024 * (buf + 1)],
            in_=x2d.ap()[TILE * ts:TILE * (ts + 1), :]),
            wait=[(VE, resid_of.get(buf, 0)), (XQ, out_st.get(buf, 0))])
        x2_ld = O.close_epoch(GX)
        w0 = [(VE, hff_done if ts == 0 else resid_of.get(buf, 0)), (GQ, wd_done)]
        for hf in range(2):
            for hcc in range(NHC):
                addD("t", lambda t, ts=ts, buf=buf, hf=hf, hcc=hcc: t.matmul(
                    p_d[buf][:, 512 * hf:512 * (hf + 1)],
                    s_hff[:, 2048 * hcc + TILE * ts:2048 * hcc + TILE * (ts + 1)],
                    s_wd[:, 1024 * hcc + 512 * hf:1024 * hcc + 512 * (hf + 1)],
                    start=(hcc == 0), stop=(hcc == NHC - 1)),
                    wait=(w0 if (hf == 0 and hcc == 0) else ()),
                    inc=(TE if (hf == 1 and hcc == NHC - 1) else None))
        addD("v", lambda v, buf=buf: v.tensor_add(
            s_x2[:, 1024 * buf:1024 * (buf + 1)],
            s_x2[:, 1024 * buf:1024 * (buf + 1)], p_d[buf][:]),
            wait=[(TE, TE.n), (GX, x2_ld)], inc=VE)
        resid_of[buf] = VE.n
        dmaD("g", XQ, lambda g, ts=ts, buf=buf: g.dma_start(
            out=out.ap()[TILE * ts:TILE * (ts + 1), :],
            in_=s_x2[:, 1024 * buf:1024 * (buf + 1)]),
            wait=[(VE, VE.n)])
        out_st[buf] = O.close_epoch(XQ)
        if DEBUG and ts == 0:
            dump("d_hff", s_hff[:, 0:2048], [(VE, hff_done)])

    final_xq = XQ.n
    final_dq = DQ.n

    with nc.Block() as block:
        @block.tensor
        def _(t):
            for f in O.q["t"]:
                f(t)

        @block.vector
        def _(v):
            for f in O.q["v"]:
                f(v)

        @block.scalar
        def _(s):
            for f in O.q["s"]:
                f(s)

        @block.sync
        def _(sy):
            for f in O.q["sy"]:
                f(sy)

        @block.gpsimd
        def _(g):
            for f in O.q["g"]:
                f(g)
            g.wait_ge(XQ.h, final_xq)
            if final_dq:
                g.wait_ge(DQ.h, final_dq)

    es2.close()
    return nc


def _shard_inputs(inputs):
    x = np.asarray(inputs["L0_attn_x"], np.float32).reshape(16384, D)
    wqT = np.ascontiguousarray(np.asarray(inputs["L0_attn_Wq"], np.float32).T)
    wkT = np.ascontiguousarray(np.asarray(inputs["L0_attn_Wk"], np.float32).T)
    wvT = np.ascontiguousarray(np.asarray(inputs["L0_attn_Wv"], np.float32).T)
    wgT = np.ascontiguousarray(np.asarray(inputs["L0_attn_Wg"], np.float32).T)
    woT = np.ascontiguousarray(np.asarray(inputs["L0_attn_Wo"], np.float32).T)
    gamT = np.ascontiguousarray(
        np.asarray(inputs["L0_attn_gamma"], np.float32).reshape(8, TILE).T)
    betT = np.ascontiguousarray(
        np.asarray(inputs["L0_attn_beta"], np.float32).reshape(8, TILE).T)
    wgi = np.ascontiguousarray(np.asarray(inputs["L0_ffn_w_gate"], np.int32).T)
    wui = np.ascontiguousarray(np.asarray(inputs["L0_ffn_w_up"], np.int32).T)
    wdi = np.ascontiguousarray(np.asarray(inputs["L0_ffn_w_down"], np.int32).T)
    in_maps = []
    for c in range(NCORES):
        lo = c * T_OWN
        if c == 0:
            xsl = np.concatenate([np.zeros((WU, D), np.float32), x[0:T_OWN]], 0)
        else:
            xsl = np.ascontiguousarray(x[lo - WU:lo + T_OWN])
        in_maps.append(dict(xs=xsl, wqT=wqT, wkT=wkT, wvT=wvT, wgT=wgT, woT=woT,
                            gamT=gamT, betT=betT, wgi=wgi, wui=wui, wdi=wdi))
    return in_maps


def kernel(**inputs):
    in_maps = _shard_inputs(inputs)
    nc = build_program()
    res = run_bass_kernel_spmd(nc, in_maps, core_ids=list(range(NCORES)))
    outs = [np.asarray(res.results[c]["out"]) for c in range(NCORES)]
    full = np.concatenate(outs, axis=0).reshape(1, 16384, D)
    if DEBUG:
        kernel.last_results = res
    return full.astype(np.float32)
